# version 11
# baseline (speedup 1.0000x reference)
import numpy as np
import ml_dtypes
import concourse.bass as bass
import concourse.mybir as mybir
from concourse.bass_utils import run_bass_kernel_spmd

F32 = mybir.dt.float32
BF16 = mybir.dt.bfloat16
AF = mybir.ActivationFunctionType
ALU = mybir.AluOpType
AX = mybir.AxisListType

P = 128
S = 2048
D = 2048
NCH = D // P
NTT = S // P
NTB = S // 512

ENGINES = ("sync", "vector", "scalar", "tensor", "gpsimd")
DMA_CHAINS = {"sync": 6, "gpsimd": 5, "scalar": 4}


class Prog:
    def __init__(self):
        self.ops = []
        self.writer = {}
        self.readers = {}
        self.chain_rr = {q: 0 for q in DMA_CHAINS}
        self.chain_last = {}
        self.chain_count = {}
        self.barriers = []

    def op(self, eng, fn, reads=(), writes=(), dma=False, tag=""):
        assert eng in ENGINES, eng
        deps = set()
        for k in reads:
            w = self.writer.get(k)
            if w is not None:
                deps.add(w)
        for k in writes:
            w = self.writer.get(k)
            if w is not None:
                deps.add(w)
            for r in self.readers.get(k, ()):
                deps.add(r)
        idx = len(self.ops)
        chain = pos = None
        if dma:
            c = self.chain_rr[eng]
            self.chain_rr[eng] = (c + 1) % DMA_CHAINS[eng]
            chain = (eng, c)
            last = self.chain_last.get(chain)
            if last is not None:
                deps.add(last)
            self.chain_last[chain] = idx
            pos = self.chain_count.get(chain, 0)
            self.chain_count[chain] = pos + 1
        deps.discard(idx)
        for k in reads:
            self.readers.setdefault(k, []).append(idx)
        for k in writes:
            self.writer[k] = idx
            self.readers[k] = []
        self.ops.append(dict(eng=eng, fn=fn, deps=deps, dma=dma, chain=chain,
                             pos=pos, tag=tag, awaited=False, inc=None))
        return idx

    def barrier(self):
        self.barriers.append(len(self.ops))
        for k in self.readers:
            self.readers[k] = []

    def emit(self, nc):
        ops = self.ops
        for o in ops:
            for d in o["deps"]:
                pr = ops[d]
                if pr["dma"]:
                    continue
                if pr["eng"] == "tensor" and o["eng"] == "tensor":
                    continue
                pr["awaited"] = True
        marks = list(self.barriers) + [len(ops)]
        start = 0
        for m in marks:
            last_per = {}
            for i in range(start, m):
                if not ops[i]["dma"]:
                    last_per[ops[i]["eng"]] = i
            for i in last_per.values():
                ops[i]["awaited"] = True
            start = m
        cnt = {e: 0 for e in ENGINES}
        for o in ops:
            if not o["dma"] and o["awaited"]:
                cnt[o["eng"]] += 1
                o["inc"] = cnt[o["eng"]]
        snaps = []
        ccnt = {e: 0 for e in ENGINES}
        chcnt = {}
        bi = 0
        for i, o in enumerate(ops):
            while bi < len(self.barriers) and self.barriers[bi] == i:
                snaps.append((dict(ccnt), dict(chcnt)))
                bi += 1
            if o["dma"]:
                chcnt[o["chain"]] = chcnt.get(o["chain"], 0) + 1
            elif o["inc"] is not None:
                ccnt[o["eng"]] = o["inc"]
        while bi < len(self.barriers):
            snaps.append((dict(ccnt), dict(chcnt)))
            bi += 1
        final_snap = (dict(ccnt), dict(chcnt))

        sems = {e: nc.semaphore(f"sem_{e}").__enter__() for e in ENGINES}
        chain_sems = {}
        for q, n in DMA_CHAINS.items():
            for c in range(n):
                chain_sems[(q, c)] = nc.semaphore(f"dq_{q}_{c}").__enter__()

        barrier_pos = self.barriers

        def make_emitter(engname):
            def _emit(engobj):
                wm = {}

                def do_wait(sem, val):
                    sid = sem.num
                    if wm.get(sid, 0) >= val:
                        return
                    engobj.wait_ge(sem, val)
                    wm[sid] = val

                def do_snap(snap):
                    ecnt, chc = snap
                    for e2 in ENGINES:
                        if e2 != engname and ecnt[e2] > 0:
                            do_wait(sems[e2], ecnt[e2])
                    for ch, n2 in chc.items():
                        if n2 > 0:
                            do_wait(chain_sems[ch], 16 * n2)

                bl = 0
                for i, o in enumerate(ops):
                    while bl < len(barrier_pos) and barrier_pos[bl] <= i:
                        do_snap(snaps[bl])
                        bl += 1
                    if o["eng"] != engname:
                        continue
                    for d in sorted(o["deps"]):
                        pr = ops[d]
                        if pr["dma"]:
                            do_wait(chain_sems[pr["chain"]], 16 * (pr["pos"] + 1))
                        else:
                            if pr["eng"] == "tensor" and engname == "tensor":
                                continue
                            do_wait(sems[pr["eng"]], pr["inc"])
                    inst = o["fn"](engobj)
                    if o["dma"]:
                        inst.then_inc(chain_sems[o["chain"]], 16)
                    elif o["inc"] is not None:
                        inst.then_inc(sems[engname], 1)
                do_snap(final_snap)
            return _emit

        with nc.Block() as block:
            block.sync(make_emitter("sync"))
            block.vector(make_emitter("vector"))
            block.scalar(make_emitter("scalar"))
            block.tensor(make_emitter("tensor"))
            block.gpsimd(make_emitter("gpsimd"))


class Builder:
    def __init__(self, nc, prog):
        self.nc = nc
        self.p = prog
        self.in_arrays = {}
        self.sb_off = 0
        self.sb_map = {}
        self.marks = []
        self.dram_cache = {}

    def hin(self, name, arr):
        if name in self.dram_cache:
            return self.dram_cache[name]
        arr = np.ascontiguousarray(arr)
        self.in_arrays[name] = arr
        dt = F32 if arr.dtype == np.float32 else BF16
        ap = self.nc.dram_tensor(name, list(arr.shape), dt,
                                 kind="ExternalInput").ap()
        self.dram_cache[name] = ap
        return ap

    def dram(self, name, shape, dtype=F32, kind="Internal"):
        if name in self.dram_cache:
            return self.dram_cache[name]
        ap = self.nc.dram_tensor(name, list(shape), dtype, kind=kind).ap()
        self.dram_cache[name] = ap
        return ap

    def init_arena(self):
        rem = self.nc.sbuf_bytes_remaining
        n = rem // 4 - 8
        self.arena_n = n
        self.arena = self.nc.alloc_sbuf_tensor("arena", [P, n], F32)
        self.psum_t = self.nc.alloc_psum_tensor("psarena", [P, 4096], F32)

    def sb(self, name, free, dtype=F32):
        if name in self.sb_map:
            off, ne, dt = self.sb_map[name]
            assert dt == dtype
        else:
            ne = free if dtype == F32 else (free + 1) // 2
            off = self.sb_off
            assert off + ne <= self.arena_n, \
                f"SBUF overflow at {name}: {off}+{ne} > {self.arena_n}"
            self.sb_off = off + ne
            self.sb_map[name] = (off, ne, dtype)
        if dtype == BF16:
            return self.arena.ap().bitcast(BF16)[:, 2 * off:2 * off + free]
        return self.arena.ap()[:, off:off + ne]

    def mark(self):
        self.marks.append((self.sb_off, dict(self.sb_map)))

    def release(self):
        self.sb_off, self.sb_map = self.marks.pop()

    def constv(self, val):
        name = f"__const_{val}"
        if name in self.sb_map:
            off, ne, dt = self.sb_map[name]
            return self.arena.ap()[:, off:off + 1]
        ap = self.sb(name, 1, F32)
        self.p.op("vector", lambda e: e.memset(ap, val), writes=[(name,)])
        return ap

    def ps(self, bank, nbanks=1, dtype=F32):
        if dtype == BF16:
            return self.psum_t.ap().bitcast(BF16)[:, bank * 1024:(bank + nbanks) * 1024]
        return self.psum_t.ap()[:, bank * 512:(bank + nbanks) * 512]

    @staticmethod
    def pk(bank, nbanks=1):
        return [("ps", bank + i) for i in range(nbanks)]


def _f32(x):
    return np.ascontiguousarray(x, dtype=np.float32)


def _bf16(x):
    return np.ascontiguousarray(np.asarray(x, dtype=np.float32)
                                .astype(ml_dtypes.bfloat16))


def pack_lhsT(w):
    K, M = w.shape
    Kc, Mc = K // P, M // P
    t = w.reshape(Kc, P, Mc, P).transpose(2, 1, 0, 3).reshape(Mc, P, Kc * P)
    return _bf16(t)


def pack_rhs(w):
    K, N = w.shape
    return _f32(w.reshape(K // P, P, N))


def pack_rhs_bf(w):
    K, N = w.shape
    return _bf16(w.reshape(K // P, P, N))


def rope_fm_tables(block, scale):
    raise NotImplementedError


def l0attn_tables():
    t = np.arange(S, dtype=np.float64)
    inv = 1.0 / (10000.0 ** (np.arange(32, dtype=np.float64) / 32.0))
    ang = np.outer(inv, t)
    cos = np.cos(ang)
    sin = np.sin(ang)
    perm = np.concatenate([np.arange(0, 16), np.arange(32, 48),
                           np.arange(16, 32), np.arange(48, 64)])
    cosq = np.empty((P, S))
    sinq = np.empty((P, S))
    for p in range(P):
        d = perm[p % 64]
        f = d % 32
        cosq[p] = cos[f]
        sinq[p] = sin[f] * (-1.0 if d < 32 else 1.0)
    cosk = np.empty((P, S))
    sink = np.empty((P, S))
    for p in range(P):
        s_ = p % 32
        d = s_ if s_ < 16 else 32 + (s_ - 16)
        f = d % 32
        cosk[p] = cos[f]
        sink[p] = sin[f] * (-1.0 if d < 32 else 1.0)
    sc = 1.0 / np.sqrt(64.0)
    return _f32(cosq * sc), _f32(sinq * sc), _f32(cosk), _f32(sink)


def l0ffn_tables():
    t = np.arange(S, dtype=np.float64)
    inv = 1.0 / (10000.0 ** (np.arange(32, dtype=np.float64) / 32.0))
    ang = np.outer(inv, t)
    cos = np.cos(ang)
    sin = np.sin(ang)
    cosr = np.empty((P, S))
    sinr = np.empty((P, S))
    for p in range(P):
        j = p % 64
        f = j % 32
        cosr[p] = cos[f]
        sinr[p] = sin[f] * (-1.0 if j < 32 else 1.0)
    return _f32(cosr), _f32(sinr)


def perm_swap16():
    m = np.zeros((P, P), np.float32)
    for p in range(P):
        m[(p // 32) * 32 + ((p % 32) ^ 16), p] = 1.0
    return m


def perm_swap32():
    m = np.zeros((P, P), np.float32)
    for p in range(P):
        m[(p // 64) * 64 + ((p % 64) ^ 32), p] = 1.0
    return m


def causal_mask_np():
    m = np.zeros((P, P), np.float32)
    m[np.triu_indices(P, 1)] = -1e30
    return m


def fm_proj(b, wname, slabs_dram, Kc, Mc, act_aps, act_key, evac,
            queue="gpsimd", banks=(0, 1, 2, 3), ntb=NTB, tw=512, wbuf=None,
            wkey=None, queues=None):
    p = b.p
    if wbuf is None:
        wbuf = [b.sb(f"W{wname}{i}", Kc * P, BF16) for i in range(2)]
    else:
        wbuf = [w[:, 0:Kc * P] for w in wbuf]
    nb = len(wbuf)
    if queues is None:
        queues = (queue,)
    wk = wkey if wkey is not None else "W" + wname
    for m in range(Mc):
        sl = wbuf[m % nb]
        q = queues[m % len(queues)]
        p.op(q, lambda e, m=m, sl=sl: e.dma_start(out=sl, in_=slabs_dram[m]),
             writes=[(wk, m % nb)], dma=True, tag=f"{wname}ld")
        for n in range(ntb):
            bank = banks[(m * ntb + n) % len(banks)]
            pp = b.ps(bank)[:, 0:tw]
            for k in range(Kc):
                p.op("tensor",
                     lambda e, sl=sl, k=k, n=n, pp=pp, st=(k == 0), sp=(k == Kc - 1),
                     a=act_aps[k]: e.matmul(pp, sl[:, k * P:(k + 1) * P],
                                            a[:, n * tw:(n + 1) * tw],
                                            start=st, stop=sp),
                     reads=[(wk, m % nb), (act_key, k)],
                     writes=b.pk(bank), tag=f"{wname}mm")
            evac(m, n, pp, b.pk(bank))


def rope_fm(b, tag, raw_key, raw_ap, out_keys, out_ap, permm, costab, sintab,
            cos_key, sin_key, perm_key, banks=(4, 5, 6, 7)):
    p = b.p
    t1 = b.sb(f"rp1{tag}", 512, F32)
    t2 = b.sb(f"rp2{tag}", 512, F32)
    for n in range(NTB):
        bank = banks[n % len(banks)]
        pp = b.ps(bank)
        sl = slice(n * 512, (n + 1) * 512)
        p.op("tensor", lambda e, pp=pp, sl=sl: e.matmul(
            pp, permm, raw_ap[:, sl], start=True, stop=True),
            reads=[perm_key, (raw_key, n)], writes=b.pk(bank), tag=f"{tag}sw")
        p.op("vector", lambda e, sl=sl: e.tensor_mul(t1, raw_ap[:, sl], costab[:, sl]),
             reads=[(raw_key, n), cos_key], writes=[(f"rp1{tag}",)])
        p.op("vector", lambda e, pp=pp, sl=sl: e.tensor_mul(t2, pp, sintab[:, sl]),
             reads=b.pk(bank) + [sin_key], writes=[(f"rp2{tag}",)])
        p.op("gpsimd", lambda e, sl=sl: e.tensor_add(out_ap[:, sl], t1, t2),
             reads=[(f"rp1{tag}",), (f"rp2{tag}",)], writes=[out_keys[n]],
             tag=f"{tag}add")


def ln_epilogue(b, tag, gen_tile, out_dram, n_range=range(NTB)):
    p = b.p
    ysb = [b.sb(f"lny{tag}{m}", 512, F32) for m in range(NCH)]
    sq = b.sb(f"lnsq{tag}", 512, F32)
    ones = b.sb(f"lnones{tag}", 1, F32)
    p.op("vector", lambda e: e.memset(ones, 1.0), writes=[("lnones", tag)])
    rows = b.sb(f"lnrows{tag}", 4 * 512, F32)
    ab = b.sb(f"lnab{tag}", 2 * 512, F32)
    rows_dram = b.dram(f"lnrowsd{tag}", [2, 512], F32)

    for n in n_range:
        st1 = b.ps(6)
        st2 = b.ps(7)
        for m in range(NCH):
            gen_tile(n, m, ysb[m])
            p.op("scalar", lambda e, m=m: e.activation(sq, ysb[m], AF.Square),
                 reads=[("lny", tag, m)], writes=[("lnsq", tag)])
            p.op("tensor", lambda e, m=m, st1=st1, ys=ysb[m]: e.matmul(
                st1[0:1, :], ones, ys, start=(m == 0), stop=(m == NCH - 1),
                skip_group_check=True),
                reads=[("lnones", tag), ("lny", tag, m)], writes=b.pk(6),
                tag="lnst1")
            p.op("tensor", lambda e, m=m, st2=st2: e.matmul(
                st2[0:1, :], ones, sq, start=(m == 0), stop=(m == NCH - 1),
                skip_group_check=True),
                reads=[("lnones", tag), ("lnsq", tag)], writes=b.pk(7),
                tag="lnst2")
        mrow = rows[0:1, 0:512]
        vrow = rows[0:1, 512:1024]
        arow = rows[0:1, 1024:1536]
        brow = rows[0:1, 1536:2048]
        p.op("vector", lambda e, st1=st1: e.tensor_scalar_mul(mrow, st1[0:1, :], 1.0 / D),
             reads=b.pk(6), writes=[("lnm", tag)])
        p.op("vector", lambda e, st2=st2: e.tensor_scalar_mul(vrow, st2[0:1, :], 1.0 / D),
             reads=b.pk(7), writes=[("lnv", tag)])
        p.op("vector", lambda e: e.tensor_mul(arow, mrow, mrow),
             reads=[("lnm", tag)], writes=[("lna", tag)])
        p.op("vector", lambda e: e.tensor_sub(vrow, vrow, arow),
             reads=[("lnv", tag), ("lna", tag)], writes=[("lnv", tag)])
        epsap = b.constv(1e-5)
        p.op("scalar", lambda e: e.activation(arow, vrow, AF.Sqrt, bias=epsap[0:1, :]),
             reads=[("lnv", tag), (f"__const_{1e-5}",)], writes=[("lna", tag)])
        p.op("vector", lambda e: e.reciprocal(arow, arow),
             reads=[("lna", tag)], writes=[("lna", tag)])
        p.op("vector", lambda e: e.tensor_mul(brow, mrow, arow),
             reads=[("lnm", tag), ("lna", tag)], writes=[("lnb", tag)])
        p.op("vector", lambda e: e.tensor_scalar_mul(brow, brow, -1.0),
             reads=[("lnb", tag)], writes=[("lnb", tag)])
        p.op("sync", lambda e: e.dma_start(out=rows_dram[0:1, :], in_=arow),
             reads=[("lna", tag)], writes=[("lnrd", tag)], dma=True)
        p.op("sync", lambda e: e.dma_start(out=rows_dram[1:2, :], in_=brow),
             reads=[("lnb", tag)], writes=[("lnrd2", tag)], dma=True)
        bc_in = bass.AP(rows_dram.tensor, 0, [[0, P], [512, 2], [1, 512]])
        p.op("gpsimd", lambda e: e.dma_start(
            out=ab.rearrange("p (a c) -> p a c", a=2), in_=bc_in),
            reads=[("lnrd", tag), ("lnrd2", tag)], writes=[("lnab", tag)],
            dma=True)
        a_b = ab[:, 0:512]
        b_b = ab[:, 512:1024]
        for m in range(NCH):
            p.op("vector", lambda e, m=m: e.tensor_mul(ysb[m], ysb[m], a_b),
                 reads=[("lny", tag, m), ("lnab", tag)], writes=[("lny", tag, m)])
            p.op("gpsimd", lambda e, m=m: e.tensor_add(ysb[m], ysb[m], b_b),
                 reads=[("lny", tag, m), ("lnab", tag)], writes=[("lny", tag, m)])
            p.op("sync", lambda e, m=m, n=n: e.dma_start(
                out=out_dram[m * P:(m + 1) * P, n * 512:(n + 1) * 512], in_=ysb[m]),
                reads=[("lny", tag, m)], writes=[("od", tag, m, n)], dma=True)


def stage_a(b, W, x0_dram, x1_dram):
    p = b.p
    NH = 32

    b.mark()
    qbf = [b.sb(f"qbfA{c}", S, BF16) for c in range(NCH)]
    krbf = [b.sb(f"krA{c}", S, BF16) for c in range(2)]
    ckv = [b.sb(f"ckvA{c}", S, BF16) for c in range(2)]
    kcb = [b.sb(f"kcA{c}", S, BF16) for c in range(2)]
    ident = b.sb("identA", P, BF16)
    p.op("sync", lambda e: e.dma_start(out=ident, in_=W["ident"]),
         writes=[("identA",)], dma=True)
    maskc = b.sb("maskA", P, F32)
    p.op("sync", lambda e: e.dma_start(out=maskc, in_=W["mask"]),
         writes=[("maskA",)], dma=True)
    permm = b.sb("permA", P, F32)
    p.op("sync", lambda e: e.dma_start(out=permm, in_=W["perm16"]),
         writes=[("permA",)], dma=True)

    b.mark()
    xbf = [b.sb(f"xbfA{c}", S, BF16) for c in range(NCH)]
    for c in range(NCH):
        p.op("gpsimd", lambda e, c=c: e.dma_start(
            out=xbf[c], in_=x0_dram[c * P:(c + 1) * P, :]),
            writes=[("xbf", c)], dma=True)
    tcos = b.sb("tcosA", S, F32)
    tsin = b.sb("tsinA", S, F32)
    praw = b.sb("prawA", S, F32)
    wsh = [b.sb(f"WshA{i}", NCH * P, BF16) for i in range(4)]

    def load_tabs(cname, sname):
        p.op("sync", lambda e: e.dma_start(out=tcos, in_=W[cname]),
             writes=[("tcosA",)], dma=True)
        p.op("sync", lambda e: e.dma_start(out=tsin, in_=W[sname]),
             writes=[("tsinA",)], dma=True)

    load_tabs("ta_cosq", "ta_sinq")

    def q_evac(m, n, pp, pk):
        p.op("scalar", lambda e, n=n, pp=pp: e.copy(
            praw[:, n * 512:(n + 1) * 512], pp), reads=pk,
            writes=[("prawA", n)], tag="qev")
        if n == NTB - 1:
            rope_fm(b, "q", "prawA", praw, [("qbf", m)] * 4, qbf[m], permm,
                    tcos, tsin, ("tcosA",), ("tsinA",), ("permA",))

    fm_proj(b, "qwA", W["q_wT"], NCH, NCH, xbf, "xbf", q_evac, wbuf=wsh, wkey="WshA", queues=("gpsimd", "scalar"))

    load_tabs("ta_cosk", "ta_sink")

    def kr_evac(m, n, pp, pk):
        p.op("scalar", lambda e, n=n, pp=pp: e.copy(
            praw[:, n * 512:(n + 1) * 512], pp), reads=pk,
            writes=[("prawA", n)], tag="krev")
        if n == NTB - 1:
            rope_fm(b, "kr", "prawA", praw, [("krbf", m)] * 4, krbf[m], permm,
                    tcos, tsin, ("tcosA",), ("tsinA",), ("permA",))

    fm_proj(b, "krwA", W["kr_wT"], NCH, 2, xbf, "xbf", kr_evac, wbuf=wsh, wkey="WshA", queues=("gpsimd", "scalar"))

    def ckv_evac(m, n, pp, pk):
        p.op("scalar", lambda e, m=m, n=n, pp=pp: e.copy(
            ckv[m][:, n * 512:(n + 1) * 512], pp), reads=pk,
            writes=[("ckv", m)], tag="ckvev")

    fm_proj(b, "dwA", W["down_wT"], NCH, 2, xbf, "xbf", ckv_evac, wbuf=wsh, wkey="WshA", queues=("gpsimd", "scalar"))

    def kc_evac(m, n, pp, pk):
        p.op("scalar", lambda e, m=m, n=n, pp=pp: e.copy(
            kcb[m][:, n * 512:(n + 1) * 512], pp), reads=pk,
            writes=[("kcb", m)], tag="kcev")

    fm_proj(b, "upkA", W["upk_wT"], 2, 2, ckv, "ckv", kc_evac, wbuf=wsh, wkey="WshA", queues=("gpsimd", "scalar"))

    p.barrier()
    b.release()

    b.mark()
    vtm = [b.sb(f"vtmA{t}", 512, BF16) for t in range(NTT)]
    upv = b.sb("upvA", 1024, BF16)
    for k in range(2):
        p.op("gpsimd", lambda e, k=k: e.dma_start(
            out=upv[:, k * 512:(k + 1) * 512], in_=W["upv_wT"][k]),
            writes=[("upvA", k)], dma=True)
    for tt in range(NTT):
        bank = tt % 4
        pp = b.ps(bank)
        for k in range(2):
            p.op("tensor", lambda e, tt=tt, k=k, pp=pp: e.matmul(
                pp, ckv[k][:, tt * P:(tt + 1) * P], upv[:, k * 512:(k + 1) * 512],
                start=(k == 0), stop=(k == 1)),
                reads=[("ckv", k), ("upvA", k)], writes=b.pk(bank), tag="vmm")
        p.op("scalar", lambda e, tt=tt, pp=pp: e.copy(vtm[tt], pp),
             reads=b.pk(bank), writes=[("vtm", tt)], tag="vev")

    Kb = [b.sb(f"KA{c}", S, BF16) for c in range(4)]
    Kd = [b.sb(f"KdA{c}", S, BF16) for c in range(4)]
    for h in range(8):
        sc_, sr = h // 4, 32 * (h % 4)
        dc, dr = h // 2, 64 * (h % 2)
        for (srcl, key, off) in ((krbf, "krbf", 0), (kcb, "kcb", 32)):
            src = srcl[sc_][sr:sr + 32, :]
            p.op("gpsimd", lambda e, src=src, o=Kb[dc][dr + off:dr + off + 32, :]:
                 e.dma_start(out=o, in_=src),
                 reads=[(key, sc_)], writes=[("Kb", dc)], dma=True)
            p.op("gpsimd", lambda e, src=src,
                 o=Kd[dc][(64 - dr) + off:(64 - dr) + off + 32, :]:
                 e.dma_start(out=o, in_=src),
                 reads=[(key, sc_)], writes=[("Kd", dc)], dma=True)

    att_dram = b.dram("attA", [D, S], BF16)
    NPR = 3
    probs = [b.sb(f"probsA{j}", S, BF16) for j in range(NPR)]
    probsT = [b.sb(f"probsTA{j}", S, BF16) for j in range(NPR)]
    nmv = [b.sb(f"nmvA{j}", 8, F32) for j in range(NPR)]
    gam = [b.sb(f"gamA{j}", 12, F32) for j in range(NPR)]
    rsum = [b.sb(f"rsA{j}", 1, F32) for j in range(NPR)]
    diagb = [b.sb(f"diagA{j}", 4 * P, BF16) for j in range(NPR)]
    attsb = [b.sb(f"attsbA{j}", P, BF16) for j in range(NPR)]

    it = 0
    scb_cur = [0]

    def emit_front_A(qc, Kkey, qap, kap, pr, i, nk, nkb):
        nm = nmv[pr]
        ga = gam[pr]
        dchunk = (i * P) // 512
        doff = (i * P) % 512
        for c in range(nkb):
            w = min(512, nk * P - c * 512)
            bank = scb_cur[0]
            scb_cur[0] = (bank + 1) % 4
            pp = b.ps(bank)
            p.op("tensor", lambda e, c=c, w=w, i=i, qap=qap, kap=kap, pp=pp:
                 e.matmul(pp[:, 0:w], qap[:, i * P:(i + 1) * P],
                          kap[:, c * 512:c * 512 + w], start=True, stop=True),
                 reads=[("qbf", qc), Kkey], writes=b.pk(bank), tag="sc")
            if c == dchunk:
                p.op("vector", lambda e, pp=pp, doff=doff: e.tensor_add(
                    pp[:, doff:doff + P], pp[:, doff:doff + P], maskc),
                    reads=b.pk(bank) + [("maskA",)], writes=b.pk(bank),
                    tag="mask")
            p.op("vector", lambda e, pp=pp, w=w, nm=nm, c=c: e.reduce_max(
                nm[:, c:c + 1], pp[:, 0:w], axis=AX.X, negate=True),
                reads=b.pk(bank), writes=[("nmA", pr, c)], tag="rmax")
            p.op("scalar", lambda e, pp=pp, w=w, c=c, pr=pr, nm=nm: e.activation(
                probs[pr][:, c * 512:c * 512 + w], pp[:, 0:w], AF.Exp,
                bias=nm[:, c:c + 1], accum_out=nm[:, 4 + c:5 + c]),
                reads=b.pk(bank) + [("nmA", pr, c)],
                writes=[("probsA", pr, c), ("seA", pr, c)], tag="exp")
        mneg = rsum[pr]
        nmk = [("nmA", pr, c) for c in range(nkb)]
        sek = [("seA", pr, c) for c in range(nkb)]
        if nkb == 1:
            p.op("vector", lambda e, nm=nm: e.reciprocal(ga[:, 0:1], nm[:, 4:5]),
                 reads=sek, writes=[("gamA", pr)], tag="cmb")
        else:
            p.op("vector", lambda e, nm=nm: e.tensor_tensor(
                mneg, nm[:, 0:1], nm[:, 1:2], ALU.min),
                reads=nmk, writes=[("MnA", pr)], tag="cmb")
            for c in range(2, nkb):
                p.op("vector", lambda e, nm=nm, c=c: e.tensor_tensor(
                    mneg, mneg, nm[:, c:c + 1], ALU.min),
                    reads=nmk + [("MnA", pr)], writes=[("MnA", pr)], tag="cmb")
            p.op("scalar", lambda e, nm=nm, nkb=nkb: e.activation(
                ga[:, 0:nkb], nm[:, 0:nkb], AF.Exp, scale=-1.0, bias=mneg),
                reads=nmk + [("MnA", pr)], writes=[("gamA", pr)], tag="cmb")
            p.op("vector", lambda e, nm=nm, nkb=nkb: e.tensor_mul(
                ga[:, 4:4 + nkb], ga[:, 0:nkb], nm[:, 4:4 + nkb]),
                reads=sek + [("gamA", pr)], writes=[("gam2A", pr)], tag="cmb")
            p.op("vector", lambda e, nkb=nkb: e.reduce_sum(
                ga[:, 8:9], ga[:, 4:4 + nkb], axis=AX.X),
                reads=[("gam2A", pr)], writes=[("ssA", pr)], tag="cmb")
            p.op("vector", lambda e: e.reciprocal(ga[:, 8:9], ga[:, 8:9]),
                 reads=[("ssA", pr)], writes=[("ssA", pr)], tag="cmb")
            p.op("vector", lambda e, nkb=nkb: e.tensor_scalar_mul(
                ga[:, 0:nkb], ga[:, 0:nkb], ga[:, 8:9]),
                reads=[("gamA", pr), ("ssA", pr)], writes=[("gamA", pr)],
                tag="cmb")
        for c in range(nkb):
            p.op("vector", lambda e, c=c: e.tensor_scalar_mul(
                diagb[pr][:, c * P:(c + 1) * P], ident, ga[:, c:c + 1]),
                reads=[("identA",), ("gamA", pr)],
                writes=[("dgA", pr, c)], tag="diag")

    def emit_tail_A(g, i, pr, nk, nkb):
        for kt in range(nk):
            bank = 4 + (kt // 4) % 2
            sub = kt % 4
            c = kt // 4
            p.op("tensor", lambda e, kt=kt, bank=bank, sub=sub, pr=pr, c=c:
                 e.matmul(b.ps(bank)[:, sub * 128:(sub + 1) * 128],
                          probs[pr][:, kt * P:(kt + 1) * P],
                          diagb[pr][:, c * P:(c + 1) * P],
                          start=(sub == 0), stop=True, skip_group_check=True),
                 reads=[("probsA", pr, c), ("dgA", pr, c)], writes=b.pk(bank),
                 tag="ptr")
            if sub == 3 or kt == nk - 1:
                n0 = (kt // 4) * 4
                cntc = kt - n0 + 1
                eng = "scalar" if (kt // 4) % 2 == 0 else "vector"
                p.op(eng, lambda e, n0=n0, cntc=cntc, bank=bank, pr=pr:
                     (e.copy if hasattr(e, "copy") else e.tensor_copy)(
                         probsT[pr][:, n0 * 128:(n0 + cntc) * 128],
                         b.ps(bank)[:, 0:cntc * 128]),
                     reads=b.pk(bank), writes=[("probsTA", pr, n0 // 4)],
                     tag="ptev")
        avb = 6 + (i % 2)
        av = b.ps(avb)
        for kt in range(nk):
            p.op("tensor", lambda e, kt=kt, g=g, av=av, nk=nk, pr=pr:
                 e.matmul(av[0:64, 0:P],
                          vtm[kt][:, (g // 4) * 64:(g // 4) * 64 + 64],
                          probsT[pr][:, kt * P:(kt + 1) * P],
                          start=(kt == 0), stop=(kt == nk - 1)),
                 reads=[("vtm", kt), ("probsTA", pr, kt // 4)],
                 writes=b.pk(avb), tag="pv")
        p.op("vector", lambda e, av=av, pr=pr, avb=avb: e.tensor_copy(
            attsb[pr][0:64, :], av[0:64, 0:P]),
            reads=b.pk(avb), writes=[("attsbA", pr)], tag="pvev")
        p.op("sync", lambda e, g=g, i=i, pr=pr: e.dma_start(
            out=att_dram[64 * g:64 * g + 64, i * P:(i + 1) * P],
            in_=attsb[pr][0:64, :]),
            reads=[("attsbA", pr)], writes=[("attAd", g, i)], dma=True)

    pend = None
    for g in range(NH):
        h = g // 4
        qc, qb_ = g // 2, 64 * (g % 2)
        aligned = (64 * (h % 2) == qb_)
        Ksrc, Kkey = (Kb, "Kb") if aligned else (Kd, "Kd")
        kc_ = h // 2
        qap = qbf[qc][qb_:qb_ + 64, :]
        kap = Ksrc[kc_][qb_:qb_ + 64, :]
        for i in range(NTT):
            nk = i + 1
            nkb = (nk * P + 511) // 512
            pr = it % NPR
            it += 1
            emit_front_A(qc, (Kkey, kc_), qap, kap, pr, i, nk, nkb)
            if pend is not None:
                emit_tail_A(*pend)
            pend = (g, i, pr, nk, nkb)
    emit_tail_A(*pend)

    p.barrier()
    b.release()
    b.release()

    b.mark()
    wall = [b.sb(f"owA{i}", NCH * P, BF16) for i in range(NCH)]
    for m in range(NCH):
        q = ("gpsimd", "sync", "scalar")[m % 3]
        p.op(q, lambda e, m=m: e.dma_start(out=wall[m], in_=W["o_wT"][m]),
             writes=[("owA", m)], dma=True, tag="owld")
    actb = [b.sb(f"oactA{k}", 512, BF16) for k in range(NCH)]
    xres = [b.sb(f"oxA{j}", 512, F32) for j in range(2)]

    def gen_tile(n, m, ysb):
        sl = wall[m]
        if m == 0:
            for k in range(NCH):
                p.op("sync", lambda e, k=k, n=n: e.dma_start(
                    out=actb[k], in_=att_dram[k * P:(k + 1) * P, n * 512:(n + 1) * 512]),
                    writes=[("oact", k)], dma=True)
        xr = xres[m % 2]
        p.op("sync", lambda e, m=m, n=n, xr=xr: e.dma_start(
            out=xr, in_=x0_dram[m * P:(m + 1) * P, n * 512:(n + 1) * 512]),
            writes=[("oxA", m % 2)], dma=True)
        bank = m % 4
        pp = b.ps(bank)
        for k in range(NCH):
            p.op("tensor", lambda e, k=k, sl=sl, pp=pp: e.matmul(
                pp, sl[:, k * P:(k + 1) * P], actb[k],
                start=(k == 0), stop=(k == NCH - 1), skip_group_check=True),
                reads=[("owA", m), ("oact", k)], writes=b.pk(bank), tag="omm")
        p.op("vector", lambda e, m=m, pp=pp, ysb=ysb, xr=xr: e.tensor_add(ysb, pp, xr),
             reads=b.pk(bank) + [("oxA", m % 2)], writes=[("lny", "A", m)], tag="oadd")

    ln_epilogue(b, "A", gen_tile, x1_dram)
    p.barrier()
    b.release()


def rms_norm(b, tag, src_dram, w_sb, w_key, out_chunks, out_key, f32_dram=None):
    p = b.p
    xs = [b.sb(f"rms_x{tag}{m}", 512, F32) for m in range(NCH)]
    xf = b.sb(f"rms_xf{tag}", 512, F32) if f32_dram is not None else None
    sq = b.sb(f"rms_sq{tag}", 512, F32)
    ones = b.constv(1.0)
    rows = b.sb(f"rms_row{tag}", 512, F32)
    abuf = b.sb(f"rms_ab{tag}", 512, F32)
    rows_dram = b.dram(f"rms_rd{tag}", [1, 512], F32)
    epsap = b.constv(1e-5)
    for n in range(NTB):
        st = b.ps(6)
        for m in range(NCH):
            p.op("sync", lambda e, m=m, n=n: e.dma_start(
                out=xs[m], in_=src_dram[m * P:(m + 1) * P, n * 512:(n + 1) * 512]),
                writes=[("rmsx", tag, m)], dma=True)
            p.op("scalar", lambda e, m=m: e.activation(sq, xs[m], AF.Square),
                 reads=[("rmsx", tag, m)], writes=[("rmssq", tag)])
            p.op("tensor", lambda e, m=m, st=st: e.matmul(
                st[0:1, :], ones, sq, start=(m == 0), stop=(m == NCH - 1),
                skip_group_check=True),
                reads=[(f"__const_{1.0}",), ("rmssq", tag)], writes=b.pk(6),
                tag="rmsst")
        p.op("vector", lambda e, st=st: e.tensor_scalar_mul(rows[0:1, :], st[0:1, :], 1.0 / D),
             reads=b.pk(6), writes=[("rmsr", tag)])
        p.op("scalar", lambda e: e.activation(rows[0:1, :], rows[0:1, :], AF.Sqrt,
                                              bias=epsap[0:1, :]),
             reads=[("rmsr", tag), (f"__const_{1e-5}",)], writes=[("rmsr", tag)])
        p.op("vector", lambda e: e.reciprocal(rows[0:1, :], rows[0:1, :]),
             reads=[("rmsr", tag)], writes=[("rmsr", tag)])
        p.op("sync", lambda e: e.dma_start(out=rows_dram, in_=rows[0:1, :]),
             reads=[("rmsr", tag)], writes=[("rmsrd", tag)], dma=True)
        bc = bass.AP(rows_dram.tensor, 0, [[0, P], [1, 512]])
        p.op("gpsimd", lambda e: e.dma_start(out=abuf, in_=bc),
             reads=[("rmsrd", tag)], writes=[("rmsab", tag)], dma=True)
        for m in range(NCH):
            p.op("vector", lambda e, m=m: e.tensor_mul(xs[m], xs[m], abuf),
                 reads=[("rmsx", tag, m), ("rmsab", tag)], writes=[("rmsx", tag, m)])
            p.op("vector", lambda e, m=m, n=n: e.tensor_scalar_mul(
                out_chunks[m][:, n * 512:(n + 1) * 512], xs[m], w_sb[:, m:m + 1]),
                reads=[("rmsx", tag, m), w_key], writes=[(out_key, m)],
                tag="rmsw")
            if f32_dram is not None:
                p.op("vector", lambda e, m=m: e.tensor_scalar_mul(
                    xf, xs[m], w_sb[:, m:m + 1]),
                    reads=[("rmsx", tag, m), w_key], writes=[("rmsxf", tag)],
                    tag="rmswf")
                p.op("sync", lambda e, m=m, n=n: e.dma_start(
                    out=f32_dram[m * P:(m + 1) * P, n * 512:(n + 1) * 512], in_=xf),
                    reads=[("rmsxf", tag)], writes=[("xf32d", tag, m, n)], dma=True)


def stage_b(b, W, x1_dram, x2_dram, phase1_only=False):
    p = b.p
    NHB = 16

    b.mark()
    ln12 = b.sb("ln12B", 32, F32)
    p.op("sync", lambda e: e.dma_start(out=ln12[:, 0:16], in_=W["ln1w16"]),
         writes=[("ln1w",)], dma=True)
    p.op("sync", lambda e: e.dma_start(out=ln12[:, 16:32], in_=W["ln2w16"]),
         writes=[("ln2w",)], dma=True)
    identb = b.sb("identB", P, BF16)
    p.op("sync", lambda e: e.dma_start(out=identb, in_=W["ident"]),
         writes=[("identB",)], dma=True)
    identf = b.sb("identfB", P, F32)
    p.op("sync", lambda e: e.dma_start(out=identf, in_=W["identf"]),
         writes=[("identfB",)], dma=True)
    maskc = b.sb("maskB", P, F32)
    p.op("sync", lambda e: e.dma_start(out=maskc, in_=W["mask"]),
         writes=[("maskB",)], dma=True)
    permm = b.sb("permB", P, F32)
    p.op("sync", lambda e: e.dma_start(out=permm, in_=W["perm32"]),
         writes=[("permB",)], dma=True)

    K_dram = b.dram("KB", [D, S], BF16)
    Q_dram = b.dram("QB", [D, S], BF16)
    v_dram = b.dram("VB", [S, D], BF16)
    att_dram = b.dram("attB", [D, S], BF16)
    h_dram = b.dram("hB", [D, S], F32)
    moe_dram = b.dram("moeB", [D, S], F32)
    comb_dram = b.dram("combB", [8, S], F32)

    b.mark()
    xn = [b.sb(f"xnB{c}", S, BF16) for c in range(NCH)]
    rms_norm(b, "B1", x1_dram, ln12[:, 0:16], ("ln1w",), xn, "xnB")

    kvd = [b.sb(f"kvdB{c}", S, BF16) for c in range(4)]
    qd = [b.sb(f"qdB{c}", S, BF16) for c in range(4)]
    wsh = [b.sb(f"WshB{i}", NCH * P, BF16) for i in range(3)]
    stg2 = [b.sb(f"stgB{j}", S, BF16) for j in range(2)]
    praw = b.sb("prawB", S, F32)
    tcos = b.sb("tcosB", S, F32)
    tsin = b.sb("tsinB", S, F32)

    def mk_evac(dst, key):
        def _evac(m, n, pp, pk):
            p.op("scalar", lambda e, m=m, n=n, pp=pp: e.copy(
                dst[m][:, n * 512:(n + 1) * 512], pp), reads=pk,
                writes=[(key, m)], tag=f"{key}ev")
        return _evac

    fm_proj(b, "kvdB", W["kvdw"], NCH, 4, xn, "xnB", mk_evac(kvd, "kvdB"), wbuf=wsh, wkey="WshB", queues=("gpsimd", "scalar", "sync"))
    fm_proj(b, "qdB", W["qdw"], NCH, 4, xn, "xnB", mk_evac(qd, "qdB"), wbuf=wsh, wkey="WshB", queues=("gpsimd", "scalar", "sync"))


    def kp_evac(m, n, pp, pk):
        stg = stg2[m % 2]
        p.op("scalar", lambda e, n=n, pp=pp, stg=stg: e.copy(
            stg[:, n * 512:(n + 1) * 512], pp), reads=pk,
            writes=[("stgB", m % 2, n)], tag="kpev")
        if n == NTB - 1:
            for half in range(2):
                hrow = 128 * (2 * m + half)
                p.op("sync", lambda e, hrow=hrow, half=half, stg=stg: e.dma_start(
                    out=K_dram[hrow:hrow + 64, :], in_=stg[64 * half:64 * half + 64, :]),
                    reads=[("stgB", m % 2, nn) for nn in range(4)],
                    writes=[("KBd", 2 * m + half, "n")], dma=True)

    fm_proj(b, "kuB", W["kuw"], 4, 8, kvd, "kvdB", kp_evac, wbuf=wsh, wkey="WshB", queues=("gpsimd", "scalar", "sync"))

    def qp_evac(m, n, pp, pk):
        stg = stg2[m % 2]
        p.op("vector", lambda e, n=n, pp=pp, stg=stg: e.tensor_scalar_mul(
            stg[:, n * 512:(n + 1) * 512], pp, 0.08838834764831845), reads=pk,
            writes=[("stgB", m % 2, n)], tag="qpev")
        if n == NTB - 1:
            for half in range(2):
                hrow = 128 * (2 * m + half)
                p.op("sync", lambda e, hrow=hrow, half=half, stg=stg: e.dma_start(
                    out=Q_dram[hrow:hrow + 64, :], in_=stg[64 * half:64 * half + 64, :]),
                    reads=[("stgB", m % 2, nn) for nn in range(4)],
                    writes=[("QBd", 2 * m + half, "n")], dma=True)

    fm_proj(b, "quB", W["quw"], 4, 8, qd, "qdB", qp_evac, wbuf=wsh, wkey="WshB", queues=("gpsimd", "scalar", "sync"))

    p.op("sync", lambda e: e.dma_start(out=tcos, in_=W["tb_cos"]),
         writes=[("tcosB",)], dma=True)
    p.op("sync", lambda e: e.dma_start(out=tsin, in_=W["tb_sin"]),
         writes=[("tsinB",)], dma=True)

    def kr_evac(m, n, pp, pk):
        stg = stg2[m % 2]
        p.op("scalar", lambda e, n=n, pp=pp: e.copy(
            praw[:, n * 512:(n + 1) * 512], pp), reads=pk,
            writes=[("prawB", n)], tag="krevB")
        if n == NTB - 1:
            rope_fm(b, "krB", "prawB", praw,
                    [("stgB", m % 2, nn) for nn in range(4)],
                    stg, permm, tcos, tsin, ("tcosB",), ("tsinB",), ("permB",))
            for half in range(2):
                hrow = 128 * (2 * m + half)
                p.op("sync", lambda e, hrow=hrow, half=half, stg=stg: e.dma_start(
                    out=K_dram[hrow + 64:hrow + 128, :],
                    in_=stg[64 * half:64 * half + 64, :]),
                    reads=[("stgB", m % 2, nn) for nn in range(4)],
                    writes=[("KBd", 2 * m + half, "r")], dma=True)

    fm_proj(b, "rkB", W["ropekw"], NCH, 8, xn, "xnB", kr_evac, wbuf=wsh, wkey="WshB", queues=("gpsimd", "scalar", "sync"))

    p.op("sync", lambda e: e.dma_start(out=tcos, in_=W["tb_cosq"]),
         writes=[("tcosB",)], dma=True)
    p.op("sync", lambda e: e.dma_start(out=tsin, in_=W["tb_sinq"]),
         writes=[("tsinB",)], dma=True)

    def qr_evac(m, n, pp, pk):
        stg = stg2[m % 2]
        p.op("scalar", lambda e, n=n, pp=pp: e.copy(
            praw[:, n * 512:(n + 1) * 512], pp), reads=pk,
            writes=[("prawB", n)], tag="qrevB")
        if n == NTB - 1:
            rope_fm(b, "qrB", "prawB", praw,
                    [("stgB", m % 2, nn) for nn in range(4)],
                    stg, permm, tcos, tsin, ("tcosB",), ("tsinB",), ("permB",))
            for half in range(2):
                hrow = 128 * (2 * m + half)
                p.op("sync", lambda e, hrow=hrow, half=half, stg=stg: e.dma_start(
                    out=Q_dram[hrow + 64:hrow + 128, :],
                    in_=stg[64 * half:64 * half + 64, :]),
                    reads=[("stgB", m % 2, nn) for nn in range(4)],
                    writes=[("QBd", 2 * m + half, "r")], dma=True)

    fm_proj(b, "rqB", W["ropeqw"], 4, 8, qd, "qdB", qr_evac, wbuf=wsh, wkey="WshB", queues=("gpsimd", "scalar", "sync"))

    vuw = b.sb("vuwB", 4 * S, BF16)
    for k in range(4):
        p.op("gpsimd", lambda e, k=k: e.dma_start(
            out=vuw[:, k * S:(k + 1) * S], in_=W["vuw"][k]),
            writes=[("vuwB", k)], dma=True)
    vsb = b.sb("vsbB", 512, BF16)
    for tt in range(NTT):
        for nb in range(4):
            bank = (tt * 4 + nb) % 4
            pp = b.ps(bank)
            for k in range(4):
                p.op("tensor", lambda e, tt=tt, nb=nb, k=k, pp=pp: e.matmul(
                    pp, kvd[k][:, tt * P:(tt + 1) * P],
                    vuw[:, k * S + nb * 512:k * S + (nb + 1) * 512],
                    start=(k == 0), stop=(k == 3)),
                    reads=[("kvdB", k), ("vuwB", k)], writes=b.pk(bank), tag="vmmB")
            p.op("vector", lambda e, pp=pp: e.tensor_copy(vsb, pp),
                 reads=b.pk(bank), writes=[("vsbB",)], tag="vevB")
            p.op("sync", lambda e, tt=tt, nb=nb: e.dma_start(
                out=v_dram[tt * P:(tt + 1) * P, nb * 512:(nb + 1) * 512], in_=vsb),
                reads=[("vsbB",)], writes=[("VBd", tt, nb)], dma=True)

    p.barrier()
    b.release()
    if phase1_only:
        b.release()
        return

    b.mark()
    qh = [b.sb(f"qhB{i}", S, BF16) for i in range(2)]
    kh = [b.sb(f"khB{i}", S, BF16) for i in range(2)]
    vh = [b.sb(f"vhB{i}", S, BF16) for i in range(2)]
    NPRB = 3
    probs = [b.sb(f"probsB{j}", S, BF16) for j in range(NPRB)]
    probsT = [b.sb(f"probsTB{j}", S, BF16) for j in range(NPRB)]
    nmv = [b.sb(f"nmvB{j}", 8, F32) for j in range(NPRB)]
    gam = [b.sb(f"gamB{j}", 12, F32) for j in range(NPRB)]
    rsum = [b.sb(f"rsB{j}", 1, F32) for j in range(NPRB)]
    diagb = [b.sb(f"diagB{j}", 4 * P, BF16) for j in range(NPRB)]
    attsb = [b.sb(f"attsbB{j}", P, BF16) for j in range(NPRB)]

    scb_curB = [0]

    def emit_front_B(sl_, qap, kap, pr, i, nk, nkb):
        nm = nmv[pr]
        ga = gam[pr]
        dchunk = (i * P) // 512
        doff = (i * P) % 512
        for c in range(nkb):
            w = min(512, nk * P - c * 512)
            bank = scb_curB[0]
            scb_curB[0] = (bank + 1) % 4
            pp = b.ps(bank)
            p.op("tensor", lambda e, c=c, w=w, i=i, qap=qap, kap=kap, pp=pp:
                 e.matmul(pp[:, 0:w], qap[:, i * P:(i + 1) * P],
                          kap[:, c * 512:c * 512 + w], start=True, stop=True),
                 reads=[("qhB", sl_), ("khB", sl_)], writes=b.pk(bank),
                 tag="scB")
            if c == dchunk:
                p.op("vector", lambda e, pp=pp, doff=doff: e.tensor_add(
                    pp[:, doff:doff + P], pp[:, doff:doff + P], maskc),
                    reads=b.pk(bank) + [("maskB",)], writes=b.pk(bank),
                    tag="maskB")
            p.op("vector", lambda e, pp=pp, w=w, nm=nm, c=c: e.reduce_max(
                nm[:, c:c + 1], pp[:, 0:w], axis=AX.X, negate=True),
                reads=b.pk(bank), writes=[("nmB", pr, c)], tag="rmaxB")
            p.op("scalar", lambda e, pp=pp, w=w, c=c, pr=pr, nm=nm: e.activation(
                probs[pr][:, c * 512:c * 512 + w], pp[:, 0:w], AF.Exp,
                bias=nm[:, c:c + 1], accum_out=nm[:, 4 + c:5 + c]),
                reads=b.pk(bank) + [("nmB", pr, c)],
                writes=[("probsB", pr, c), ("seB", pr, c)], tag="expB")
        mneg = rsum[pr]
        nmk = [("nmB", pr, c) for c in range(nkb)]
        sek = [("seB", pr, c) for c in range(nkb)]
        if nkb == 1:
            p.op("vector", lambda e, nm=nm: e.reciprocal(ga[:, 0:1], nm[:, 4:5]),
                 reads=sek, writes=[("gamB", pr)], tag="cmbB")
        else:
            p.op("vector", lambda e, nm=nm: e.tensor_tensor(
                mneg, nm[:, 0:1], nm[:, 1:2], ALU.min),
                reads=nmk, writes=[("MnB", pr)], tag="cmbB")
            for c in range(2, nkb):
                p.op("vector", lambda e, nm=nm, c=c: e.tensor_tensor(
                    mneg, mneg, nm[:, c:c + 1], ALU.min),
                    reads=nmk + [("MnB", pr)], writes=[("MnB", pr)], tag="cmbB")
            p.op("scalar", lambda e, nm=nm, nkb=nkb: e.activation(
                ga[:, 0:nkb], nm[:, 0:nkb], AF.Exp, scale=-1.0, bias=mneg),
                reads=nmk + [("MnB", pr)], writes=[("gamB", pr)], tag="cmbB")
            p.op("vector", lambda e, nm=nm, nkb=nkb: e.tensor_mul(
                ga[:, 4:4 + nkb], ga[:, 0:nkb], nm[:, 4:4 + nkb]),
                reads=sek + [("gamB", pr)], writes=[("gam2B", pr)], tag="cmbB")
            p.op("vector", lambda e, nkb=nkb: e.reduce_sum(
                ga[:, 8:9], ga[:, 4:4 + nkb], axis=AX.X),
                reads=[("gam2B", pr)], writes=[("ssB", pr)], tag="cmbB")
            p.op("vector", lambda e: e.reciprocal(ga[:, 8:9], ga[:, 8:9]),
                 reads=[("ssB", pr)], writes=[("ssB", pr)], tag="cmbB")
            p.op("vector", lambda e, nkb=nkb: e.tensor_scalar_mul(
                ga[:, 0:nkb], ga[:, 0:nkb], ga[:, 8:9]),
                reads=[("gamB", pr), ("ssB", pr)], writes=[("gamB", pr)],
                tag="cmbB")
        for c in range(nkb):
            p.op("vector", lambda e, c=c: e.tensor_scalar_mul(
                diagb[pr][:, c * P:(c + 1) * P], identb, ga[:, c:c + 1]),
                reads=[("identB",), ("gamB", pr)],
                writes=[("dgB", pr, c)], tag="diagB")

    def emit_tail_B(g, i, pr, nk, nkb, sl_, vap):
        for kt in range(nk):
            bank = 4 + (kt // 4) % 2
            sub = kt % 4
            c = kt // 4
            eng = "scalar" if (kt // 4) % 2 == 0 else "vector"
            p.op("tensor", lambda e, kt=kt, bank=bank, sub=sub, pr=pr, c=c:
                 e.matmul(b.ps(bank)[:, sub * 128:(sub + 1) * 128],
                          probs[pr][:, kt * P:(kt + 1) * P],
                          diagb[pr][:, c * P:(c + 1) * P],
                          start=(sub == 0), stop=True, skip_group_check=True),
                 reads=[("probsB", pr, c), ("dgB", pr, c)], writes=b.pk(bank),
                 tag="ptrB")
            if sub == 3 or kt == nk - 1:
                n0 = (kt // 4) * 4
                cntc = kt - n0 + 1
                p.op(eng, lambda e, n0=n0, cntc=cntc, bank=bank, pr=pr:
                     (e.copy if hasattr(e, "copy") else e.tensor_copy)(
                         probsT[pr][:, n0 * 128:(n0 + cntc) * 128],
                         b.ps(bank)[:, 0:cntc * 128]),
                     reads=b.pk(bank), writes=[("probsTB", pr, n0 // 4)],
                     tag="ptevB")
        avb = 6 + (i % 2)
        av = b.ps(avb)
        for kt in range(nk):
            p.op("tensor", lambda e, kt=kt, av=av, nk=nk, vap=vap, pr=pr:
                 e.matmul(av[:, 0:P], vap[:, kt * P:(kt + 1) * P],
                          probsT[pr][:, kt * P:(kt + 1) * P],
                          start=(kt == 0), stop=(kt == nk - 1)),
                 reads=[("vhB", sl_), ("probsTB", pr, kt // 4)],
                 writes=b.pk(avb), tag="pvB")
        p.op("vector", lambda e, av=av, pr=pr, avb=avb: e.tensor_copy(
            attsb[pr], av[:, 0:P]),
            reads=b.pk(avb), writes=[("attsbB", pr)], tag="pvevB")
        p.op("sync", lambda e, g=g, i=i, pr=pr: e.dma_start(
            out=att_dram[g * P:(g + 1) * P, i * P:(i + 1) * P],
            in_=attsb[pr]),
            reads=[("attsbB", pr)], writes=[("attBd", g, i)], dma=True)

    it = 0
    pend = None
    for g in range(NHB):
        sl_ = g % 2
        p.op("sync", lambda e, g=g, d=qh[sl_]: e.dma_start(
            out=d, in_=Q_dram[g * P:(g + 1) * P, :]),
            reads=[("QBd", g, "n"), ("QBd", g, "r")],
            writes=[("qhB", sl_)], dma=True)
        p.op("sync", lambda e, g=g, d=kh[sl_]: e.dma_start(
            out=d, in_=K_dram[g * P:(g + 1) * P, :]),
            reads=[("KBd", g, "n"), ("KBd", g, "r")],
            writes=[("khB", sl_)], dma=True)
        vin = bass.AP(v_dram.tensor, g * P,
                      [[D, P], [D * P, NTT], [1, P]])
        p.op("gpsimd", lambda e, g=g, d=vh[sl_], vin=vin: e.dma_start(
            out=d.rearrange("p (t c) -> p t c", t=NTT), in_=vin),
            reads=[("VBd", tt, nb) for tt in range(NTT) for nb in range(4)],
            writes=[("vhB", sl_)], dma=True)
        qap, kap, vap = qh[sl_], kh[sl_], vh[sl_]
        for i in range(NTT):
            nk = i + 1
            nkb = (nk * P + 511) // 512
            pr = it % NPRB
            it += 1
            emit_front_B(sl_, qap, kap, pr, i, nk, nkb)
            if pend is not None:
                emit_tail_B(*pend)
            pend = (g, i, pr, nk, nkb, sl_, vap)
    emit_tail_B(*pend)

    p.barrier()
    b.release()

    b.mark()
    wall = [b.sb(f"owB{i}", NCH * P, BF16) for i in range(NCH)]
    for m in range(NCH):
        q = ("gpsimd", "sync", "scalar")[m % 3]
        p.op(q, lambda e, m=m: e.dma_start(out=wall[m], in_=W["owB"][m]),
             writes=[("owB", m)], dma=True, tag="owldB")
    actb = [b.sb(f"oactB{k}", 512, BF16) for k in range(NCH)]
    xres = [b.sb(f"oxB{j}", 512, F32) for j in range(2)]
    hsb = [b.sb(f"hsbB{j}", 512, F32) for j in range(2)]
    for n in range(NTB):
        for k in range(NCH):
            p.op("sync", lambda e, k=k, n=n: e.dma_start(
                out=actb[k], in_=att_dram[k * P:(k + 1) * P, n * 512:(n + 1) * 512]),
                writes=[("oactB", k)], dma=True)
        for m in range(NCH):
            sl = wall[m]
            xr = xres[m % 2]
            hs = hsb[m % 2]
            p.op("sync", lambda e, m=m, n=n, xr=xr: e.dma_start(
                out=xr, in_=x1_dram[m * P:(m + 1) * P, n * 512:(n + 1) * 512]),
                writes=[("oxB", m % 2)], dma=True)
            bank = m % 4
            pp = b.ps(bank)
            for k in range(NCH):
                p.op("tensor", lambda e, k=k, sl=sl, pp=pp: e.matmul(
                    pp, sl[:, k * P:(k + 1) * P], actb[k],
                    start=(k == 0), stop=(k == NCH - 1)),
                    reads=[("owB", m), ("oactB", k)], writes=b.pk(bank),
                    tag="ommB")
            p.op("vector", lambda e, pp=pp, xr=xr, hs=hs: e.tensor_add(hs, pp, xr),
                 reads=b.pk(bank) + [("oxB", m % 2)], writes=[("hsbB", m % 2)],
                 tag="oaddB")
            p.op("sync", lambda e, m=m, n=n, hs=hs: e.dma_start(
                out=h_dram[m * P:(m + 1) * P, n * 512:(n + 1) * 512], in_=hs),
                reads=[("hsbB", m % 2)], writes=[("hBd", m, n)], dma=True)
    p.barrier()
    b.release()

    b.mark()
    xm = [b.sb(f"xmB{c}", S, BF16) for c in range(NCH)]
    xm32_dram = b.dram("xm32B", [D, S], F32)
    rms_norm(b, "B2", h_dram, ln12[:, 16:32], ("ln2w",), xm, "xmB",
             f32_dram=xm32_dram)

    rweights = b.sb("rwB", NCH * 8, F32)
    for k in range(NCH):
        p.op("sync", lambda e, k=k: e.dma_start(
            out=rweights[:, k * 8:k * 8 + 7], in_=W["routerw"][k]),
            writes=[("rwB", k)], dma=True)
    xftile = [b.sb(f"xftB{i}", P, F32) for i in range(2)]
    rbias = b.sb("rbB", 8, F32)
    bcb = bass.AP(W["rb7"].tensor, 0, [[0, P], [1, 7]])
    p.op("gpsimd", lambda e: e.dma_start(out=rbias[:, 0:7], in_=bcb),
         writes=[("rbB",)], dma=True)
    lgt = b.sb("lgB", 8, F32)
    m1 = b.sb("m1B", 1, F32)
    m2 = b.sb("m2B", 1, F32)
    msk = b.sb("mskB", 8, F32)
    comb = b.sb("combB", 8, F32)
    combT = b.sb("combTB", S, F32)
    for tt in range(NTT):
        pp = b.ps(5)
        for k in range(NCH):
            xft = xftile[k % 2]
            p.op("sync", lambda e, tt=tt, k=k, xft=xft: e.dma_start(
                out=xft, in_=xm32_dram[k * P:(k + 1) * P, tt * P:(tt + 1) * P]),
                reads=[("xf32d", "B2", k, (tt * P) // 512)],
                writes=[("xftB", k % 2)], dma=True)
            p.op("tensor", lambda e, tt=tt, k=k, pp=pp, xft=xft: e.matmul(
                pp[:, 0:7], xft, rweights[:, k * 8:k * 8 + 7],
                start=(k == 0), stop=(k == NCH - 1)),
                reads=[("xftB", k % 2), ("rwB", k)], writes=b.pk(5), tag="rlog")
        p.op("vector", lambda e, pp=pp: e.tensor_add(lgt[:, 0:7], pp[:, 0:7], rbias[:, 0:7]),
             reads=b.pk(5) + [("rbB",)], writes=[("lgB",)])
        p.op("scalar", lambda e: e.activation(lgt[:, 0:7], lgt[:, 0:7], AF.Sigmoid),
             reads=[("lgB",)], writes=[("lgB",)])
        p.op("vector", lambda e: e.reduce_max(m1, lgt[:, 0:7], axis=AX.X),
             reads=[("lgB",)], writes=[("m1B",)])
        p.op("vector", lambda e: e.tensor_single_scalar(msk[:, 0:7], lgt[:, 0:7], m1, ALU.is_equal),
             reads=[("lgB",), ("m1B",)], writes=[("mskB",)])
        p.op("vector", lambda e: e.scalar_tensor_tensor(
            msk[:, 0:7], msk[:, 0:7], -1e9, lgt[:, 0:7], ALU.mult, ALU.add),
            reads=[("mskB",), ("lgB",)], writes=[("mskB",)])
        p.op("vector", lambda e: e.reduce_max(m2, msk[:, 0:7], axis=AX.X),
             reads=[("mskB",)], writes=[("m2B",)])
        p.op("vector", lambda e: e.tensor_single_scalar(msk[:, 0:7], lgt[:, 0:7], m2, ALU.is_ge),
             reads=[("lgB",), ("m2B",)], writes=[("mskB",)])
        p.op("vector", lambda e: e.tensor_add(m1, m1, m2),
             reads=[("m1B",), ("m2B",)], writes=[("m1B",)])
        p.op("vector", lambda e: e.reciprocal(m1, m1),
             reads=[("m1B",)], writes=[("m1B",)])
        p.op("vector", lambda e: e.tensor_mul(comb[:, 0:7], lgt[:, 0:7], msk[:, 0:7]),
             reads=[("lgB",), ("mskB",)], writes=[("combB",)])
        p.op("vector", lambda e: e.tensor_scalar_mul(comb[:, 0:7], comb[:, 0:7], m1),
             reads=[("combB",), ("m1B",)], writes=[("combB",)])
        pt = b.ps(4)
        p.op("tensor", lambda e, pt=pt: e.matmul(
            pt[0:7, 0:P], comb[:, 0:7], identf, start=True, stop=True),
            reads=[("combB",), ("identfB",)], writes=b.pk(4), tag="rtr")
        p.op("vector", lambda e, tt=tt, pt=pt: e.tensor_copy(
            combT[0:7, tt * P:(tt + 1) * P], pt[0:7, 0:P]),
            reads=b.pk(4), writes=[("combTB",)])
    p.op("sync", lambda e: e.dma_start(out=comb_dram[0:7, :], in_=combT[0:7, :]),
         reads=[("combTB",)], writes=[("combBd",)], dma=True)

    NWB = 4
    wg2 = [b.sb(f"wgB{i}", NCH * P, BF16) for i in range(NWB)]
    wu2 = [b.sb(f"wuB{i}", NCH * P, BF16) for i in range(NWB)]
    wd2 = [b.sb(f"wdB{i}", 8 * P, BF16) for i in range(NWB)]
    hbf = [b.sb(f"hbfB{m}", S, BF16) for m in range(8)]
    gs2 = [b.sb(f"gsB{j}", 512, F32) for j in range(2)]
    ceb = b.sb("cebB", S, F32)
    ysb2 = [b.sb(f"ymoeB{j}", 512, F32) for j in range(2)]

    for e_i in range(8):
        shared = (e_i == 0)
        if not shared:
            bce = bass.AP(comb_dram.tensor, (e_i - 1) * S, [[0, P], [1, S]])
            p.op("gpsimd", lambda e, bce=bce: e.dma_start(out=ceb, in_=bce),
                 reads=[("combBd",)], writes=[("cebB",)], dma=True)
        for m in range(8):
            slg = wg2[m % NWB]
            gname = "shg" if shared else "rgate"
            uname = "shu" if shared else "rup"
            gidx = m if shared else (e_i - 1) * 8 + m
            p.op("sync", lambda e, slg=slg, gname=gname, gidx=gidx: e.dma_start(
                out=slg, in_=W[gname][gidx]),
                writes=[("wgB", m % NWB)], dma=True, tag="wgld")
            slu = wu2[m % NWB]
            p.op("scalar", lambda e, slu=slu, uname=uname, gidx=gidx: e.dma_start(
                out=slu, in_=W[uname][gidx]),
                writes=[("wuB", m % NWB)], dma=True, tag="wuld")
            for n in range(NTB):
                gs = gs2[n % 2]
                bank = n % 2
                pp = b.ps(bank)
                for k in range(NCH):
                    p.op("tensor", lambda e, k=k, n=n, slg=slg, pp=pp: e.matmul(
                        pp, slg[:, k * P:(k + 1) * P],
                        xm[k][:, n * 512:(n + 1) * 512],
                        start=(k == 0), stop=(k == NCH - 1)),
                        reads=[("wgB", m % NWB), ("xmB", k)], writes=b.pk(bank),
                        tag="gmm")
                p.op("scalar", lambda e, pp=pp, gs=gs: e.activation(gs, pp, AF.Silu),
                     reads=b.pk(bank), writes=[("gsB", n % 2)], tag="silu")
                bank2 = 2 + n % 2
                pp2 = b.ps(bank2)
                for k in range(NCH):
                    p.op("tensor", lambda e, k=k, n=n, slu=slu, pp2=pp2: e.matmul(
                        pp2, slu[:, k * P:(k + 1) * P],
                        xm[k][:, n * 512:(n + 1) * 512],
                        start=(k == 0), stop=(k == NCH - 1)),
                        reads=[("wuB", m % NWB), ("xmB", k)], writes=b.pk(bank2),
                        tag="umm")
                p.op("vector", lambda e, pp2=pp2, gs=gs: e.scalar_tensor_tensor(
                    gs, gs, 0.125, pp2, ALU.mult, ALU.mult),
                    reads=[("gsB", n % 2)] + b.pk(bank2), writes=[("gsB", n % 2)],
                    tag="hmul")
                if shared:
                    p.op("vector", lambda e, m=m, n=n, gs=gs: e.tensor_copy(
                        hbf[m][:, n * 512:(n + 1) * 512], gs),
                        reads=[("gsB", n % 2)], writes=[("hbfB", m)], tag="hcp")
                else:
                    p.op("vector", lambda e, m=m, n=n, gs=gs: e.tensor_mul(
                        hbf[m][:, n * 512:(n + 1) * 512], gs,
                        ceb[:, n * 512:(n + 1) * 512]),
                        reads=[("gsB", n % 2), ("cebB",)], writes=[("hbfB", m)],
                        tag="hce")
        for dm in range(NCH):
            sld = wd2[dm % NWB]
            didx = dm if shared else (e_i - 1) * NCH + dm
            dname = "shd" if shared else "rdown"
            q = ("sync", "scalar", "gpsimd")[dm % 3]
            p.op(q, lambda e, sld=sld, dname=dname, didx=didx: e.dma_start(
                out=sld, in_=W[dname][didx]),
                writes=[("wdB", dm % NWB)], dma=True, tag="wdld")
            for n in range(NTB):
                ysb = ysb2[n % 2]
                bank = 4 + (dm * NTB + n) % 4
                pp = b.ps(bank)
                for k in range(8):
                    p.op("tensor", lambda e, k=k, n=n, sld=sld, pp=pp: e.matmul(
                        pp, sld[:, k * P:(k + 1) * P],
                        hbf[k][:, n * 512:(n + 1) * 512],
                        start=(k == 0), stop=(k == 7)),
                        reads=[("wdB", dm % NWB), ("hbfB", k)], writes=b.pk(bank),
                        tag="dmm")
                p.op("vector", lambda e, pp=pp, ysb=ysb: e.tensor_copy(ysb, pp),
                     reads=b.pk(bank), writes=[("ymoeB", n % 2)], tag="dev")
                if shared:
                    p.op("gpsimd", lambda e, dm=dm, n=n, ysb=ysb: e.dma_start(
                        out=moe_dram[dm * P:(dm + 1) * P, n * 512:(n + 1) * 512],
                        in_=ysb),
                        reads=[("ymoeB", n % 2)], writes=[("moeBd", dm, n)], dma=True)
                else:
                    p.op("gpsimd", lambda e, dm=dm, n=n, ysb=ysb: e.dma_start(
                        out=moe_dram[dm * P:(dm + 1) * P, n * 512:(n + 1) * 512],
                        in_=ysb, accum_op=ALU.add),
                        reads=[("ymoeB", n % 2)], writes=[("moeBd", dm, n)], dma=True)
    p.barrier()
    b.release()

    b.mark()
    t1 = [b.sb(f"lnt1B{j}", 512, F32) for j in range(2)]
    t2 = [b.sb(f"lnt2B{j}", 512, F32) for j in range(2)]
    t3 = [b.sb(f"lnt3B{j}", 512, F32) for j in range(2)]

    def gen_tile(n, m, ysb2):
        j = m % 2
        p.op("sync", lambda e, m=m, n=n, j=j: e.dma_start(
            out=t1[j], in_=x1_dram[m * P:(m + 1) * P, n * 512:(n + 1) * 512]),
            writes=[("lt1B", j)], dma=True)
        p.op("sync", lambda e, m=m, n=n, j=j: e.dma_start(
            out=t2[j], in_=h_dram[m * P:(m + 1) * P, n * 512:(n + 1) * 512]),
            writes=[("lt2B", j)], dma=True)
        p.op("sync", lambda e, m=m, n=n, j=j: e.dma_start(
            out=t3[j], in_=moe_dram[m * P:(m + 1) * P, n * 512:(n + 1) * 512]),
            writes=[("lt3B", j)], dma=True)
        p.op("vector", lambda e, ysb2=ysb2, j=j, m=m: e.tensor_add(ysb2, t1[j], t2[j]),
             reads=[("lt1B", j), ("lt2B", j)], writes=[("lny", "B", m)])
        p.op("vector", lambda e, m=m, ysb2=ysb2, j=j: e.tensor_add(ysb2, ysb2, t3[j]),
             reads=[("lny", "B", m), ("lt3B", j)], writes=[("lny", "B", m)])

    ln_epilogue(b, "B", gen_tile, x2_dram)
    p.barrier()
    b.release()
    b.release()


def stage_c(b, W, x2_dram, x3_dram):
    p = b.p
    NC2 = 32

    b.mark()
    identb = b.sb("identC", P, BF16)
    p.op("sync", lambda e: e.dma_start(out=identb, in_=W["ident"]),
         writes=[("identC",)], dma=True)
    ident_t = identb.tensor
    ident_stride = identb.ap[0][0]
    ident_off0 = identb.offset

    selu = b.sb("seluC", 8 * P, BF16)
    p.op("sync", lambda e: e.dma_start(
        out=selu.rearrange("p (a c) -> p a c", a=8), in_=W["selu8"]),
        writes=[("seluC",)], dma=True)
    selc = b.sb("selcC", 16 * P, BF16)
    p.op("sync", lambda e: e.dma_start(
        out=selc.rearrange("p (a c) -> p a c", a=16), in_=W["selc16"]),
        writes=[("selcC",)], dma=True)
    dtpb = b.sb("dtpbC", NC2, F32)
    p.op("sync", lambda e: e.dma_start(out=dtpb, in_=W["dtpb"]),
         writes=[("dtpbC",)], dma=True)
    convw = b.sb("convwC", NC2 * 4, F32)
    p.op("sync", lambda e: e.dma_start(out=convw, in_=W["convw"]),
         writes=[("convwC",)], dma=True)
    convb = b.sb("convbC", NC2, F32)
    p.op("sync", lambda e: e.dma_start(out=convb, in_=W["convb"]),
         writes=[("convbC",)], dma=True)
    alog = b.sb("alogC", NC2 * 16, F32)
    p.op("sync", lambda e: e.dma_start(out=alog, in_=W["alog"]),
         writes=[("alogC",)], dma=True)
    dvec = b.sb("dvecC", NC2, F32)
    p.op("sync", lambda e: e.dma_start(out=dvec, in_=W["dvec"]),
         writes=[("dvecC",)], dma=True)

    xss = [b.sb(f"xssC{c}", S, BF16) for c in range(4)]

    C_dram = b.dram("CmC", [4096, S], BF16)
    B_dram = b.dram("BmC", [512, S], BF16)
    zc_dram = b.dram("zcC", [4096, S], BF16)
    delta_dram = b.dram("deltaC", [4096, S], F32)
    outpre_dram = b.dram("outpreC", [4096, S], BF16)

    b.mark()
    xc = [b.sb(f"xcC{c}", S, BF16) for c in range(NCH)]
    for c in range(NCH):
        p.op("gpsimd", lambda e, c=c: e.dma_start(
            out=xc[c], in_=x2_dram[c * P:(c + 1) * P, :]),
            writes=[("xcC", c)], dma=True)
    wsh = [b.sb(f"WshC{i}", NCH * P, BF16) for i in range(6)]
    stg = b.sb("stgC", S, BF16)
    praw = b.sb("prawC", S, F32)
    spt = b.sb("sptC", S, F32)

    def inproj_evac(m, n, pp, pk):
        if 32 <= m < 36:
            dst, key = xss[m - 32], ("xssC", m - 32)
        else:
            dst, key = stg, ("stgC", n)
        p.op("scalar", lambda e, n=n, pp=pp, dst=dst: e.copy(
            dst[:, n * 512:(n + 1) * 512], pp), reads=pk, writes=[key],
            tag="ipev")
        if not (32 <= m < 36) and n == NTB - 1:
            if m < 32:
                tgt, wkey2 = zc_dram[m * P:(m + 1) * P, :], ("zcCd", m)
            elif 36 <= m < 40:
                tgt, wkey2 = B_dram[(m - 36) * P:(m - 35) * P, :], ("BmCd", m - 36)
            else:
                tgt, wkey2 = C_dram[(m - 40) * P:(m - 39) * P, :], ("CmCd", m - 40)
            p.op("sync", lambda e, tgt=tgt: e.dma_start(out=tgt, in_=stg),
                reads=[("stgC", nn) for nn in range(4)],
                writes=[wkey2], dma=True)

    fm_proj(b, "ipC", W["inprojT"], NCH, 72, xc, "xcC", inproj_evac,
            wbuf=wsh, wkey="WshC", queues=("gpsimd", "scalar", "sync"))

    dt1 = b.sb("dt1C", S, BF16)

    def dt1_evac(m, n, pp, pk):
        p.op("scalar", lambda e, n=n, pp=pp: e.copy(
            dt1[:, n * 512:(n + 1) * 512], pp), reads=pk, writes=[("dt1C", 0)],
            tag="dt1ev")

    fm_proj(b, "dt1C", W["dtinT"], NCH, 1, xc, "xcC", dt1_evac,
            wbuf=wsh, wkey="WshC", queues=("gpsimd", "scalar", "sync"))

    def dt2_evac(m, n, pp, pk):
        p.op("scalar", lambda e, n=n, pp=pp: e.copy(
            praw[:, n * 512:(n + 1) * 512], pp), reads=pk, writes=[("prawC", n)],
            tag="dt2ev")
        if n == NTB - 1:
            bap = dtpb[:, m:m + 1]
            rk = [("prawC", nn) for nn in range(4)] + [("dtpbC",)]
            p.op("scalar", lambda e, bap=bap: e.activation(spt, praw, AF.Abs, bias=bap),
                 reads=rk, writes=[("sptC",)])
            p.op("scalar", lambda e: e.activation(spt, spt, AF.Exp, scale=-1.0),
                 reads=[("sptC",)], writes=[("sptC",)])
            p.op("scalar", lambda e: e.activation(spt, spt, AF.Ln, bias=1.0),
                 reads=[("sptC",)], writes=[("sptC",)])
            p.op("scalar", lambda e, bap=bap: e.activation(praw, praw, AF.Relu, bias=bap),
                 reads=rk, writes=[("prawC", 0), ("prawC", 1), ("prawC", 2), ("prawC", 3)])
            p.op("vector", lambda e: e.tensor_add(praw, praw, spt),
                 reads=[("prawC", 0), ("sptC",)],
                 writes=[("prawC", 0), ("prawC", 1), ("prawC", 2), ("prawC", 3)])
            p.op("sync", lambda e, m=m: e.dma_start(
                out=delta_dram[m * P:(m + 1) * P, :], in_=praw),
                reads=[("prawC", 0)], writes=[("deltaCd", m)], dma=True)

    fm_proj(b, "dt2C", W["dtpT"], 1, NC2, [dt1], "dt1C", dt2_evac,
            wbuf=wsh, wkey="WshC", queues=("gpsimd", "scalar", "sync"))

    p.barrier()
    b.release()

    b.mark()
    dsl = [b.sb(f"delC{j}", S, F32) for j in range(2)]
    xcv = [b.sb(f"xcvC{j}", 3 + S + 1, F32) for j in range(2)]
    tconv = [b.sb(f"tconvC{j}", S, F32) for j in range(2)]
    ui = [b.sb(f"uiC{j}", S, F32) for j in range(2)]
    acol = [b.sb(f"acolC{j}", 16, F32) for j in range(2)]
    NJ = 3
    dan = [b.sb(f"danC{j}", S, BF16) for j in range(NJ)]
    xnn = [b.sb(f"xnnC{j}", S, BF16) for j in range(NJ)]
    hnn = [b.sb(f"hnnC{j}", S, BF16) for j in range(NJ)]
    NB4 = 4
    bsb = [b.sb(f"bsbC{j}", S, BF16) for j in range(NB4)]
    csb = [b.sb(f"csbC{j}", S, BF16) for j in range(NB4)]
    zdub = [b.sb(f"zdubC{j}", S, BF16) for j in range(2)]
    yi = [b.sb(f"yiC{j}", S, F32) for j in range(2)]
    zg = [b.sb(f"zgC{j}", S, BF16) for j in range(2)]
    cci = [b.sb(f"cciC{i}", S, BF16) for i in range(2)]
    og = b.sb("ogC", S, BF16)

    for j in range(2):
        p.op("vector", lambda e, j=j: e.memset(xcv[j][:, 0:3], 0.0),
             writes=[("xcvz", j)])

    def bc_loads(i, n):
        j4 = n % NB4
        bsrc = bass.AP(B_dram.tensor, (16 * i + n) * S, [[0, P], [1, S]])
        p.op("sync", lambda e, d=bsb[j4], bsrc=bsrc: e.dma_start(out=d, in_=bsrc),
             reads=[("BmCd", (16 * i + n) // P)], writes=[("bsbC", j4)],
             dma=True, tag="bbc")

    bc_loads(0, 0)
    bc_loads(0, 1)

    for i in range(NC2):
        j2 = i % 2
        p.op("sync", lambda e, i=i, d=dsl[j2]: e.dma_start(
            out=d, in_=delta_dram[i * P:(i + 1) * P, :]),
            reads=[("deltaCd", i)], writes=[("delC", j2)], dma=True)
        p.op("gpsimd", lambda e, i=i, d=zg[j2]: e.dma_start(
            out=d, in_=zc_dram[i * P:(i + 1) * P, :]),
            reads=[("zcCd", i)], writes=[("zgC", j2)], dma=True)
        p.op("scalar", lambda e, i=i, d=cci[i % 2]: e.dma_start(
            out=d, in_=C_dram[i * P:(i + 1) * P, :]),
            reads=[("CmCd", i)], writes=[("cciC", i % 2)], dma=True)
        sel_u = selu[:, (i % 8) * P:(i % 8 + 1) * P]
        xs_c = i // 8
        for nn in range(NTB):
            bank = nn % 4
            pp = b.ps(bank)
            p.op("tensor", lambda e, nn=nn, pp=pp, sel_u=sel_u, xs_c=xs_c:
                 e.matmul(pp, sel_u, xss[xs_c][:, nn * 512:(nn + 1) * 512],
                          start=True, stop=True),
                 reads=[("seluC",), ("xssC", xs_c)], writes=b.pk(bank),
                 tag="selu")
            p.op("scalar", lambda e, nn=nn, pp=pp, j2=j2: e.copy(
                xcv[j2][:, 3 + nn * 512:3 + (nn + 1) * 512], pp),
                reads=b.pk(bank), writes=[("xcvC", j2, nn)], tag="seluev")
        xk = [("xcvC", j2, nn) for nn in range(4)] + [("xcvz", j2), ("convwC",)]
        p.op("vector", lambda e, i=i, j2=j2: e.tensor_scalar_mul(
            tconv[j2], xcv[j2][:, 0:S], convw[:, 4 * i:4 * i + 1]),
            reads=xk, writes=[("tconvC", j2)], tag="conv0")
        for tau in range(1, 4):
            p.op("vector", lambda e, i=i, tau=tau, j2=j2: e.scalar_tensor_tensor(
                tconv[j2], xcv[j2][:, tau:tau + S],
                convw[:, 4 * i + tau:4 * i + tau + 1],
                tconv[j2], ALU.mult, ALU.add),
                reads=xk + [("tconvC", j2)], writes=[("tconvC", j2)], tag="conv")
        p.op("scalar", lambda e, i=i, j2=j2: e.activation(
            ui[j2], tconv[j2], AF.Silu, bias=convb[:, i:i + 1]),
            reads=[("tconvC", j2), ("convbC",)], writes=[("uiC", j2)], tag="silu_u")
        p.op("gpsimd", lambda e, j2=j2: e.tensor_mul(zdub[j2], dsl[j2], ui[j2]),
             reads=[("delC", j2), ("uiC", j2)], writes=[("zdubC", j2)], tag="zdu")
        p.op("scalar", lambda e, i=i, j2=j2: e.activation(
            acol[j2], alog[:, 16 * i:16 * i + 16], AF.Exp),
            reads=[("alogC",)], writes=[("acolC", j2)])
        p.op("vector", lambda e, j2=j2: e.tensor_scalar_mul(acol[j2], acol[j2], -1.0),
             reads=[("acolC", j2)], writes=[("acolC", j2)])
        p.op("scalar", lambda e, i=i, j2=j2: e.activation(
            yi[j2], ui[j2], AF.Copy, scale=dvec[:, i:i + 1]),
            reads=[("uiC", j2), ("dvecC",)], writes=[("yiC", j2)], tag="yinit")
        for n in range(16):
            j = n % NJ
            j4 = n % NB4
            if n < 14:
                bc_loads(i, n + 2)
            elif i < NC2 - 1:
                bc_loads(i + 1, n - 14)
            sel_c = selc[:, n * P:(n + 1) * P]
            for nn in range(NTB):
                pp = b.ps(4 + nn)
                p.op("tensor", lambda e, nn=nn, pp=pp, sel_c=sel_c,
                     cap=cci[i % 2]:
                     e.matmul(pp, sel_c, cap[:, nn * 512:(nn + 1) * 512],
                              start=True, stop=True),
                     reads=[("selcC",), ("cciC", i % 2)], writes=b.pk(4 + nn),
                     tag="selC")
            p.op("scalar", lambda e, j4=j4: e.copy(csb[j4], b.ps(4, 4)),
                 reads=b.pk(4, 4), writes=[("csbC", j4)], tag="cev")
            p.op("scalar", lambda e, n=n, j=j, j2=j2: e.activation(
                dan[j], dsl[j2], AF.Exp, scale=acol[j2][:, n:n + 1]),
                reads=[("delC", j2), ("acolC", j2)], writes=[("danC", j)],
                tag="dA")
            p.op("vector", lambda e, j=j, j4=j4, j2=j2: e.tensor_mul(
                xnn[j], zdub[j2], bsb[j4]),
                reads=[("zdubC", j2), ("bsbC", j4)], writes=[("xnnC", j)], tag="xB")
            p.op("vector", lambda e, j=j: e.tensor_tensor_scan(
                hnn[j], dan[j], xnn[j], 0.0, ALU.mult, ALU.add),
                reads=[("danC", j), ("xnnC", j)], writes=[("hnnC", j)],
                tag="scan")
            heng = "vector" if n % 2 == 0 else "gpsimd"
            p.op(heng, lambda e, j=j, j4=j4: e.tensor_mul(hnn[j], hnn[j], csb[j4]),
                 reads=[("hnnC", j), ("csbC", j4)], writes=[("hnnC", j)], tag="hC")
            p.op("gpsimd", lambda e, j=j, j2=j2: e.tensor_add(yi[j2], yi[j2], hnn[j]),
                 reads=[("yiC", j2), ("hnnC", j)], writes=[("yiC", j2)], tag="yadd")
        p.op("scalar", lambda e, j2=j2: e.activation(tconv[j2], zg[j2], AF.Silu),
             reads=[("zgC", j2)], writes=[("tconvC", j2)], tag="sz")
        p.op("vector", lambda e, j2=j2: e.tensor_mul(og, yi[j2], tconv[j2]),
             reads=[("yiC", j2), ("tconvC", j2)], writes=[("ogC",)], tag="og")
        p.op("sync", lambda e, i=i: e.dma_start(
            out=outpre_dram[i * P:(i + 1) * P, :], in_=og),
            reads=[("ogC",)], writes=[("opCd", i)], dma=True)

    p.barrier()
    b.release()
    b.release()

    b.mark()
    wbuf3 = [b.sb(f"opwC{i}", NC2 * P, BF16) for i in range(3)]
    actb = [b.sb(f"opactC{k}", 512, BF16) for k in range(NC2)]
    xres = [b.sb(f"opxC{j}", 512, F32) for j in range(2)]

    def gen_tile(n, m, ysb):
        sl = wbuf3[m % 3]
        q = ("gpsimd", "sync", "scalar")[m % 3]
        p.op(q, lambda e, m=m, sl=sl: e.dma_start(out=sl, in_=W["outpT"][m]),
             writes=[("opwC", m % 3)], dma=True, tag="opwld")
        if m == 0:
            for k in range(NC2):
                p.op("sync", lambda e, k=k, n=n: e.dma_start(
                    out=actb[k], in_=outpre_dram[k * P:(k + 1) * P, n * 512:(n + 1) * 512]),
                    writes=[("opactC", k)], dma=True)
        xr = xres[m % 2]
        p.op("sync", lambda e, m=m, n=n, xr=xr: e.dma_start(
            out=xr, in_=x2_dram[m * P:(m + 1) * P, n * 512:(n + 1) * 512]),
            writes=[("opxC", m % 2)], dma=True)
        bank = m % 4
        pp = b.ps(bank)
        for k in range(NC2):
            p.op("tensor", lambda e, k=k, sl=sl, pp=pp: e.matmul(
                pp, sl[:, k * P:(k + 1) * P], actb[k],
                start=(k == 0), stop=(k == NC2 - 1)),
                reads=[("opwC", m % 3), ("opactC", k)], writes=b.pk(bank),
                tag="opmm")
        p.op("vector", lambda e, pp=pp, ysb=ysb, m=m, xr=xr: e.tensor_add(ysb, pp, xr),
             reads=b.pk(bank) + [("opxC", m % 2)], writes=[("lny", "C", m)],
             tag="opadd")

    ln_epilogue(b, "C", gen_tile, x3_dram)
    p.barrier()
    b.release()


def stage_d(b, W, x3_dram, x4_dram):
    p = b.p

    b.mark()
    identf = b.sb("identfD", P, F32)
    p.op("sync", lambda e: e.dma_start(out=identf, in_=W["identf"]),
         writes=[("identfD",)], dma=True)
    moe_dram = b.dram("moeD", [D, S], F32)
    comb_dram = b.dram("combD", [8, S], F32)

    b.mark()
    rw = b.sb("rwD", NCH * 8, F32)
    for k in range(NCH):
        p.op("sync", lambda e, k=k: e.dma_start(
            out=rw[:, k * 8:(k + 1) * 8], in_=W["WrD"][k]),
            writes=[("rwD", k)], dma=True)
    xft = [b.sb(f"xftD{i}", P, F32) for i in range(2)]
    lgt = b.sb("lgD", 8, F32)
    m1n = b.sb("m1D", 1, F32)
    m1p = b.sb("m1pD", 1, F32)
    m2 = b.sb("m2D", 1, F32)
    den = b.sb("denD", 1, F32)
    msk = b.sb("mskD", 8, F32)
    comb = b.sb("combD", 8, F32)
    combT = b.sb("combTD", S, F32)
    for tt in range(NTT):
        pp = b.ps(5)
        for k in range(NCH):
            xf = xft[k % 2]
            p.op("sync", lambda e, tt=tt, k=k, xf=xf: e.dma_start(
                out=xf, in_=x3_dram[k * P:(k + 1) * P, tt * P:(tt + 1) * P]),
                writes=[("xftD", k % 2)], dma=True)
            p.op("tensor", lambda e, k=k, pp=pp, xf=xf: e.matmul(
                pp[:, 0:8], xf, rw[:, k * 8:(k + 1) * 8],
                start=(k == 0), stop=(k == NCH - 1)),
                reads=[("xftD", k % 2), ("rwD", k)], writes=b.pk(5), tag="rlogD")
        p.op("vector", lambda e, pp=pp: e.tensor_copy(lgt, pp[:, 0:8]),
             reads=b.pk(5), writes=[("lgD",)])
        p.op("vector", lambda e: e.reduce_max(m1n, lgt, axis=AX.X, negate=True),
             reads=[("lgD",)], writes=[("m1D",)])
        p.op("vector", lambda e: e.tensor_scalar_mul(m1p, m1n, -1.0),
             reads=[("m1D",)], writes=[("m1pD",)])
        p.op("vector", lambda e: e.tensor_single_scalar(msk, lgt, m1p, ALU.is_ge),
             reads=[("lgD",), ("m1pD",)], writes=[("mskD",)])
        p.op("vector", lambda e: e.scalar_tensor_tensor(
            msk, msk, -1e9, lgt, ALU.mult, ALU.add),
            reads=[("mskD",), ("lgD",)], writes=[("mskD",)])
        p.op("vector", lambda e: e.reduce_max(m2, msk, axis=AX.X),
             reads=[("mskD",)], writes=[("m2D",)])
        p.op("vector", lambda e: e.tensor_single_scalar(msk, lgt, m2, ALU.is_ge),
             reads=[("lgD",), ("m2D",)], writes=[("mskD",)])
        p.op("scalar", lambda e: e.activation(lgt, lgt, AF.Exp, bias=m1n),
             reads=[("lgD",), ("m1D",)], writes=[("lgD",)])
        p.op("scalar", lambda e: e.activation(den, m2, AF.Exp, bias=m1n),
             reads=[("m2D",), ("m1D",)], writes=[("denD",)])
        p.op("vector", lambda e: e.tensor_scalar_add(den, den, 1.0),
             reads=[("denD",)], writes=[("denD",)])
        p.op("vector", lambda e: e.reciprocal(den, den),
             reads=[("denD",)], writes=[("denD",)])
        p.op("vector", lambda e: e.tensor_mul(comb, lgt, msk),
             reads=[("lgD",), ("mskD",)], writes=[("combD",)])
        p.op("vector", lambda e: e.tensor_scalar_mul(comb, comb, den),
             reads=[("combD",), ("denD",)], writes=[("combD",)])
        pt = b.ps(4)
        p.op("tensor", lambda e, pt=pt: e.matmul(
            pt[0:8, 0:P], comb, identf, start=True, stop=True),
            reads=[("combD",), ("identfD",)], writes=b.pk(4), tag="rtrD")
        p.op("vector", lambda e, tt=tt, pt=pt: e.tensor_copy(
            combT[0:8, tt * P:(tt + 1) * P], pt[0:8, 0:P]),
            reads=b.pk(4), writes=[("combTD",)])
    p.op("sync", lambda e: e.dma_start(out=comb_dram, in_=combT[0:8, :]),
         reads=[("combTD",)], writes=[("combDd",)], dma=True)
    p.barrier()
    b.release()

    b.mark()
    x3b = [b.sb(f"x3bD{c}", S, BF16) for c in range(NCH)]
    for c in range(NCH):
        p.op("gpsimd", lambda e, c=c: e.dma_start(
            out=x3b[c], in_=x3_dram[c * P:(c + 1) * P, :]),
            writes=[("x3bD", c)], dma=True)
    NI = 11
    NWB = 4
    wg2 = [b.sb(f"wgD{i}", NCH * P, BF16) for i in range(NWB)]
    wu2 = [b.sb(f"wuD{i}", NCH * P, BF16) for i in range(NWB)]
    wd2 = [b.sb(f"wdD{i}", NI * P, BF16) for i in range(NWB)]
    hbf = [b.sb(f"hbfD{m}", S, BF16) for m in range(NI)]
    gs2 = [b.sb(f"gsD{j}", 512, F32) for j in range(2)]
    ceb = b.sb("cebD", S, F32)
    ysb2 = [b.sb(f"ymoeD{j}", 512, F32) for j in range(2)]

    for e_i in range(8):
        bce = bass.AP(comb_dram.tensor, e_i * S, [[0, P], [1, S]])
        p.op("gpsimd", lambda e, bce=bce: e.dma_start(out=ceb, in_=bce),
             reads=[("combDd",)], writes=[("cebD",)], dma=True)
        for m in range(NI):
            slg = wg2[m % NWB]
            slu = wu2[m % NWB]
            p.op("sync", lambda e, slg=slg, gi=e_i * NI + m: e.dma_start(
                out=slg, in_=W["WgD"][gi]),
                writes=[("wgD", m % NWB)], dma=True, tag="wgldD")
            p.op("scalar", lambda e, slu=slu, gi=e_i * NI + m: e.dma_start(
                out=slu, in_=W["WuD"][gi]),
                writes=[("wuD", m % NWB)], dma=True, tag="wuldD")
            for n in range(NTB):
                gs = gs2[n % 2]
                bank = n % 2
                pp = b.ps(bank)
                for k in range(NCH):
                    p.op("tensor", lambda e, k=k, n=n, slg=slg, pp=pp: e.matmul(
                        pp, slg[:, k * P:(k + 1) * P],
                        x3b[k][:, n * 512:(n + 1) * 512],
                        start=(k == 0), stop=(k == NCH - 1)),
                        reads=[("wgD", m % NWB), ("x3bD", k)], writes=b.pk(bank),
                        tag="gmmD")
                p.op("scalar", lambda e, pp=pp, gs=gs: e.activation(gs, pp, AF.Silu),
                     reads=b.pk(bank), writes=[("gsD", n % 2)], tag="siluD")
                bank2 = 2 + n % 2
                pp2 = b.ps(bank2)
                for k in range(NCH):
                    p.op("tensor", lambda e, k=k, n=n, slu=slu, pp2=pp2: e.matmul(
                        pp2, slu[:, k * P:(k + 1) * P],
                        x3b[k][:, n * 512:(n + 1) * 512],
                        start=(k == 0), stop=(k == NCH - 1)),
                        reads=[("wuD", m % NWB), ("x3bD", k)],
                        writes=b.pk(bank2), tag="ummD")
                p.op("vector", lambda e, pp2=pp2, gs=gs: e.tensor_mul(gs, gs, pp2),
                     reads=[("gsD", n % 2)] + b.pk(bank2), writes=[("gsD", n % 2)],
                     tag="hmulD")
                p.op("vector", lambda e, m=m, n=n, gs=gs: e.tensor_mul(
                    hbf[m][:, n * 512:(n + 1) * 512], gs,
                    ceb[:, n * 512:(n + 1) * 512]),
                    reads=[("gsD", n % 2), ("cebD",)], writes=[("hbfD", m)],
                    tag="hceD")
        for dm in range(NCH):
            sld = wd2[dm % NWB]
            q = ("sync", "scalar", "gpsimd")[dm % 3]
            p.op(q, lambda e, sld=sld, di=e_i * NCH + dm: e.dma_start(
                out=sld, in_=W["WdD"][di]),
                writes=[("wdD", dm % NWB)], dma=True, tag="wdldD")
            for n in range(NTB):
                ysb = ysb2[n % 2]
                bank = 4 + (dm * NTB + n) % 4
                pp = b.ps(bank)
                for k in range(NI):
                    p.op("tensor", lambda e, k=k, n=n, sld=sld, pp=pp: e.matmul(
                        pp, sld[:, k * P:(k + 1) * P],
                        hbf[k][:, n * 512:(n + 1) * 512],
                        start=(k == 0), stop=(k == NI - 1)),
                        reads=[("wdD", dm % NWB), ("hbfD", k)], writes=b.pk(bank),
                        tag="dmmD")
                p.op("vector", lambda e, pp=pp, ysb=ysb: e.tensor_copy(ysb, pp),
                     reads=b.pk(bank), writes=[("ymoeD", n % 2)], tag="devD")
                if e_i == 0:
                    p.op("gpsimd", lambda e, dm=dm, n=n, ysb=ysb: e.dma_start(
                        out=moe_dram[dm * P:(dm + 1) * P, n * 512:(n + 1) * 512],
                        in_=ysb),
                        reads=[("ymoeD", n % 2)], writes=[("moeDd", dm, n)], dma=True)
                else:
                    p.op("gpsimd", lambda e, dm=dm, n=n, ysb=ysb: e.dma_start(
                        out=moe_dram[dm * P:(dm + 1) * P, n * 512:(n + 1) * 512],
                        in_=ysb, accum_op=ALU.add),
                        reads=[("ymoeD", n % 2)], writes=[("moeDd", dm, n)], dma=True)
    p.barrier()
    b.release()

    b.mark()
    t1 = [b.sb(f"lnt1D{j}", 512, F32) for j in range(2)]
    t2 = [b.sb(f"lnt2D{j}", 512, F32) for j in range(2)]

    def gen_tile(n, m, ysb2):
        j = m % 2
        p.op("sync", lambda e, m=m, n=n, j=j: e.dma_start(
            out=t1[j], in_=x3_dram[m * P:(m + 1) * P, n * 512:(n + 1) * 512]),
            writes=[("lt1D", j)], dma=True)
        p.op("sync", lambda e, m=m, n=n, j=j: e.dma_start(
            out=t2[j], in_=moe_dram[m * P:(m + 1) * P, n * 512:(n + 1) * 512]),
            writes=[("lt2D", j)], dma=True)
        p.op("vector", lambda e, ysb2=ysb2, m=m, j=j: e.tensor_add(ysb2, t1[j], t2[j]),
             reads=[("lt1D", j), ("lt2D", j)], writes=[("lny", "D", m)])

    ln_epilogue(b, "D", gen_tile, x4_dram)
    p.barrier()
    b.release()
    b.release()


def host_pack(inputs):
    W = {}
    W["ident"] = np.eye(P, dtype=ml_dtypes.bfloat16)
    W["mask"] = causal_mask_np()
    W["perm16"] = perm_swap16()
    W["perm32"] = perm_swap32()
    cq, sq, ck, sk = l0attn_tables()
    W["ta_cosq"], W["ta_sinq"], W["ta_cosk"], W["ta_sink"] = cq, sq, ck, sk
    perm = np.concatenate([np.arange(0, 16), np.arange(32, 48),
                           np.arange(16, 32), np.arange(48, 64)])
    qwT = inputs["L0_attn_q_w"].T.reshape(D, 32, 64)[:, :, perm].reshape(D, D)
    W["q_wT"] = pack_lhsT(qwT)
    W["kr_wT"] = pack_lhsT(inputs["L0_attn_kr_w"].T)
    W["down_wT"] = pack_lhsT(inputs["L0_attn_down_w"].T)
    W["upk_wT"] = pack_lhsT(inputs["L0_attn_upk_w"].T)
    W["upv_wT"] = pack_rhs_bf(inputs["L0_attn_upv_w"].T)
    W["o_wT"] = pack_lhsT(inputs["L0_attn_o_w"].T)
    W["identf"] = np.eye(P, dtype=np.float32)
    W["ln1w16"] = _f32(inputs["L0_ffn_ln1_w"].reshape(NCH, P).T)
    W["ln2w16"] = _f32(inputs["L0_ffn_ln2_w"].reshape(NCH, P).T)
    W["kvdw"] = pack_lhsT(inputs["L0_ffn_kv_d_w"])
    W["qdw"] = pack_lhsT(inputs["L0_ffn_q_d_w"])
    W["kuw"] = pack_lhsT(inputs["L0_ffn_k_u_w"])
    W["quw"] = pack_lhsT(inputs["L0_ffn_q_u_w"])
    W["ropekw"] = pack_lhsT(inputs["L0_ffn_rope_k_w"])
    W["ropeqw"] = pack_lhsT(inputs["L0_ffn_rope_q_w"])
    W["vuw"] = pack_rhs_bf(inputs["L0_ffn_v_u_w"])
    W["owB"] = pack_lhsT(inputs["L0_ffn_o_w"])
    cr, sr = l0ffn_tables()
    W["tb_cos"], W["tb_sin"] = cr, sr
    scq = 1.0 / np.sqrt(128.0)
    W["tb_cosq"], W["tb_sinq"] = _f32(cr * scq), _f32(sr * scq)
    W["routerw"] = pack_rhs(inputs["L0_ffn_router_w"])
    W["rb7"] = _f32(inputs["L0_ffn_router_b"].reshape(1, 7))
    W["shg"] = pack_lhsT(inputs["L0_ffn_sh_gate"])
    W["shu"] = pack_lhsT(inputs["L0_ffn_sh_up"])
    W["shd"] = pack_lhsT(inputs["L0_ffn_sh_down"])
    W["rgate"] = np.concatenate([pack_lhsT(inputs["L0_ffn_r_gate"][e])
                                 for e in range(7)], axis=0)
    W["rup"] = np.concatenate([pack_lhsT(inputs["L0_ffn_r_up"][e])
                               for e in range(7)], axis=0)
    W["rdown"] = np.concatenate([pack_lhsT(inputs["L0_ffn_r_down"][e])
                                 for e in range(7)], axis=0)
    NC2 = 32
    W["inprojT"] = pack_lhsT(inputs["L1_attn_in_proj_w"].T)
    W["dtinT"] = pack_lhsT(inputs["L1_attn_dt_in_proj_w"].T)
    W["dtpT"] = pack_lhsT(inputs["L1_attn_dt_proj_w"].T)
    W["dtpb"] = _f32(inputs["L1_attn_dt_proj_b"].reshape(NC2, P).T)
    W["convw"] = _f32(inputs["L1_attn_conv_w"].reshape(NC2, P, 4)
                      .transpose(1, 0, 2).reshape(P, NC2 * 4))
    W["convb"] = _f32(inputs["L1_attn_conv_b"].reshape(NC2, P).T)
    W["alog"] = _f32(inputs["L1_attn_A_log"].reshape(NC2, P, 16)
                     .transpose(1, 0, 2).reshape(P, NC2 * 16))
    W["dvec"] = _f32(inputs["L1_attn_D"].reshape(NC2, P).T)
    W["outpT"] = pack_lhsT(inputs["L1_attn_out_proj_w"].T)
    selu8 = np.zeros((8, P, P), np.float32)
    for k in range(8):
        for pp_ in range(P):
            selu8[k, 16 * k + (pp_ % 16), pp_] = 1.0
    W["selu8"] = np.ascontiguousarray(selu8.transpose(1, 0, 2)).astype(ml_dtypes.bfloat16)
    selc16 = np.zeros((16, P, P), np.float32)
    for nn_ in range(16):
        for pp_ in range(P):
            selc16[nn_, nn_ + 16 * (pp_ // 16), pp_] = 1.0
    W["selc16"] = np.ascontiguousarray(selc16.transpose(1, 0, 2)).astype(ml_dtypes.bfloat16)
    W["WrD"] = pack_rhs(inputs["L1_ffn_Wr"])
    W["WgD"] = np.concatenate([pack_lhsT(inputs["L1_ffn_Wg"][e])
                               for e in range(8)], axis=0)
    W["WuD"] = np.concatenate([pack_lhsT(inputs["L1_ffn_Wu"][e])
                               for e in range(8)], axis=0)
    W["WdD"] = np.concatenate([pack_lhsT(inputs["L1_ffn_Wd"][e])
                               for e in range(8)], axis=0)
    return W


def build_program(inputs, n_stages=4, taps=(), only=None):
    prog = Prog()
    nc = bass.Bass("TRN2", target_bir_lowering=False, debug=False)
    b = Builder(nc, prog)
    Wh = host_pack(inputs)
    W = {k: b.hin(k, v) for k, v in Wh.items()}
    x0 = b.dram("x0", [D, S], F32, kind="ExternalInput")
    out = b.dram("xout", [D, S], F32, kind="ExternalOutput")
    b.init_arena()
    if only == "b1":
        x1i = b.dram("x1i", [D, S], F32, kind="ExternalInput")
        stage_b(b, W, x1i, out, phase1_only=True)
    elif only == "c":
        x2i = b.dram("x2i", [D, S], F32, kind="ExternalInput")
        stage_c(b, W, x2i, out)
    elif only == "d":
        x3i = b.dram("x3i", [D, S], F32, kind="ExternalInput")
        stage_d(b, W, x3i, out)
    else:
        d1 = out if n_stages == 1 else b.dram("x1", [D, S], F32)
        stage_a(b, W, x0, d1)
        if n_stages >= 2:
            d2 = out if n_stages == 2 else b.dram("x2", [D, S], F32)
            stage_b(b, W, d1, d2)
        if n_stages >= 3:
            d3 = out if n_stages == 3 else b.dram("x3", [D, S], F32)
            stage_c(b, W, d2, d3)
        if n_stages >= 4:
            stage_d(b, W, d3, out)

    for t in taps:
        ap = b.dram_cache[t]
        sh = [ap.tensor.shape[i] for i in range(len(ap.tensor.shape))] \
            if hasattr(ap.tensor, "shape") else list(ap.shape)
        dt = ap.dtype
        tout = b.dram("tap_" + t, list(ap.shape), dt, kind="ExternalOutput")
        prog.op("sync", lambda e, tout=tout, ap=ap: e.dma_start(out=tout, in_=ap),
                reads=[], writes=[], dma=True)
    prog.emit(nc)
    return nc, b


def kernel(**inputs):
    nc, b = build_program(inputs, n_stages=4)
    x = inputs["L0_attn_hidden_states"]
    in_maps = []
    for core in range(8):
        m = dict(b.in_arrays)
        m["x0"] = np.ascontiguousarray(x[core].T.astype(np.float32))
        in_maps.append(m)
    res = run_bass_kernel_spmd(nc, in_maps, core_ids=list(range(8)))
    outs = [np.ascontiguousarray(r["xout"].T) for r in res.results]
    return np.stack(outs)


# version 12
# speedup vs baseline: 1.2208x; 1.2208x over previous
import numpy as np
import ml_dtypes
import concourse.bass as bass
import concourse.mybir as mybir
from concourse.bass_utils import run_bass_kernel_spmd

F32 = mybir.dt.float32
BF16 = mybir.dt.bfloat16
AF = mybir.ActivationFunctionType
ALU = mybir.AluOpType
AX = mybir.AxisListType

P = 128
S = 2048
D = 2048
NCH = D // P
NTT = S // P
NTB = S // 512

ENGINES = ("sync", "vector", "scalar", "tensor", "gpsimd")
DMA_CHAINS = {"sync": 6, "gpsimd": 5, "scalar": 4}


class Prog:
    def __init__(self):
        self.ops = []
        self.writer = {}
        self.readers = {}
        self.chain_rr = {q: 0 for q in DMA_CHAINS}
        self.chain_last = {}
        self.chain_count = {}
        self.barriers = []

    def op(self, eng, fn, reads=(), writes=(), dma=False, tag=""):
        assert eng in ENGINES, eng
        deps = set()
        for k in reads:
            w = self.writer.get(k)
            if w is not None:
                deps.add(w)
        for k in writes:
            w = self.writer.get(k)
            if w is not None:
                deps.add(w)
            for r in self.readers.get(k, ()):
                deps.add(r)
        idx = len(self.ops)
        chain = pos = None
        if dma:
            c = self.chain_rr[eng]
            self.chain_rr[eng] = (c + 1) % DMA_CHAINS[eng]
            chain = (eng, c)
            last = self.chain_last.get(chain)
            if last is not None:
                deps.add(last)
            self.chain_last[chain] = idx
            pos = self.chain_count.get(chain, 0)
            self.chain_count[chain] = pos + 1
        deps.discard(idx)
        for k in reads:
            self.readers.setdefault(k, []).append(idx)
        for k in writes:
            self.writer[k] = idx
            self.readers[k] = []
        self.ops.append(dict(eng=eng, fn=fn, deps=deps, dma=dma, chain=chain,
                             pos=pos, tag=tag, awaited=False, inc=None))
        return idx

    def barrier(self):
        self.barriers.append(len(self.ops))
        for k in self.readers:
            self.readers[k] = []

    def emit(self, nc):
        ops = self.ops
        for o in ops:
            for d in o["deps"]:
                pr = ops[d]
                if pr["dma"]:
                    continue
                if pr["eng"] == "tensor" and o["eng"] == "tensor":
                    continue
                pr["awaited"] = True
        marks = list(self.barriers) + [len(ops)]
        start = 0
        for m in marks:
            last_per = {}
            for i in range(start, m):
                if not ops[i]["dma"]:
                    last_per[ops[i]["eng"]] = i
            for i in last_per.values():
                ops[i]["awaited"] = True
            start = m
        cnt = {e: 0 for e in ENGINES}
        for o in ops:
            if not o["dma"] and o["awaited"]:
                cnt[o["eng"]] += 1
                o["inc"] = cnt[o["eng"]]
        snaps = []
        ccnt = {e: 0 for e in ENGINES}
        chcnt = {}
        bi = 0
        for i, o in enumerate(ops):
            while bi < len(self.barriers) and self.barriers[bi] == i:
                snaps.append((dict(ccnt), dict(chcnt)))
                bi += 1
            if o["dma"]:
                chcnt[o["chain"]] = chcnt.get(o["chain"], 0) + 1
            elif o["inc"] is not None:
                ccnt[o["eng"]] = o["inc"]
        while bi < len(self.barriers):
            snaps.append((dict(ccnt), dict(chcnt)))
            bi += 1
        final_snap = (dict(ccnt), dict(chcnt))

        sems = {e: nc.semaphore(f"sem_{e}").__enter__() for e in ENGINES}
        chain_sems = {}
        for q, n in DMA_CHAINS.items():
            for c in range(n):
                chain_sems[(q, c)] = nc.semaphore(f"dq_{q}_{c}").__enter__()

        barrier_pos = self.barriers

        def make_emitter(engname):
            def _emit(engobj):
                wm = {}

                def do_wait(sem, val):
                    sid = sem.num
                    if wm.get(sid, 0) >= val:
                        return
                    engobj.wait_ge(sem, val)
                    wm[sid] = val

                def do_snap(snap):
                    ecnt, chc = snap
                    for e2 in ENGINES:
                        if e2 != engname and ecnt[e2] > 0:
                            do_wait(sems[e2], ecnt[e2])
                    for ch, n2 in chc.items():
                        if n2 > 0:
                            do_wait(chain_sems[ch], 16 * n2)

                bl = 0
                for i, o in enumerate(ops):
                    while bl < len(barrier_pos) and barrier_pos[bl] <= i:
                        do_snap(snaps[bl])
                        bl += 1
                    if o["eng"] != engname:
                        continue
                    for d in sorted(o["deps"]):
                        pr = ops[d]
                        if pr["dma"]:
                            do_wait(chain_sems[pr["chain"]], 16 * (pr["pos"] + 1))
                        else:
                            if pr["eng"] == "tensor" and engname == "tensor":
                                continue
                            do_wait(sems[pr["eng"]], pr["inc"])
                    inst = o["fn"](engobj)
                    if o["dma"]:
                        inst.then_inc(chain_sems[o["chain"]], 16)
                    elif o["inc"] is not None:
                        inst.then_inc(sems[engname], 1)
                do_snap(final_snap)
            return _emit

        with nc.Block() as block:
            block.sync(make_emitter("sync"))
            block.vector(make_emitter("vector"))
            block.scalar(make_emitter("scalar"))
            block.tensor(make_emitter("tensor"))
            block.gpsimd(make_emitter("gpsimd"))


class Builder:
    def __init__(self, nc, prog):
        self.nc = nc
        self.p = prog
        self.in_arrays = {}
        self.sb_off = 0
        self.sb_map = {}
        self.marks = []
        self.dram_cache = {}

    def hin(self, name, arr):
        if name in self.dram_cache:
            return self.dram_cache[name]
        arr = np.ascontiguousarray(arr)
        self.in_arrays[name] = arr
        dt = F32 if arr.dtype == np.float32 else BF16
        ap = self.nc.dram_tensor(name, list(arr.shape), dt,
                                 kind="ExternalInput").ap()
        self.dram_cache[name] = ap
        return ap

    def dram(self, name, shape, dtype=F32, kind="Internal"):
        if name in self.dram_cache:
            return self.dram_cache[name]
        ap = self.nc.dram_tensor(name, list(shape), dtype, kind=kind).ap()
        self.dram_cache[name] = ap
        return ap

    def init_arena(self):
        rem = self.nc.sbuf_bytes_remaining
        n = rem // 4 - 8
        self.arena_n = n
        self.arena = self.nc.alloc_sbuf_tensor("arena", [P, n], F32)
        self.psum_t = self.nc.alloc_psum_tensor("psarena", [P, 4096], F32)

    def sb(self, name, free, dtype=F32):
        if name in self.sb_map:
            off, ne, dt = self.sb_map[name]
            assert dt == dtype
        else:
            ne = free if dtype == F32 else (free + 1) // 2
            off = self.sb_off
            assert off + ne <= self.arena_n, \
                f"SBUF overflow at {name}: {off}+{ne} > {self.arena_n}"
            self.sb_off = off + ne
            self.sb_map[name] = (off, ne, dtype)
        if dtype == BF16:
            return self.arena.ap().bitcast(BF16)[:, 2 * off:2 * off + free]
        return self.arena.ap()[:, off:off + ne]

    def mark(self):
        self.marks.append((self.sb_off, dict(self.sb_map)))

    def release(self):
        self.sb_off, self.sb_map = self.marks.pop()

    def constv(self, val):
        name = f"__const_{val}"
        if name in self.sb_map:
            off, ne, dt = self.sb_map[name]
            return self.arena.ap()[:, off:off + 1]
        ap = self.sb(name, 1, F32)
        self.p.op("vector", lambda e: e.memset(ap, val), writes=[(name,)])
        return ap

    def ps(self, bank, nbanks=1, dtype=F32):
        if dtype == BF16:
            return self.psum_t.ap().bitcast(BF16)[:, bank * 1024:(bank + nbanks) * 1024]
        return self.psum_t.ap()[:, bank * 512:(bank + nbanks) * 512]

    @staticmethod
    def pk(bank, nbanks=1):
        return [("ps", bank + i) for i in range(nbanks)]


def _f32(x):
    return np.ascontiguousarray(x, dtype=np.float32)


def _bf16(x):
    return np.ascontiguousarray(np.asarray(x, dtype=np.float32)
                                .astype(ml_dtypes.bfloat16))


def pack_lhsT(w):
    K, M = w.shape
    Kc, Mc = K // P, M // P
    t = w.reshape(Kc, P, Mc, P).transpose(2, 1, 0, 3).reshape(Mc, P, Kc * P)
    return _bf16(t)


def pack_rhs(w):
    K, N = w.shape
    return _f32(w.reshape(K // P, P, N))


def pack_rhs_bf(w):
    K, N = w.shape
    return _bf16(w.reshape(K // P, P, N))


def rope_fm_tables(block, scale):
    raise NotImplementedError


def l0attn_tables():
    t = np.arange(S, dtype=np.float64)
    inv = 1.0 / (10000.0 ** (np.arange(32, dtype=np.float64) / 32.0))
    ang = np.outer(inv, t)
    cos = np.cos(ang)
    sin = np.sin(ang)
    perm = np.concatenate([np.arange(0, 16), np.arange(32, 48),
                           np.arange(16, 32), np.arange(48, 64)])
    cosq = np.empty((P, S))
    sinq = np.empty((P, S))
    for p in range(P):
        d = perm[p % 64]
        f = d % 32
        cosq[p] = cos[f]
        sinq[p] = sin[f] * (-1.0 if d < 32 else 1.0)
    cosk = np.empty((P, S))
    sink = np.empty((P, S))
    for p in range(P):
        s_ = p % 32
        d = s_ if s_ < 16 else 32 + (s_ - 16)
        f = d % 32
        cosk[p] = cos[f]
        sink[p] = sin[f] * (-1.0 if d < 32 else 1.0)
    sc = 1.0 / np.sqrt(64.0)
    return _f32(cosq * sc), _f32(sinq * sc), _f32(cosk), _f32(sink)


def l0ffn_tables():
    t = np.arange(S, dtype=np.float64)
    inv = 1.0 / (10000.0 ** (np.arange(32, dtype=np.float64) / 32.0))
    ang = np.outer(inv, t)
    cos = np.cos(ang)
    sin = np.sin(ang)
    cosr = np.empty((P, S))
    sinr = np.empty((P, S))
    for p in range(P):
        j = p % 64
        f = j % 32
        cosr[p] = cos[f]
        sinr[p] = sin[f] * (-1.0 if j < 32 else 1.0)
    return _f32(cosr), _f32(sinr)


def perm_swap16():
    m = np.zeros((P, P), np.float32)
    for p in range(P):
        m[(p // 32) * 32 + ((p % 32) ^ 16), p] = 1.0
    return m


def perm_swap32():
    m = np.zeros((P, P), np.float32)
    for p in range(P):
        m[(p // 64) * 64 + ((p % 64) ^ 32), p] = 1.0
    return m


def causal_mask_np():
    m = np.zeros((P, P), np.float32)
    m[np.triu_indices(P, 1)] = -1e30
    return m


def fm_proj(b, wname, slabs_dram, Kc, Mc, act_aps, act_key, evac,
            queue="gpsimd", banks=(0, 1, 2, 3), ntb=NTB, tw=512, wbuf=None,
            wkey=None, queues=None):
    p = b.p
    if wbuf is None:
        wbuf = [b.sb(f"W{wname}{i}", Kc * P, BF16) for i in range(2)]
    else:
        wbuf = [w[:, 0:Kc * P] for w in wbuf]
    nb = len(wbuf)
    if queues is None:
        queues = (queue,)
    wk = wkey if wkey is not None else "W" + wname
    for m in range(Mc):
        sl = wbuf[m % nb]
        q = queues[m % len(queues)]
        p.op(q, lambda e, m=m, sl=sl: e.dma_start(out=sl, in_=slabs_dram[m]),
             writes=[(wk, m % nb)], dma=True, tag=f"{wname}ld")
        for n in range(ntb):
            bank = banks[(m * ntb + n) % len(banks)]
            pp = b.ps(bank)[:, 0:tw]
            for k in range(Kc):
                p.op("tensor",
                     lambda e, sl=sl, k=k, n=n, pp=pp, st=(k == 0), sp=(k == Kc - 1),
                     a=act_aps[k]: e.matmul(pp, sl[:, k * P:(k + 1) * P],
                                            a[:, n * tw:(n + 1) * tw],
                                            start=st, stop=sp),
                     reads=[(wk, m % nb), (act_key, k)],
                     writes=b.pk(bank), tag=f"{wname}mm")
            evac(m, n, pp, b.pk(bank))


def rope_fm(b, tag, raw_key, raw_ap, out_keys, out_ap, permm, costab, sintab,
            cos_key, sin_key, perm_key, banks=(4, 5, 6, 7)):
    p = b.p
    t1 = b.sb(f"rp1{tag}", 512, F32)
    t2 = b.sb(f"rp2{tag}", 512, F32)
    for n in range(NTB):
        bank = banks[n % len(banks)]
        pp = b.ps(bank)
        sl = slice(n * 512, (n + 1) * 512)
        p.op("tensor", lambda e, pp=pp, sl=sl: e.matmul(
            pp, permm, raw_ap[:, sl], start=True, stop=True),
            reads=[perm_key, (raw_key, n)], writes=b.pk(bank), tag=f"{tag}sw")
        p.op("vector", lambda e, sl=sl: e.tensor_mul(t1, raw_ap[:, sl], costab[:, sl]),
             reads=[(raw_key, n), cos_key], writes=[(f"rp1{tag}",)])
        p.op("vector", lambda e, pp=pp, sl=sl: e.tensor_mul(t2, pp, sintab[:, sl]),
             reads=b.pk(bank) + [sin_key], writes=[(f"rp2{tag}",)])
        p.op("gpsimd", lambda e, sl=sl: e.tensor_add(out_ap[:, sl], t1, t2),
             reads=[(f"rp1{tag}",), (f"rp2{tag}",)], writes=[out_keys[n]],
             tag=f"{tag}add")


def ln_epilogue(b, tag, gen_tile, out_dram, n_range=range(NTB)):
    p = b.p
    ysb = [b.sb(f"lny{tag}{m}", 512, F32) for m in range(NCH)]
    sq = b.sb(f"lnsq{tag}", 512, F32)
    ones = b.sb(f"lnones{tag}", 1, F32)
    p.op("vector", lambda e: e.memset(ones, 1.0), writes=[("lnones", tag)])
    rows = b.sb(f"lnrows{tag}", 4 * 512, F32)
    ab = b.sb(f"lnab{tag}", 2 * 512, F32)
    rows_dram = b.dram(f"lnrowsd{tag}", [2, 512], F32)

    for n in n_range:
        st1 = b.ps(6)
        st2 = b.ps(7)
        for m in range(NCH):
            gen_tile(n, m, ysb[m])
            p.op("scalar", lambda e, m=m: e.activation(sq, ysb[m], AF.Square),
                 reads=[("lny", tag, m)], writes=[("lnsq", tag)])
            p.op("tensor", lambda e, m=m, st1=st1, ys=ysb[m]: e.matmul(
                st1[0:1, :], ones, ys, start=(m == 0), stop=(m == NCH - 1),
                skip_group_check=True),
                reads=[("lnones", tag), ("lny", tag, m)], writes=b.pk(6),
                tag="lnst1")
            p.op("tensor", lambda e, m=m, st2=st2: e.matmul(
                st2[0:1, :], ones, sq, start=(m == 0), stop=(m == NCH - 1),
                skip_group_check=True),
                reads=[("lnones", tag), ("lnsq", tag)], writes=b.pk(7),
                tag="lnst2")
        mrow = rows[0:1, 0:512]
        vrow = rows[0:1, 512:1024]
        arow = rows[0:1, 1024:1536]
        brow = rows[0:1, 1536:2048]
        p.op("vector", lambda e, st1=st1: e.tensor_scalar_mul(mrow, st1[0:1, :], 1.0 / D),
             reads=b.pk(6), writes=[("lnm", tag)])
        p.op("vector", lambda e, st2=st2: e.tensor_scalar_mul(vrow, st2[0:1, :], 1.0 / D),
             reads=b.pk(7), writes=[("lnv", tag)])
        p.op("vector", lambda e: e.tensor_mul(arow, mrow, mrow),
             reads=[("lnm", tag)], writes=[("lna", tag)])
        p.op("vector", lambda e: e.tensor_sub(vrow, vrow, arow),
             reads=[("lnv", tag), ("lna", tag)], writes=[("lnv", tag)])
        epsap = b.constv(1e-5)
        p.op("scalar", lambda e: e.activation(arow, vrow, AF.Sqrt, bias=epsap[0:1, :]),
             reads=[("lnv", tag), (f"__const_{1e-5}",)], writes=[("lna", tag)])
        p.op("vector", lambda e: e.reciprocal(arow, arow),
             reads=[("lna", tag)], writes=[("lna", tag)])
        p.op("vector", lambda e: e.tensor_mul(brow, mrow, arow),
             reads=[("lnm", tag), ("lna", tag)], writes=[("lnb", tag)])
        p.op("vector", lambda e: e.tensor_scalar_mul(brow, brow, -1.0),
             reads=[("lnb", tag)], writes=[("lnb", tag)])
        p.op("sync", lambda e: e.dma_start(out=rows_dram[0:1, :], in_=arow),
             reads=[("lna", tag)], writes=[("lnrd", tag)], dma=True)
        p.op("sync", lambda e: e.dma_start(out=rows_dram[1:2, :], in_=brow),
             reads=[("lnb", tag)], writes=[("lnrd2", tag)], dma=True)
        bc_in = bass.AP(rows_dram.tensor, 0, [[0, P], [512, 2], [1, 512]])
        p.op("gpsimd", lambda e: e.dma_start(
            out=ab.rearrange("p (a c) -> p a c", a=2), in_=bc_in),
            reads=[("lnrd", tag), ("lnrd2", tag)], writes=[("lnab", tag)],
            dma=True)
        a_b = ab[:, 0:512]
        b_b = ab[:, 512:1024]
        for m in range(NCH):
            p.op("vector", lambda e, m=m: e.tensor_mul(ysb[m], ysb[m], a_b),
                 reads=[("lny", tag, m), ("lnab", tag)], writes=[("lny", tag, m)])
            p.op("gpsimd", lambda e, m=m: e.tensor_add(ysb[m], ysb[m], b_b),
                 reads=[("lny", tag, m), ("lnab", tag)], writes=[("lny", tag, m)])
            p.op("sync", lambda e, m=m, n=n: e.dma_start(
                out=out_dram[m * P:(m + 1) * P, n * 512:(n + 1) * 512], in_=ysb[m]),
                reads=[("lny", tag, m)], writes=[("od", tag, m, n)], dma=True)


def stage_a(b, W, x0_dram, x1_dram):
    p = b.p
    NH = 32

    b.mark()
    qbf = [b.sb(f"qbfA{c}", S, BF16) for c in range(NCH)]
    krbf = [b.sb(f"krA{c}", S, BF16) for c in range(2)]
    ckv = [b.sb(f"ckvA{c}", S, BF16) for c in range(2)]
    kcb = [b.sb(f"kcA{c}", S, BF16) for c in range(2)]
    ident = b.sb("identA", P, BF16)
    p.op("sync", lambda e: e.dma_start(out=ident, in_=W["ident"]),
         writes=[("identA",)], dma=True)
    maskc = b.sb("maskA", P, F32)
    p.op("sync", lambda e: e.dma_start(out=maskc, in_=W["mask"]),
         writes=[("maskA",)], dma=True)
    permm = b.sb("permA", P, F32)
    p.op("sync", lambda e: e.dma_start(out=permm, in_=W["perm16"]),
         writes=[("permA",)], dma=True)

    b.mark()
    xbf = [b.sb(f"xbfA{c}", S, BF16) for c in range(NCH)]
    for c in range(NCH):
        p.op("gpsimd", lambda e, c=c: e.dma_start(
            out=xbf[c], in_=x0_dram[c * P:(c + 1) * P, :]),
            writes=[("xbf", c)], dma=True)
    tcos = b.sb("tcosA", S, F32)
    tsin = b.sb("tsinA", S, F32)
    praw = b.sb("prawA", S, F32)
    wsh = [b.sb(f"WshA{i}", NCH * P, BF16) for i in range(4)]

    def load_tabs(cname, sname):
        p.op("sync", lambda e: e.dma_start(out=tcos, in_=W[cname]),
             writes=[("tcosA",)], dma=True)
        p.op("sync", lambda e: e.dma_start(out=tsin, in_=W[sname]),
             writes=[("tsinA",)], dma=True)

    load_tabs("ta_cosq", "ta_sinq")

    def q_evac(m, n, pp, pk):
        p.op("scalar", lambda e, n=n, pp=pp: e.copy(
            praw[:, n * 512:(n + 1) * 512], pp), reads=pk,
            writes=[("prawA", n)], tag="qev")
        if n == NTB - 1:
            rope_fm(b, "q", "prawA", praw, [("qbf", m)] * 4, qbf[m], permm,
                    tcos, tsin, ("tcosA",), ("tsinA",), ("permA",))

    fm_proj(b, "qwA", W["q_wT"], NCH, NCH, xbf, "xbf", q_evac, wbuf=wsh, wkey="WshA", queues=("gpsimd", "scalar"))

    load_tabs("ta_cosk", "ta_sink")

    def kr_evac(m, n, pp, pk):
        p.op("scalar", lambda e, n=n, pp=pp: e.copy(
            praw[:, n * 512:(n + 1) * 512], pp), reads=pk,
            writes=[("prawA", n)], tag="krev")
        if n == NTB - 1:
            rope_fm(b, "kr", "prawA", praw, [("krbf", m)] * 4, krbf[m], permm,
                    tcos, tsin, ("tcosA",), ("tsinA",), ("permA",))

    fm_proj(b, "krwA", W["kr_wT"], NCH, 2, xbf, "xbf", kr_evac, wbuf=wsh, wkey="WshA", queues=("gpsimd", "scalar"))

    def ckv_evac(m, n, pp, pk):
        p.op("scalar", lambda e, m=m, n=n, pp=pp: e.copy(
            ckv[m][:, n * 512:(n + 1) * 512], pp), reads=pk,
            writes=[("ckv", m)], tag="ckvev")

    fm_proj(b, "dwA", W["down_wT"], NCH, 2, xbf, "xbf", ckv_evac, wbuf=wsh, wkey="WshA", queues=("gpsimd", "scalar"))

    def kc_evac(m, n, pp, pk):
        p.op("scalar", lambda e, m=m, n=n, pp=pp: e.copy(
            kcb[m][:, n * 512:(n + 1) * 512], pp), reads=pk,
            writes=[("kcb", m)], tag="kcev")

    fm_proj(b, "upkA", W["upk_wT"], 2, 2, ckv, "ckv", kc_evac, wbuf=wsh, wkey="WshA", queues=("gpsimd", "scalar"))

    p.barrier()
    b.release()

    b.mark()
    vtm = [b.sb(f"vtmA{t}", 512, BF16) for t in range(NTT)]
    upv = b.sb("upvA", 1024, BF16)
    for k in range(2):
        p.op("gpsimd", lambda e, k=k: e.dma_start(
            out=upv[:, k * 512:(k + 1) * 512], in_=W["upv_wT"][k]),
            writes=[("upvA", k)], dma=True)
    for tt in range(NTT):
        bank = tt % 4
        pp = b.ps(bank)
        for k in range(2):
            p.op("tensor", lambda e, tt=tt, k=k, pp=pp: e.matmul(
                pp, ckv[k][:, tt * P:(tt + 1) * P], upv[:, k * 512:(k + 1) * 512],
                start=(k == 0), stop=(k == 1)),
                reads=[("ckv", k), ("upvA", k)], writes=b.pk(bank), tag="vmm")
        p.op("scalar", lambda e, tt=tt, pp=pp: e.copy(vtm[tt], pp),
             reads=b.pk(bank), writes=[("vtm", tt)], tag="vev")

    Kb = [b.sb(f"KA{c}", S, BF16) for c in range(4)]
    Kd = [b.sb(f"KdA{c}", S, BF16) for c in range(4)]
    for h in range(8):
        sc_, sr = h // 4, 32 * (h % 4)
        dc, dr = h // 2, 64 * (h % 2)
        for (srcl, key, off) in ((krbf, "krbf", 0), (kcb, "kcb", 32)):
            src = srcl[sc_][sr:sr + 32, :]
            p.op("gpsimd", lambda e, src=src, o=Kb[dc][dr + off:dr + off + 32, :]:
                 e.dma_start(out=o, in_=src),
                 reads=[(key, sc_)], writes=[("Kb", dc)], dma=True)
            p.op("gpsimd", lambda e, src=src,
                 o=Kd[dc][(64 - dr) + off:(64 - dr) + off + 32, :]:
                 e.dma_start(out=o, in_=src),
                 reads=[(key, sc_)], writes=[("Kd", dc)], dma=True)

    att_dram = b.dram("attA", [D, S], BF16)
    NPR = 3
    probs = [b.sb(f"probsA{j}", S, BF16) for j in range(NPR)]
    probsT = [b.sb(f"probsTA{j}", S, BF16) for j in range(NPR)]
    nmv = [b.sb(f"nmvA{j}", 8, F32) for j in range(NPR)]
    gam = [b.sb(f"gamA{j}", 12, F32) for j in range(NPR)]
    rsum = [b.sb(f"rsA{j}", 1, F32) for j in range(NPR)]
    diagb = [b.sb(f"diagA{j}", 4 * P, BF16) for j in range(NPR)]
    attsb = [b.sb(f"attsbA{j}", P, BF16) for j in range(NPR)]

    it = 0
    scb_cur = [0]

    def emit_front_A(qc, Kkey, qap, kap, pr, i, nk, nkb):
        nm = nmv[pr]
        ga = gam[pr]
        dchunk = (i * P) // 512
        doff = (i * P) % 512
        for c in range(nkb):
            w = min(512, nk * P - c * 512)
            bank = scb_cur[0]
            scb_cur[0] = (bank + 1) % 4
            pp = b.ps(bank)
            p.op("tensor", lambda e, c=c, w=w, i=i, qap=qap, kap=kap, pp=pp:
                 e.matmul(pp[:, 0:w], qap[:, i * P:(i + 1) * P],
                          kap[:, c * 512:c * 512 + w], start=True, stop=True),
                 reads=[("qbf", qc), Kkey], writes=b.pk(bank), tag="sc")
            if c == dchunk:
                p.op("vector", lambda e, pp=pp, doff=doff: e.tensor_add(
                    pp[:, doff:doff + P], pp[:, doff:doff + P], maskc),
                    reads=b.pk(bank) + [("maskA",)], writes=b.pk(bank),
                    tag="mask")
            p.op("vector", lambda e, pp=pp, w=w, nm=nm, c=c: e.reduce_max(
                nm[:, c:c + 1], pp[:, 0:w], axis=AX.X, negate=True),
                reads=b.pk(bank), writes=[("nmA", pr, c)], tag="rmax")
            p.op("scalar", lambda e, pp=pp, w=w, c=c, pr=pr, nm=nm: e.activation(
                probs[pr][:, c * 512:c * 512 + w], pp[:, 0:w], AF.Exp,
                bias=nm[:, c:c + 1], accum_out=nm[:, 4 + c:5 + c]),
                reads=b.pk(bank) + [("nmA", pr, c)],
                writes=[("probsA", pr, c), ("seA", pr, c)], tag="exp")
        mneg = rsum[pr]
        nmk = [("nmA", pr, c) for c in range(nkb)]
        sek = [("seA", pr, c) for c in range(nkb)]
        if nkb == 1:
            p.op("vector", lambda e, nm=nm: e.reciprocal(ga[:, 0:1], nm[:, 4:5]),
                 reads=sek, writes=[("gamA", pr)], tag="cmb")
        else:
            p.op("vector", lambda e, nm=nm: e.tensor_tensor(
                mneg, nm[:, 0:1], nm[:, 1:2], ALU.min),
                reads=nmk, writes=[("MnA", pr)], tag="cmb")
            for c in range(2, nkb):
                p.op("vector", lambda e, nm=nm, c=c: e.tensor_tensor(
                    mneg, mneg, nm[:, c:c + 1], ALU.min),
                    reads=nmk + [("MnA", pr)], writes=[("MnA", pr)], tag="cmb")
            p.op("scalar", lambda e, nm=nm, nkb=nkb: e.activation(
                ga[:, 0:nkb], nm[:, 0:nkb], AF.Exp, scale=-1.0, bias=mneg),
                reads=nmk + [("MnA", pr)], writes=[("gamA", pr)], tag="cmb")
            p.op("vector", lambda e, nm=nm, nkb=nkb: e.tensor_mul(
                ga[:, 4:4 + nkb], ga[:, 0:nkb], nm[:, 4:4 + nkb]),
                reads=sek + [("gamA", pr)], writes=[("gam2A", pr)], tag="cmb")
            p.op("vector", lambda e, nkb=nkb: e.reduce_sum(
                ga[:, 8:9], ga[:, 4:4 + nkb], axis=AX.X),
                reads=[("gam2A", pr)], writes=[("ssA", pr)], tag="cmb")
            p.op("vector", lambda e: e.reciprocal(ga[:, 8:9], ga[:, 8:9]),
                 reads=[("ssA", pr)], writes=[("ssA", pr)], tag="cmb")
            p.op("vector", lambda e, nkb=nkb: e.tensor_scalar_mul(
                ga[:, 0:nkb], ga[:, 0:nkb], ga[:, 8:9]),
                reads=[("gamA", pr), ("ssA", pr)], writes=[("gamA", pr)],
                tag="cmb")
        for c in range(nkb):
            p.op("vector", lambda e, c=c: e.tensor_scalar_mul(
                diagb[pr][:, c * P:(c + 1) * P], ident, ga[:, c:c + 1]),
                reads=[("identA",), ("gamA", pr)],
                writes=[("dgA", pr, c)], tag="diag")

    def emit_tail_A(g, i, pr, nk, nkb):
        for kt in range(nk):
            bank = 4 + (kt // 4) % 2
            sub = kt % 4
            c = kt // 4
            p.op("tensor", lambda e, kt=kt, bank=bank, sub=sub, pr=pr, c=c:
                 e.matmul(b.ps(bank)[:, sub * 128:(sub + 1) * 128],
                          probs[pr][:, kt * P:(kt + 1) * P],
                          diagb[pr][:, c * P:(c + 1) * P],
                          start=(sub == 0), stop=True, skip_group_check=True),
                 reads=[("probsA", pr, c), ("dgA", pr, c)], writes=b.pk(bank),
                 tag="ptr")
            if sub == 3 or kt == nk - 1:
                n0 = (kt // 4) * 4
                cntc = kt - n0 + 1
                eng = "scalar" if (kt // 4) % 2 == 0 else "vector"
                p.op(eng, lambda e, n0=n0, cntc=cntc, bank=bank, pr=pr:
                     (e.copy if hasattr(e, "copy") else e.tensor_copy)(
                         probsT[pr][:, n0 * 128:(n0 + cntc) * 128],
                         b.ps(bank)[:, 0:cntc * 128]),
                     reads=b.pk(bank), writes=[("probsTA", pr, n0 // 4)],
                     tag="ptev")
        avb = 6 + (i % 2)
        av = b.ps(avb)
        for kt in range(nk):
            p.op("tensor", lambda e, kt=kt, g=g, av=av, nk=nk, pr=pr:
                 e.matmul(av[0:64, 0:P],
                          vtm[kt][:, (g // 4) * 64:(g // 4) * 64 + 64],
                          probsT[pr][:, kt * P:(kt + 1) * P],
                          start=(kt == 0), stop=(kt == nk - 1)),
                 reads=[("vtm", kt), ("probsTA", pr, kt // 4)],
                 writes=b.pk(avb), tag="pv")
        p.op("vector", lambda e, av=av, pr=pr, avb=avb: e.tensor_copy(
            attsb[pr][0:64, :], av[0:64, 0:P]),
            reads=b.pk(avb), writes=[("attsbA", pr)], tag="pvev")
        p.op("sync", lambda e, g=g, i=i, pr=pr: e.dma_start(
            out=att_dram[64 * g:64 * g + 64, i * P:(i + 1) * P],
            in_=attsb[pr][0:64, :]),
            reads=[("attsbA", pr)], writes=[("attAd", g, i)], dma=True)

    pend = None
    for g in range(NH):
        h = g // 4
        qc, qb_ = g // 2, 64 * (g % 2)
        aligned = (64 * (h % 2) == qb_)
        Ksrc, Kkey = (Kb, "Kb") if aligned else (Kd, "Kd")
        kc_ = h // 2
        qap = qbf[qc][qb_:qb_ + 64, :]
        kap = Ksrc[kc_][qb_:qb_ + 64, :]
        for i in range(NTT):
            nk = i + 1
            nkb = (nk * P + 511) // 512
            pr = it % NPR
            it += 1
            emit_front_A(qc, (Kkey, kc_), qap, kap, pr, i, nk, nkb)
            if pend is not None:
                emit_tail_A(*pend)
            pend = (g, i, pr, nk, nkb)
    emit_tail_A(*pend)

    p.barrier()
    b.release()
    b.release()

    b.mark()
    wall = [b.sb(f"owA{i}", NCH * P, BF16) for i in range(NCH)]
    for m in range(NCH):
        q = ("gpsimd", "sync", "scalar")[m % 3]
        p.op(q, lambda e, m=m: e.dma_start(out=wall[m], in_=W["o_wT"][m]),
             writes=[("owA", m)], dma=True, tag="owld")
    actb = [b.sb(f"oactA{k}", 512, BF16) for k in range(NCH)]
    xres = [b.sb(f"oxA{j}", 512, F32) for j in range(2)]

    def gen_tile(n, m, ysb):
        sl = wall[m]
        if m == 0:
            for k in range(NCH):
                p.op("sync", lambda e, k=k, n=n: e.dma_start(
                    out=actb[k], in_=att_dram[k * P:(k + 1) * P, n * 512:(n + 1) * 512]),
                    writes=[("oact", k)], dma=True)
        xr = xres[m % 2]
        p.op("sync", lambda e, m=m, n=n, xr=xr: e.dma_start(
            out=xr, in_=x0_dram[m * P:(m + 1) * P, n * 512:(n + 1) * 512]),
            writes=[("oxA", m % 2)], dma=True)
        bank = m % 4
        pp = b.ps(bank)
        for k in range(NCH):
            p.op("tensor", lambda e, k=k, sl=sl, pp=pp: e.matmul(
                pp, sl[:, k * P:(k + 1) * P], actb[k],
                start=(k == 0), stop=(k == NCH - 1), skip_group_check=True),
                reads=[("owA", m), ("oact", k)], writes=b.pk(bank), tag="omm")
        p.op("vector", lambda e, m=m, pp=pp, ysb=ysb, xr=xr: e.tensor_add(ysb, pp, xr),
             reads=b.pk(bank) + [("oxA", m % 2)], writes=[("lny", "A", m)], tag="oadd")

    ln_epilogue(b, "A", gen_tile, x1_dram)
    p.barrier()
    b.release()


def rms_norm(b, tag, src_dram, w_sb, w_key, out_chunks, out_key, f32_dram=None):
    p = b.p
    xs = [b.sb(f"rms_x{tag}{m}", 512, F32) for m in range(NCH)]
    xf = b.sb(f"rms_xf{tag}", 512, F32) if f32_dram is not None else None
    sq = b.sb(f"rms_sq{tag}", 512, F32)
    ones = b.constv(1.0)
    rows = b.sb(f"rms_row{tag}", 512, F32)
    abuf = b.sb(f"rms_ab{tag}", 512, F32)
    rows_dram = b.dram(f"rms_rd{tag}", [1, 512], F32)
    epsap = b.constv(1e-5)
    for n in range(NTB):
        st = b.ps(6)
        for m in range(NCH):
            p.op("sync", lambda e, m=m, n=n: e.dma_start(
                out=xs[m], in_=src_dram[m * P:(m + 1) * P, n * 512:(n + 1) * 512]),
                writes=[("rmsx", tag, m)], dma=True)
            p.op("scalar", lambda e, m=m: e.activation(sq, xs[m], AF.Square),
                 reads=[("rmsx", tag, m)], writes=[("rmssq", tag)])
            p.op("tensor", lambda e, m=m, st=st: e.matmul(
                st[0:1, :], ones, sq, start=(m == 0), stop=(m == NCH - 1),
                skip_group_check=True),
                reads=[(f"__const_{1.0}",), ("rmssq", tag)], writes=b.pk(6),
                tag="rmsst")
        p.op("vector", lambda e, st=st: e.tensor_scalar_mul(rows[0:1, :], st[0:1, :], 1.0 / D),
             reads=b.pk(6), writes=[("rmsr", tag)])
        p.op("scalar", lambda e: e.activation(rows[0:1, :], rows[0:1, :], AF.Sqrt,
                                              bias=epsap[0:1, :]),
             reads=[("rmsr", tag), (f"__const_{1e-5}",)], writes=[("rmsr", tag)])
        p.op("vector", lambda e: e.reciprocal(rows[0:1, :], rows[0:1, :]),
             reads=[("rmsr", tag)], writes=[("rmsr", tag)])
        p.op("sync", lambda e: e.dma_start(out=rows_dram, in_=rows[0:1, :]),
             reads=[("rmsr", tag)], writes=[("rmsrd", tag)], dma=True)
        bc = bass.AP(rows_dram.tensor, 0, [[0, P], [1, 512]])
        p.op("gpsimd", lambda e: e.dma_start(out=abuf, in_=bc),
             reads=[("rmsrd", tag)], writes=[("rmsab", tag)], dma=True)
        for m in range(NCH):
            p.op("vector", lambda e, m=m: e.tensor_mul(xs[m], xs[m], abuf),
                 reads=[("rmsx", tag, m), ("rmsab", tag)], writes=[("rmsx", tag, m)])
            p.op("vector", lambda e, m=m, n=n: e.tensor_scalar_mul(
                out_chunks[m][:, n * 512:(n + 1) * 512], xs[m], w_sb[:, m:m + 1]),
                reads=[("rmsx", tag, m), w_key], writes=[(out_key, m)],
                tag="rmsw")
            if f32_dram is not None:
                p.op("vector", lambda e, m=m: e.tensor_scalar_mul(
                    xf, xs[m], w_sb[:, m:m + 1]),
                    reads=[("rmsx", tag, m), w_key], writes=[("rmsxf", tag)],
                    tag="rmswf")
                p.op("sync", lambda e, m=m, n=n: e.dma_start(
                    out=f32_dram[m * P:(m + 1) * P, n * 512:(n + 1) * 512], in_=xf),
                    reads=[("rmsxf", tag)], writes=[("xf32d", tag, m, n)], dma=True)


def stage_b(b, W, x1_dram, x2_dram, phase1_only=False):
    p = b.p
    NHB = 16

    b.mark()
    ln12 = b.sb("ln12B", 32, F32)
    p.op("sync", lambda e: e.dma_start(out=ln12[:, 0:16], in_=W["ln1w16"]),
         writes=[("ln1w",)], dma=True)
    p.op("sync", lambda e: e.dma_start(out=ln12[:, 16:32], in_=W["ln2w16"]),
         writes=[("ln2w",)], dma=True)
    identb = b.sb("identB", P, BF16)
    p.op("sync", lambda e: e.dma_start(out=identb, in_=W["ident"]),
         writes=[("identB",)], dma=True)
    identf = b.sb("identfB", P, F32)
    p.op("sync", lambda e: e.dma_start(out=identf, in_=W["identf"]),
         writes=[("identfB",)], dma=True)
    maskc = b.sb("maskB", P, F32)
    p.op("sync", lambda e: e.dma_start(out=maskc, in_=W["mask"]),
         writes=[("maskB",)], dma=True)
    permm = b.sb("permB", P, F32)
    p.op("sync", lambda e: e.dma_start(out=permm, in_=W["perm32"]),
         writes=[("permB",)], dma=True)

    K_dram = b.dram("KB", [D, S], BF16)
    Q_dram = b.dram("QB", [D, S], BF16)
    v_dram = b.dram("VB", [S, D], BF16)
    att_dram = b.dram("attB", [D, S], BF16)
    h_dram = b.dram("hB", [D, S], F32)
    moe_dram = b.dram("moeB", [D, S], F32)
    comb_dram = b.dram("combB", [8, S], F32)

    b.mark()
    xn = [b.sb(f"xnB{c}", S, BF16) for c in range(NCH)]
    rms_norm(b, "B1", x1_dram, ln12[:, 0:16], ("ln1w",), xn, "xnB")

    kvd = [b.sb(f"kvdB{c}", S, BF16) for c in range(4)]
    qd = [b.sb(f"qdB{c}", S, BF16) for c in range(4)]
    wsh = [b.sb(f"WshB{i}", NCH * P, BF16) for i in range(3)]
    stg2 = [b.sb(f"stgB{j}", S, BF16) for j in range(2)]
    praw = b.sb("prawB", S, F32)
    tcos = b.sb("tcosB", S, F32)
    tsin = b.sb("tsinB", S, F32)

    def mk_evac(dst, key):
        def _evac(m, n, pp, pk):
            p.op("scalar", lambda e, m=m, n=n, pp=pp: e.copy(
                dst[m][:, n * 512:(n + 1) * 512], pp), reads=pk,
                writes=[(key, m)], tag=f"{key}ev")
        return _evac

    fm_proj(b, "kvdB", W["kvdw"], NCH, 4, xn, "xnB", mk_evac(kvd, "kvdB"), wbuf=wsh, wkey="WshB", queues=("gpsimd", "scalar", "sync"))
    fm_proj(b, "qdB", W["qdw"], NCH, 4, xn, "xnB", mk_evac(qd, "qdB"), wbuf=wsh, wkey="WshB", queues=("gpsimd", "scalar", "sync"))


    def kp_evac(m, n, pp, pk):
        stg = stg2[m % 2]
        p.op("scalar", lambda e, n=n, pp=pp, stg=stg: e.copy(
            stg[:, n * 512:(n + 1) * 512], pp), reads=pk,
            writes=[("stgB", m % 2, n)], tag="kpev")
        if n == NTB - 1:
            for half in range(2):
                hrow = 128 * (2 * m + half)
                p.op("sync", lambda e, hrow=hrow, half=half, stg=stg: e.dma_start(
                    out=K_dram[hrow:hrow + 64, :], in_=stg[64 * half:64 * half + 64, :]),
                    reads=[("stgB", m % 2, nn) for nn in range(4)],
                    writes=[("KBd", 2 * m + half, "n")], dma=True)

    fm_proj(b, "kuB", W["kuw"], 4, 8, kvd, "kvdB", kp_evac, wbuf=wsh, wkey="WshB", queues=("gpsimd", "scalar", "sync"))

    def qp_evac(m, n, pp, pk):
        stg = stg2[m % 2]
        p.op("vector", lambda e, n=n, pp=pp, stg=stg: e.tensor_scalar_mul(
            stg[:, n * 512:(n + 1) * 512], pp, 0.08838834764831845), reads=pk,
            writes=[("stgB", m % 2, n)], tag="qpev")
        if n == NTB - 1:
            for half in range(2):
                hrow = 128 * (2 * m + half)
                p.op("sync", lambda e, hrow=hrow, half=half, stg=stg: e.dma_start(
                    out=Q_dram[hrow:hrow + 64, :], in_=stg[64 * half:64 * half + 64, :]),
                    reads=[("stgB", m % 2, nn) for nn in range(4)],
                    writes=[("QBd", 2 * m + half, "n")], dma=True)

    fm_proj(b, "quB", W["quw"], 4, 8, qd, "qdB", qp_evac, wbuf=wsh, wkey="WshB", queues=("gpsimd", "scalar", "sync"))

    p.op("sync", lambda e: e.dma_start(out=tcos, in_=W["tb_cos"]),
         writes=[("tcosB",)], dma=True)
    p.op("sync", lambda e: e.dma_start(out=tsin, in_=W["tb_sin"]),
         writes=[("tsinB",)], dma=True)

    def kr_evac(m, n, pp, pk):
        stg = stg2[m % 2]
        p.op("scalar", lambda e, n=n, pp=pp: e.copy(
            praw[:, n * 512:(n + 1) * 512], pp), reads=pk,
            writes=[("prawB", n)], tag="krevB")
        if n == NTB - 1:
            rope_fm(b, "krB", "prawB", praw,
                    [("stgB", m % 2, nn) for nn in range(4)],
                    stg, permm, tcos, tsin, ("tcosB",), ("tsinB",), ("permB",))
            for half in range(2):
                hrow = 128 * (2 * m + half)
                p.op("sync", lambda e, hrow=hrow, half=half, stg=stg: e.dma_start(
                    out=K_dram[hrow + 64:hrow + 128, :],
                    in_=stg[64 * half:64 * half + 64, :]),
                    reads=[("stgB", m % 2, nn) for nn in range(4)],
                    writes=[("KBd", 2 * m + half, "r")], dma=True)

    fm_proj(b, "rkB", W["ropekw"], NCH, 8, xn, "xnB", kr_evac, wbuf=wsh, wkey="WshB", queues=("gpsimd", "scalar", "sync"))

    p.op("sync", lambda e: e.dma_start(out=tcos, in_=W["tb_cosq"]),
         writes=[("tcosB",)], dma=True)
    p.op("sync", lambda e: e.dma_start(out=tsin, in_=W["tb_sinq"]),
         writes=[("tsinB",)], dma=True)

    def qr_evac(m, n, pp, pk):
        stg = stg2[m % 2]
        p.op("scalar", lambda e, n=n, pp=pp: e.copy(
            praw[:, n * 512:(n + 1) * 512], pp), reads=pk,
            writes=[("prawB", n)], tag="qrevB")
        if n == NTB - 1:
            rope_fm(b, "qrB", "prawB", praw,
                    [("stgB", m % 2, nn) for nn in range(4)],
                    stg, permm, tcos, tsin, ("tcosB",), ("tsinB",), ("permB",))
            for half in range(2):
                hrow = 128 * (2 * m + half)
                p.op("sync", lambda e, hrow=hrow, half=half, stg=stg: e.dma_start(
                    out=Q_dram[hrow + 64:hrow + 128, :],
                    in_=stg[64 * half:64 * half + 64, :]),
                    reads=[("stgB", m % 2, nn) for nn in range(4)],
                    writes=[("QBd", 2 * m + half, "r")], dma=True)

    fm_proj(b, "rqB", W["ropeqw"], 4, 8, qd, "qdB", qr_evac, wbuf=wsh, wkey="WshB", queues=("gpsimd", "scalar", "sync"))

    vuw = b.sb("vuwB", 4 * S, BF16)
    for k in range(4):
        p.op("gpsimd", lambda e, k=k: e.dma_start(
            out=vuw[:, k * S:(k + 1) * S], in_=W["vuw"][k]),
            writes=[("vuwB", k)], dma=True)
    vsb = b.sb("vsbB", 512, BF16)
    for tt in range(NTT):
        for nb in range(4):
            bank = (tt * 4 + nb) % 4
            pp = b.ps(bank)
            for k in range(4):
                p.op("tensor", lambda e, tt=tt, nb=nb, k=k, pp=pp: e.matmul(
                    pp, kvd[k][:, tt * P:(tt + 1) * P],
                    vuw[:, k * S + nb * 512:k * S + (nb + 1) * 512],
                    start=(k == 0), stop=(k == 3)),
                    reads=[("kvdB", k), ("vuwB", k)], writes=b.pk(bank), tag="vmmB")
            p.op("vector", lambda e, pp=pp: e.tensor_copy(vsb, pp),
                 reads=b.pk(bank), writes=[("vsbB",)], tag="vevB")
            p.op("sync", lambda e, tt=tt, nb=nb: e.dma_start(
                out=v_dram[tt * P:(tt + 1) * P, nb * 512:(nb + 1) * 512], in_=vsb),
                reads=[("vsbB",)], writes=[("VBd", tt, nb)], dma=True)

    p.barrier()
    b.release()
    if phase1_only:
        b.release()
        return

    b.mark()
    qh = [b.sb(f"qhB{i}", S, BF16) for i in range(2)]
    kh = [b.sb(f"khB{i}", S, BF16) for i in range(2)]
    vh = [b.sb(f"vhB{i}", S, BF16) for i in range(2)]
    NPRB = 3
    probs = [b.sb(f"probsB{j}", S, BF16) for j in range(NPRB)]
    probsT = [b.sb(f"probsTB{j}", S, BF16) for j in range(NPRB)]
    nmv = [b.sb(f"nmvB{j}", 8, F32) for j in range(NPRB)]
    gam = [b.sb(f"gamB{j}", 12, F32) for j in range(NPRB)]
    rsum = [b.sb(f"rsB{j}", 1, F32) for j in range(NPRB)]
    diagb = [b.sb(f"diagB{j}", 4 * P, BF16) for j in range(NPRB)]
    attsb = [b.sb(f"attsbB{j}", P, BF16) for j in range(NPRB)]

    scb_curB = [0]

    def emit_front_B(sl_, qap, kap, pr, i, nk, nkb):
        nm = nmv[pr]
        ga = gam[pr]
        dchunk = (i * P) // 512
        doff = (i * P) % 512
        for c in range(nkb):
            w = min(512, nk * P - c * 512)
            bank = scb_curB[0]
            scb_curB[0] = (bank + 1) % 4
            pp = b.ps(bank)
            p.op("tensor", lambda e, c=c, w=w, i=i, qap=qap, kap=kap, pp=pp:
                 e.matmul(pp[:, 0:w], qap[:, i * P:(i + 1) * P],
                          kap[:, c * 512:c * 512 + w], start=True, stop=True),
                 reads=[("qhB", sl_), ("khB", sl_)], writes=b.pk(bank),
                 tag="scB")
            if c == dchunk:
                p.op("vector", lambda e, pp=pp, doff=doff: e.tensor_add(
                    pp[:, doff:doff + P], pp[:, doff:doff + P], maskc),
                    reads=b.pk(bank) + [("maskB",)], writes=b.pk(bank),
                    tag="maskB")
            p.op("vector", lambda e, pp=pp, w=w, nm=nm, c=c: e.reduce_max(
                nm[:, c:c + 1], pp[:, 0:w], axis=AX.X, negate=True),
                reads=b.pk(bank), writes=[("nmB", pr, c)], tag="rmaxB")
            p.op("scalar", lambda e, pp=pp, w=w, c=c, pr=pr, nm=nm: e.activation(
                probs[pr][:, c * 512:c * 512 + w], pp[:, 0:w], AF.Exp,
                bias=nm[:, c:c + 1], accum_out=nm[:, 4 + c:5 + c]),
                reads=b.pk(bank) + [("nmB", pr, c)],
                writes=[("probsB", pr, c), ("seB", pr, c)], tag="expB")
        mneg = rsum[pr]
        nmk = [("nmB", pr, c) for c in range(nkb)]
        sek = [("seB", pr, c) for c in range(nkb)]
        if nkb == 1:
            p.op("vector", lambda e, nm=nm: e.reciprocal(ga[:, 0:1], nm[:, 4:5]),
                 reads=sek, writes=[("gamB", pr)], tag="cmbB")
        else:
            p.op("vector", lambda e, nm=nm: e.tensor_tensor(
                mneg, nm[:, 0:1], nm[:, 1:2], ALU.min),
                reads=nmk, writes=[("MnB", pr)], tag="cmbB")
            for c in range(2, nkb):
                p.op("vector", lambda e, nm=nm, c=c: e.tensor_tensor(
                    mneg, mneg, nm[:, c:c + 1], ALU.min),
                    reads=nmk + [("MnB", pr)], writes=[("MnB", pr)], tag="cmbB")
            p.op("scalar", lambda e, nm=nm, nkb=nkb: e.activation(
                ga[:, 0:nkb], nm[:, 0:nkb], AF.Exp, scale=-1.0, bias=mneg),
                reads=nmk + [("MnB", pr)], writes=[("gamB", pr)], tag="cmbB")
            p.op("vector", lambda e, nm=nm, nkb=nkb: e.tensor_mul(
                ga[:, 4:4 + nkb], ga[:, 0:nkb], nm[:, 4:4 + nkb]),
                reads=sek + [("gamB", pr)], writes=[("gam2B", pr)], tag="cmbB")
            p.op("vector", lambda e, nkb=nkb: e.reduce_sum(
                ga[:, 8:9], ga[:, 4:4 + nkb], axis=AX.X),
                reads=[("gam2B", pr)], writes=[("ssB", pr)], tag="cmbB")
            p.op("vector", lambda e: e.reciprocal(ga[:, 8:9], ga[:, 8:9]),
                 reads=[("ssB", pr)], writes=[("ssB", pr)], tag="cmbB")
            p.op("vector", lambda e, nkb=nkb: e.tensor_scalar_mul(
                ga[:, 0:nkb], ga[:, 0:nkb], ga[:, 8:9]),
                reads=[("gamB", pr), ("ssB", pr)], writes=[("gamB", pr)],
                tag="cmbB")
        for c in range(nkb):
            p.op("vector", lambda e, c=c: e.tensor_scalar_mul(
                diagb[pr][:, c * P:(c + 1) * P], identb, ga[:, c:c + 1]),
                reads=[("identB",), ("gamB", pr)],
                writes=[("dgB", pr, c)], tag="diagB")

    def emit_tail_B(g, i, pr, nk, nkb, sl_, vap):
        for kt in range(nk):
            bank = 4 + (kt // 4) % 2
            sub = kt % 4
            c = kt // 4
            eng = "scalar" if (kt // 4) % 2 == 0 else "vector"
            p.op("tensor", lambda e, kt=kt, bank=bank, sub=sub, pr=pr, c=c:
                 e.matmul(b.ps(bank)[:, sub * 128:(sub + 1) * 128],
                          probs[pr][:, kt * P:(kt + 1) * P],
                          diagb[pr][:, c * P:(c + 1) * P],
                          start=(sub == 0), stop=True, skip_group_check=True),
                 reads=[("probsB", pr, c), ("dgB", pr, c)], writes=b.pk(bank),
                 tag="ptrB")
            if sub == 3 or kt == nk - 1:
                n0 = (kt // 4) * 4
                cntc = kt - n0 + 1
                p.op(eng, lambda e, n0=n0, cntc=cntc, bank=bank, pr=pr:
                     (e.copy if hasattr(e, "copy") else e.tensor_copy)(
                         probsT[pr][:, n0 * 128:(n0 + cntc) * 128],
                         b.ps(bank)[:, 0:cntc * 128]),
                     reads=b.pk(bank), writes=[("probsTB", pr, n0 // 4)],
                     tag="ptevB")
        avb = 6 + (i % 2)
        av = b.ps(avb)
        for kt in range(nk):
            p.op("tensor", lambda e, kt=kt, av=av, nk=nk, vap=vap, pr=pr:
                 e.matmul(av[:, 0:P], vap[:, kt * P:(kt + 1) * P],
                          probsT[pr][:, kt * P:(kt + 1) * P],
                          start=(kt == 0), stop=(kt == nk - 1)),
                 reads=[("vhB", sl_), ("probsTB", pr, kt // 4)],
                 writes=b.pk(avb), tag="pvB")
        p.op("vector", lambda e, av=av, pr=pr, avb=avb: e.tensor_copy(
            attsb[pr], av[:, 0:P]),
            reads=b.pk(avb), writes=[("attsbB", pr)], tag="pvevB")
        p.op("sync", lambda e, g=g, i=i, pr=pr: e.dma_start(
            out=att_dram[g * P:(g + 1) * P, i * P:(i + 1) * P],
            in_=attsb[pr]),
            reads=[("attsbB", pr)], writes=[("attBd", g, i)], dma=True)

    it = 0
    pend = None
    for g in range(NHB):
        sl_ = g % 2
        p.op("sync", lambda e, g=g, d=qh[sl_]: e.dma_start(
            out=d, in_=Q_dram[g * P:(g + 1) * P, :]),
            reads=[("QBd", g, "n"), ("QBd", g, "r")],
            writes=[("qhB", sl_)], dma=True)
        p.op("sync", lambda e, g=g, d=kh[sl_]: e.dma_start(
            out=d, in_=K_dram[g * P:(g + 1) * P, :]),
            reads=[("KBd", g, "n"), ("KBd", g, "r")],
            writes=[("khB", sl_)], dma=True)
        vin = bass.AP(v_dram.tensor, g * P,
                      [[D, P], [D * P, NTT], [1, P]])
        p.op("gpsimd", lambda e, g=g, d=vh[sl_], vin=vin: e.dma_start(
            out=d.rearrange("p (t c) -> p t c", t=NTT), in_=vin),
            reads=[("VBd", tt, nb) for tt in range(NTT) for nb in range(4)],
            writes=[("vhB", sl_)], dma=True)
        qap, kap, vap = qh[sl_], kh[sl_], vh[sl_]
        for i in range(NTT):
            nk = i + 1
            nkb = (nk * P + 511) // 512
            pr = it % NPRB
            it += 1
            emit_front_B(sl_, qap, kap, pr, i, nk, nkb)
            if pend is not None:
                emit_tail_B(*pend)
            pend = (g, i, pr, nk, nkb, sl_, vap)
    emit_tail_B(*pend)

    p.barrier()
    b.release()

    b.mark()
    wall = [b.sb(f"owB{i}", NCH * P, BF16) for i in range(NCH)]
    for m in range(NCH):
        q = ("gpsimd", "sync", "scalar")[m % 3]
        p.op(q, lambda e, m=m: e.dma_start(out=wall[m], in_=W["owB"][m]),
             writes=[("owB", m)], dma=True, tag="owldB")
    actb = [b.sb(f"oactB{k}", 512, BF16) for k in range(NCH)]
    xres = [b.sb(f"oxB{j}", 512, F32) for j in range(2)]
    hsb = [b.sb(f"hsbB{j}", 512, F32) for j in range(2)]
    for n in range(NTB):
        for k in range(NCH):
            p.op("sync", lambda e, k=k, n=n: e.dma_start(
                out=actb[k], in_=att_dram[k * P:(k + 1) * P, n * 512:(n + 1) * 512]),
                writes=[("oactB", k)], dma=True)
        for m in range(NCH):
            sl = wall[m]
            xr = xres[m % 2]
            hs = hsb[m % 2]
            p.op("sync", lambda e, m=m, n=n, xr=xr: e.dma_start(
                out=xr, in_=x1_dram[m * P:(m + 1) * P, n * 512:(n + 1) * 512]),
                writes=[("oxB", m % 2)], dma=True)
            bank = m % 4
            pp = b.ps(bank)
            for k in range(NCH):
                p.op("tensor", lambda e, k=k, sl=sl, pp=pp: e.matmul(
                    pp, sl[:, k * P:(k + 1) * P], actb[k],
                    start=(k == 0), stop=(k == NCH - 1)),
                    reads=[("owB", m), ("oactB", k)], writes=b.pk(bank),
                    tag="ommB")
            p.op("vector", lambda e, pp=pp, xr=xr, hs=hs: e.tensor_add(hs, pp, xr),
                 reads=b.pk(bank) + [("oxB", m % 2)], writes=[("hsbB", m % 2)],
                 tag="oaddB")
            p.op("sync", lambda e, m=m, n=n, hs=hs: e.dma_start(
                out=h_dram[m * P:(m + 1) * P, n * 512:(n + 1) * 512], in_=hs),
                reads=[("hsbB", m % 2)], writes=[("hBd", m, n)], dma=True)
    p.barrier()
    b.release()

    b.mark()
    xm = [b.sb(f"xmB{c}", S, BF16) for c in range(NCH)]
    xm32_dram = b.dram("xm32B", [D, S], F32)
    rms_norm(b, "B2", h_dram, ln12[:, 16:32], ("ln2w",), xm, "xmB",
             f32_dram=xm32_dram)

    rweights = b.sb("rwB", NCH * 8, F32)
    for k in range(NCH):
        p.op("sync", lambda e, k=k: e.dma_start(
            out=rweights[:, k * 8:k * 8 + 7], in_=W["routerw"][k]),
            writes=[("rwB", k)], dma=True)
    xftile = [b.sb(f"xftB{i}", P, F32) for i in range(2)]
    rbias = b.sb("rbB", 8, F32)
    bcb = bass.AP(W["rb7"].tensor, 0, [[0, P], [1, 7]])
    p.op("gpsimd", lambda e: e.dma_start(out=rbias[:, 0:7], in_=bcb),
         writes=[("rbB",)], dma=True)
    lgt = b.sb("lgB", 8, F32)
    m1 = b.sb("m1B", 1, F32)
    m2 = b.sb("m2B", 1, F32)
    msk = b.sb("mskB", 8, F32)
    comb = b.sb("combB", 8, F32)
    combT = b.sb("combTB", S, F32)
    for tt in range(NTT):
        pp = b.ps(5)
        for k in range(NCH):
            xft = xftile[k % 2]
            p.op("sync", lambda e, tt=tt, k=k, xft=xft: e.dma_start(
                out=xft, in_=xm32_dram[k * P:(k + 1) * P, tt * P:(tt + 1) * P]),
                reads=[("xf32d", "B2", k, (tt * P) // 512)],
                writes=[("xftB", k % 2)], dma=True)
            p.op("tensor", lambda e, tt=tt, k=k, pp=pp, xft=xft: e.matmul(
                pp[:, 0:7], xft, rweights[:, k * 8:k * 8 + 7],
                start=(k == 0), stop=(k == NCH - 1)),
                reads=[("xftB", k % 2), ("rwB", k)], writes=b.pk(5), tag="rlog")
        p.op("vector", lambda e, pp=pp: e.tensor_add(lgt[:, 0:7], pp[:, 0:7], rbias[:, 0:7]),
             reads=b.pk(5) + [("rbB",)], writes=[("lgB",)])
        p.op("scalar", lambda e: e.activation(lgt[:, 0:7], lgt[:, 0:7], AF.Sigmoid),
             reads=[("lgB",)], writes=[("lgB",)])
        p.op("vector", lambda e: e.reduce_max(m1, lgt[:, 0:7], axis=AX.X),
             reads=[("lgB",)], writes=[("m1B",)])
        p.op("vector", lambda e: e.tensor_single_scalar(msk[:, 0:7], lgt[:, 0:7], m1, ALU.is_equal),
             reads=[("lgB",), ("m1B",)], writes=[("mskB",)])
        p.op("vector", lambda e: e.scalar_tensor_tensor(
            msk[:, 0:7], msk[:, 0:7], -1e9, lgt[:, 0:7], ALU.mult, ALU.add),
            reads=[("mskB",), ("lgB",)], writes=[("mskB",)])
        p.op("vector", lambda e: e.reduce_max(m2, msk[:, 0:7], axis=AX.X),
             reads=[("mskB",)], writes=[("m2B",)])
        p.op("vector", lambda e: e.tensor_single_scalar(msk[:, 0:7], lgt[:, 0:7], m2, ALU.is_ge),
             reads=[("lgB",), ("m2B",)], writes=[("mskB",)])
        p.op("vector", lambda e: e.tensor_add(m1, m1, m2),
             reads=[("m1B",), ("m2B",)], writes=[("m1B",)])
        p.op("vector", lambda e: e.reciprocal(m1, m1),
             reads=[("m1B",)], writes=[("m1B",)])
        p.op("vector", lambda e: e.tensor_mul(comb[:, 0:7], lgt[:, 0:7], msk[:, 0:7]),
             reads=[("lgB",), ("mskB",)], writes=[("combB",)])
        p.op("vector", lambda e: e.tensor_scalar_mul(comb[:, 0:7], comb[:, 0:7], m1),
             reads=[("combB",), ("m1B",)], writes=[("combB",)])
        pt = b.ps(4)
        p.op("tensor", lambda e, pt=pt: e.matmul(
            pt[0:7, 0:P], comb[:, 0:7], identf, start=True, stop=True),
            reads=[("combB",), ("identfB",)], writes=b.pk(4), tag="rtr")
        p.op("vector", lambda e, tt=tt, pt=pt: e.tensor_copy(
            combT[0:7, tt * P:(tt + 1) * P], pt[0:7, 0:P]),
            reads=b.pk(4), writes=[("combTB",)])
    p.op("sync", lambda e: e.dma_start(out=comb_dram[0:7, :], in_=combT[0:7, :]),
         reads=[("combTB",)], writes=[("combBd",)], dma=True)

    NWB = 4
    wg2 = [b.sb(f"wgB{i}", NCH * P, BF16) for i in range(NWB)]
    wu2 = [b.sb(f"wuB{i}", NCH * P, BF16) for i in range(NWB)]
    wd2 = [b.sb(f"wdB{i}", 8 * P, BF16) for i in range(NWB)]
    hbf = [b.sb(f"hbfB{m}", S, BF16) for m in range(8)]
    gs2 = [b.sb(f"gsB{j}", 512, F32) for j in range(2)]
    ceb = b.sb("cebB", S, F32)
    ysb2 = [b.sb(f"ymoeB{j}", 512, F32) for j in range(2)]

    for e_i in range(8):
        shared = (e_i == 0)
        if not shared:
            bce = bass.AP(comb_dram.tensor, (e_i - 1) * S, [[0, P], [1, S]])
            p.op("gpsimd", lambda e, bce=bce: e.dma_start(out=ceb, in_=bce),
                 reads=[("combBd",)], writes=[("cebB",)], dma=True)
        for m in range(8):
            slg = wg2[m % NWB]
            gname = "shg" if shared else "rgate"
            uname = "shu" if shared else "rup"
            gidx = m if shared else (e_i - 1) * 8 + m
            p.op("sync", lambda e, slg=slg, gname=gname, gidx=gidx: e.dma_start(
                out=slg, in_=W[gname][gidx]),
                writes=[("wgB", m % NWB)], dma=True, tag="wgld")
            slu = wu2[m % NWB]
            p.op("scalar", lambda e, slu=slu, uname=uname, gidx=gidx: e.dma_start(
                out=slu, in_=W[uname][gidx]),
                writes=[("wuB", m % NWB)], dma=True, tag="wuld")
            for n in range(NTB):
                gs = gs2[n % 2]
                bank = n % 2
                pp = b.ps(bank)
                for k in range(NCH):
                    p.op("tensor", lambda e, k=k, n=n, slg=slg, pp=pp: e.matmul(
                        pp, slg[:, k * P:(k + 1) * P],
                        xm[k][:, n * 512:(n + 1) * 512],
                        start=(k == 0), stop=(k == NCH - 1)),
                        reads=[("wgB", m % NWB), ("xmB", k)], writes=b.pk(bank),
                        tag="gmm")
                p.op("scalar", lambda e, pp=pp, gs=gs: e.activation(gs, pp, AF.Silu),
                     reads=b.pk(bank), writes=[("gsB", n % 2)], tag="silu")
                bank2 = 2 + n % 2
                pp2 = b.ps(bank2)
                for k in range(NCH):
                    p.op("tensor", lambda e, k=k, n=n, slu=slu, pp2=pp2: e.matmul(
                        pp2, slu[:, k * P:(k + 1) * P],
                        xm[k][:, n * 512:(n + 1) * 512],
                        start=(k == 0), stop=(k == NCH - 1)),
                        reads=[("wuB", m % NWB), ("xmB", k)], writes=b.pk(bank2),
                        tag="umm")
                p.op("vector", lambda e, pp2=pp2, gs=gs: e.scalar_tensor_tensor(
                    gs, gs, 0.125, pp2, ALU.mult, ALU.mult),
                    reads=[("gsB", n % 2)] + b.pk(bank2), writes=[("gsB", n % 2)],
                    tag="hmul")
                if shared:
                    p.op("vector", lambda e, m=m, n=n, gs=gs: e.tensor_copy(
                        hbf[m][:, n * 512:(n + 1) * 512], gs),
                        reads=[("gsB", n % 2)], writes=[("hbfB", m)], tag="hcp")
                else:
                    p.op("vector", lambda e, m=m, n=n, gs=gs: e.tensor_mul(
                        hbf[m][:, n * 512:(n + 1) * 512], gs,
                        ceb[:, n * 512:(n + 1) * 512]),
                        reads=[("gsB", n % 2), ("cebB",)], writes=[("hbfB", m)],
                        tag="hce")
        for dm in range(NCH):
            sld = wd2[dm % NWB]
            didx = dm if shared else (e_i - 1) * NCH + dm
            dname = "shd" if shared else "rdown"
            q = ("sync", "scalar", "gpsimd")[dm % 3]
            p.op(q, lambda e, sld=sld, dname=dname, didx=didx: e.dma_start(
                out=sld, in_=W[dname][didx]),
                writes=[("wdB", dm % NWB)], dma=True, tag="wdld")
            for n in range(NTB):
                ysb = ysb2[n % 2]
                bank = 4 + (dm * NTB + n) % 4
                pp = b.ps(bank)
                for k in range(8):
                    p.op("tensor", lambda e, k=k, n=n, sld=sld, pp=pp: e.matmul(
                        pp, sld[:, k * P:(k + 1) * P],
                        hbf[k][:, n * 512:(n + 1) * 512],
                        start=(k == 0), stop=(k == 7)),
                        reads=[("wdB", dm % NWB), ("hbfB", k)], writes=b.pk(bank),
                        tag="dmm")
                p.op("vector", lambda e, pp=pp, ysb=ysb: e.tensor_copy(ysb, pp),
                     reads=b.pk(bank), writes=[("ymoeB", n % 2)], tag="dev")
                if shared:
                    p.op("gpsimd", lambda e, dm=dm, n=n, ysb=ysb: e.dma_start(
                        out=moe_dram[dm * P:(dm + 1) * P, n * 512:(n + 1) * 512],
                        in_=ysb),
                        reads=[("ymoeB", n % 2)], writes=[("moeBd", dm, n)], dma=True)
                else:
                    p.op("gpsimd", lambda e, dm=dm, n=n, ysb=ysb: e.dma_start(
                        out=moe_dram[dm * P:(dm + 1) * P, n * 512:(n + 1) * 512],
                        in_=ysb, accum_op=ALU.add),
                        reads=[("ymoeB", n % 2)], writes=[("moeBd", dm, n)], dma=True)
    p.barrier()
    b.release()

    b.mark()
    t1 = [b.sb(f"lnt1B{j}", 512, F32) for j in range(2)]
    t2 = [b.sb(f"lnt2B{j}", 512, F32) for j in range(2)]
    t3 = [b.sb(f"lnt3B{j}", 512, F32) for j in range(2)]

    def gen_tile(n, m, ysb2):
        j = m % 2
        p.op("sync", lambda e, m=m, n=n, j=j: e.dma_start(
            out=t1[j], in_=x1_dram[m * P:(m + 1) * P, n * 512:(n + 1) * 512]),
            writes=[("lt1B", j)], dma=True)
        p.op("sync", lambda e, m=m, n=n, j=j: e.dma_start(
            out=t2[j], in_=h_dram[m * P:(m + 1) * P, n * 512:(n + 1) * 512]),
            writes=[("lt2B", j)], dma=True)
        p.op("sync", lambda e, m=m, n=n, j=j: e.dma_start(
            out=t3[j], in_=moe_dram[m * P:(m + 1) * P, n * 512:(n + 1) * 512]),
            writes=[("lt3B", j)], dma=True)
        p.op("vector", lambda e, ysb2=ysb2, j=j, m=m: e.tensor_add(ysb2, t1[j], t2[j]),
             reads=[("lt1B", j), ("lt2B", j)], writes=[("lny", "B", m)])
        p.op("vector", lambda e, m=m, ysb2=ysb2, j=j: e.tensor_add(ysb2, ysb2, t3[j]),
             reads=[("lny", "B", m), ("lt3B", j)], writes=[("lny", "B", m)])

    ln_epilogue(b, "B", gen_tile, x2_dram)
    p.barrier()
    b.release()
    b.release()


def stage_c(b, W, x2_dram, x3_dram):
    p = b.p
    NC2 = 32

    b.mark()
    identb = b.sb("identC", P, BF16)
    p.op("sync", lambda e: e.dma_start(out=identb, in_=W["ident"]),
         writes=[("identC",)], dma=True)
    ident_t = identb.tensor
    ident_stride = identb.ap[0][0]
    ident_off0 = identb.offset

    selu = b.sb("seluC", 8 * P, BF16)
    p.op("sync", lambda e: e.dma_start(
        out=selu.rearrange("p (a c) -> p a c", a=8), in_=W["selu8"]),
        writes=[("seluC",)], dma=True)
    selc = b.sb("selcC", 16 * P, BF16)
    p.op("sync", lambda e: e.dma_start(
        out=selc.rearrange("p (a c) -> p a c", a=16), in_=W["selc16"]),
        writes=[("selcC",)], dma=True)
    dtpb = b.sb("dtpbC", NC2, F32)
    p.op("sync", lambda e: e.dma_start(out=dtpb, in_=W["dtpb"]),
         writes=[("dtpbC",)], dma=True)
    convw = b.sb("convwC", NC2 * 4, F32)
    p.op("sync", lambda e: e.dma_start(out=convw, in_=W["convw"]),
         writes=[("convwC",)], dma=True)
    convb = b.sb("convbC", NC2, F32)
    p.op("sync", lambda e: e.dma_start(out=convb, in_=W["convb"]),
         writes=[("convbC",)], dma=True)
    alog = b.sb("alogC", NC2 * 16, F32)
    p.op("sync", lambda e: e.dma_start(out=alog, in_=W["alog"]),
         writes=[("alogC",)], dma=True)
    dvec = b.sb("dvecC", NC2, F32)
    p.op("sync", lambda e: e.dma_start(out=dvec, in_=W["dvec"]),
         writes=[("dvecC",)], dma=True)

    xss = [b.sb(f"xssC{c}", S, BF16) for c in range(4)]

    C_dram = b.dram("CmC", [4096, S], BF16)
    B_dram = b.dram("BmC", [512, S], BF16)
    zc_dram = b.dram("zcC", [4096, S], BF16)
    delta_dram = b.dram("deltaC", [4096, S], F32)
    outpre_dram = b.dram("outpreC", [4096, S], BF16)

    b.mark()
    xc = [b.sb(f"xcC{c}", S, BF16) for c in range(NCH)]
    for c in range(NCH):
        p.op("gpsimd", lambda e, c=c: e.dma_start(
            out=xc[c], in_=x2_dram[c * P:(c + 1) * P, :]),
            writes=[("xcC", c)], dma=True)
    wsh = [b.sb(f"WshC{i}", NCH * P, BF16) for i in range(6)]
    stg = b.sb("stgC", S, BF16)
    praw = b.sb("prawC", S, F32)
    spt = b.sb("sptC", S, F32)

    def inproj_evac(m, n, pp, pk):
        if 32 <= m < 36:
            dst, key = xss[m - 32], ("xssC", m - 32)
        else:
            dst, key = stg, ("stgC", n)
        p.op("scalar", lambda e, n=n, pp=pp, dst=dst: e.copy(
            dst[:, n * 512:(n + 1) * 512], pp), reads=pk, writes=[key],
            tag="ipev")
        if not (32 <= m < 36) and n == NTB - 1:
            if m < 32:
                tgt, wkey2 = zc_dram[m * P:(m + 1) * P, :], ("zcCd", m)
            elif 36 <= m < 40:
                tgt, wkey2 = B_dram[(m - 36) * P:(m - 35) * P, :], ("BmCd", m - 36)
            else:
                tgt, wkey2 = C_dram[(m - 40) * P:(m - 39) * P, :], ("CmCd", m - 40)
            p.op("sync", lambda e, tgt=tgt: e.dma_start(out=tgt, in_=stg),
                reads=[("stgC", nn) for nn in range(4)],
                writes=[wkey2], dma=True)

    fm_proj(b, "ipC", W["inprojT"], NCH, 72, xc, "xcC", inproj_evac,
            wbuf=wsh, wkey="WshC", queues=("gpsimd", "scalar", "sync"))

    dt1 = b.sb("dt1C", S, BF16)

    def dt1_evac(m, n, pp, pk):
        p.op("scalar", lambda e, n=n, pp=pp: e.copy(
            dt1[:, n * 512:(n + 1) * 512], pp), reads=pk, writes=[("dt1C", 0)],
            tag="dt1ev")

    fm_proj(b, "dt1C", W["dtinT"], NCH, 1, xc, "xcC", dt1_evac,
            wbuf=wsh, wkey="WshC", queues=("gpsimd", "scalar", "sync"))

    def dt2_evac(m, n, pp, pk):
        p.op("scalar", lambda e, n=n, pp=pp: e.copy(
            praw[:, n * 512:(n + 1) * 512], pp), reads=pk, writes=[("prawC", n)],
            tag="dt2ev")
        if n == NTB - 1:
            bap = dtpb[:, m:m + 1]
            rk = [("prawC", nn) for nn in range(4)] + [("dtpbC",)]
            p.op("scalar", lambda e, bap=bap: e.activation(spt, praw, AF.Abs, bias=bap),
                 reads=rk, writes=[("sptC",)])
            p.op("scalar", lambda e: e.activation(spt, spt, AF.Exp, scale=-1.0),
                 reads=[("sptC",)], writes=[("sptC",)])
            p.op("scalar", lambda e: e.activation(spt, spt, AF.Ln, bias=1.0),
                 reads=[("sptC",)], writes=[("sptC",)])
            p.op("scalar", lambda e, bap=bap: e.activation(praw, praw, AF.Relu, bias=bap),
                 reads=rk, writes=[("prawC", 0), ("prawC", 1), ("prawC", 2), ("prawC", 3)])
            p.op("vector", lambda e: e.tensor_add(praw, praw, spt),
                 reads=[("prawC", 0), ("sptC",)],
                 writes=[("prawC", 0), ("prawC", 1), ("prawC", 2), ("prawC", 3)])
            p.op("sync", lambda e, m=m: e.dma_start(
                out=delta_dram[m * P:(m + 1) * P, :], in_=praw),
                reads=[("prawC", 0)], writes=[("deltaCd", m)], dma=True)

    fm_proj(b, "dt2C", W["dtpT"], 1, NC2, [dt1], "dt1C", dt2_evac,
            wbuf=wsh, wkey="WshC", queues=("gpsimd", "scalar", "sync"))

    p.barrier()
    b.release()

    b.mark()
    dsl = [b.sb(f"delC{j}", S, BF16) for j in range(2)]
    xcv = [b.sb(f"xcvC{j}", 3 + S + 1, F32) for j in range(2)]
    tconv = [b.sb(f"tconvC{j}", S, F32) for j in range(2)]
    ui = [b.sb(f"uiC{j}", S, F32) for j in range(2)]
    acol = [b.sb(f"acolC{j}", 16, F32) for j in range(2)]
    NJ = 3
    dan = [b.sb(f"danC{j}", S, BF16) for j in range(NJ)]
    xnn = [b.sb(f"xnnC{j}", S, BF16) for j in range(NJ)]
    hnn = [b.sb(f"hnnC{j}", S, BF16) for j in range(NJ)]
    NB4 = 4
    bsb = [b.sb(f"bsbC{j}", S, BF16) for j in range(NB4)]
    csb = [b.sb(f"csbC{j}", S, BF16) for j in range(NB4)]
    zdub = [b.sb(f"zdubC{j}", S, BF16) for j in range(2)]
    yi = [b.sb(f"yiC{j}", S, F32) for j in range(2)]
    zg = [b.sb(f"zgC{j}", S, BF16) for j in range(2)]
    cci = [b.sb(f"cciC{i}", S, BF16) for i in range(2)]
    og = b.sb("ogC", S, BF16)

    for j in range(2):
        p.op("vector", lambda e, j=j: e.memset(xcv[j][:, 0:3], 0.0),
             writes=[("xcvz", j)])

    def bc_loads(i, n):
        j4 = n % NB4
        bsrc = bass.AP(B_dram.tensor, (16 * i + n) * S, [[0, P], [1, S]])
        p.op("sync", lambda e, d=bsb[j4], bsrc=bsrc: e.dma_start(out=d, in_=bsrc),
             reads=[("BmCd", (16 * i + n) // P)], writes=[("bsbC", j4)],
             dma=True, tag="bbc")

    bc_loads(0, 0)
    bc_loads(0, 1)

    for i in range(NC2):
        j2 = i % 2
        p.op("gpsimd", lambda e, i=i, d=dsl[j2]: e.dma_start(
            out=d, in_=delta_dram[i * P:(i + 1) * P, :]),
            reads=[("deltaCd", i)], writes=[("delC", j2)], dma=True)
        p.op("sync", lambda e, i=i, d=zg[j2]: e.dma_start(
            out=d, in_=zc_dram[i * P:(i + 1) * P, :]),
            reads=[("zcCd", i)], writes=[("zgC", j2)], dma=True)
        p.op("scalar", lambda e, i=i, d=cci[i % 2]: e.dma_start(
            out=d, in_=C_dram[i * P:(i + 1) * P, :]),
            reads=[("CmCd", i)], writes=[("cciC", i % 2)], dma=True)
        sel_u = selu[:, (i % 8) * P:(i % 8 + 1) * P]
        xs_c = i // 8
        for nn in range(NTB):
            bank = nn % 4
            pp = b.ps(bank)
            p.op("tensor", lambda e, nn=nn, pp=pp, sel_u=sel_u, xs_c=xs_c:
                 e.matmul(pp, sel_u, xss[xs_c][:, nn * 512:(nn + 1) * 512],
                          start=True, stop=True),
                 reads=[("seluC",), ("xssC", xs_c)], writes=b.pk(bank),
                 tag="selu")
            p.op("scalar", lambda e, nn=nn, pp=pp, j2=j2: e.copy(
                xcv[j2][:, 3 + nn * 512:3 + (nn + 1) * 512], pp),
                reads=b.pk(bank), writes=[("xcvC", j2, nn)], tag="seluev")
        xk = [("xcvC", j2, nn) for nn in range(4)] + [("xcvz", j2), ("convwC",)]
        p.op("vector", lambda e, i=i, j2=j2: e.tensor_scalar_mul(
            tconv[j2], xcv[j2][:, 0:S], convw[:, 4 * i:4 * i + 1]),
            reads=xk, writes=[("tconvC", j2)], tag="conv0")
        for tau in range(1, 4):
            p.op("vector", lambda e, i=i, tau=tau, j2=j2: e.scalar_tensor_tensor(
                tconv[j2], xcv[j2][:, tau:tau + S],
                convw[:, 4 * i + tau:4 * i + tau + 1],
                tconv[j2], ALU.mult, ALU.add),
                reads=xk + [("tconvC", j2)], writes=[("tconvC", j2)], tag="conv")
        p.op("scalar", lambda e, i=i, j2=j2: e.activation(
            ui[j2], tconv[j2], AF.Silu, bias=convb[:, i:i + 1]),
            reads=[("tconvC", j2), ("convbC",)], writes=[("uiC", j2)], tag="silu_u")
        p.op("vector", lambda e, j2=j2: e.tensor_mul(zdub[j2], dsl[j2], ui[j2]),
             reads=[("delC", j2), ("uiC", j2)], writes=[("zdubC", j2)], tag="zdu")
        p.op("scalar", lambda e, i=i, j2=j2: e.activation(
            acol[j2], alog[:, 16 * i:16 * i + 16], AF.Exp),
            reads=[("alogC",)], writes=[("acolC", j2)])
        p.op("vector", lambda e, j2=j2: e.tensor_scalar_mul(acol[j2], acol[j2], -1.0),
             reads=[("acolC", j2)], writes=[("acolC", j2)])
        p.op("scalar", lambda e, i=i, j2=j2: e.activation(
            yi[j2], ui[j2], AF.Copy, scale=dvec[:, i:i + 1]),
            reads=[("uiC", j2), ("dvecC",)], writes=[("yiC", j2)], tag="yinit")
        for n in range(16):
            j = n % NJ
            j4 = n % NB4
            if n < 14:
                bc_loads(i, n + 2)
            elif i < NC2 - 1:
                bc_loads(i + 1, n - 14)
            sel_c = selc[:, n * P:(n + 1) * P]
            for nn in range(NTB):
                pp = b.ps(4 + nn)
                p.op("tensor", lambda e, nn=nn, pp=pp, sel_c=sel_c,
                     cap=cci[i % 2]:
                     e.matmul(pp, sel_c, cap[:, nn * 512:(nn + 1) * 512],
                              start=True, stop=True),
                     reads=[("selcC",), ("cciC", i % 2)], writes=b.pk(4 + nn),
                     tag="selC")
            p.op("scalar", lambda e, j4=j4: e.copy(csb[j4], b.ps(4, 4)),
                 reads=b.pk(4, 4), writes=[("csbC", j4)], tag="cev")
            p.op("scalar", lambda e, n=n, j=j, j2=j2: e.activation(
                dan[j], dsl[j2], AF.Exp, scale=acol[j2][:, n:n + 1]),
                reads=[("delC", j2), ("acolC", j2)], writes=[("danC", j)],
                tag="dA")
            p.op("vector", lambda e, j=j, j4=j4, j2=j2: e.tensor_mul(
                xnn[j], zdub[j2], bsb[j4]),
                reads=[("zdubC", j2), ("bsbC", j4)], writes=[("xnnC", j)], tag="xB")
            p.op("vector", lambda e, j=j: e.tensor_tensor_scan(
                hnn[j], dan[j], xnn[j], 0.0, ALU.mult, ALU.add),
                reads=[("danC", j), ("xnnC", j)], writes=[("hnnC", j)],
                tag="scan")
            p.op("vector", lambda e, j=j, j4=j4: e.tensor_mul(hnn[j], hnn[j], csb[j4]),
                 reads=[("hnnC", j), ("csbC", j4)], writes=[("hnnC", j)], tag="hC")
            p.op("vector", lambda e, j=j, j2=j2: e.tensor_add(yi[j2], yi[j2], hnn[j]),
                 reads=[("yiC", j2), ("hnnC", j)], writes=[("yiC", j2)], tag="yadd")
        p.op("scalar", lambda e, j2=j2: e.activation(tconv[j2], zg[j2], AF.Silu),
             reads=[("zgC", j2)], writes=[("tconvC", j2)], tag="sz")
        p.op("vector", lambda e, j2=j2: e.tensor_mul(og, yi[j2], tconv[j2]),
             reads=[("yiC", j2), ("tconvC", j2)], writes=[("ogC",)], tag="og")
        p.op("sync", lambda e, i=i: e.dma_start(
            out=outpre_dram[i * P:(i + 1) * P, :], in_=og),
            reads=[("ogC",)], writes=[("opCd", i)], dma=True)

    p.barrier()
    b.release()
    b.release()

    b.mark()
    wbuf3 = [b.sb(f"opwC{i}", NC2 * P, BF16) for i in range(3)]
    actb = [b.sb(f"opactC{k}", 512, BF16) for k in range(NC2)]
    xres = [b.sb(f"opxC{j}", 512, F32) for j in range(2)]

    def gen_tile(n, m, ysb):
        sl = wbuf3[m % 3]
        q = ("gpsimd", "sync", "scalar")[m % 3]
        p.op(q, lambda e, m=m, sl=sl: e.dma_start(out=sl, in_=W["outpT"][m]),
             writes=[("opwC", m % 3)], dma=True, tag="opwld")
        if m == 0:
            for k in range(NC2):
                p.op("sync", lambda e, k=k, n=n: e.dma_start(
                    out=actb[k], in_=outpre_dram[k * P:(k + 1) * P, n * 512:(n + 1) * 512]),
                    writes=[("opactC", k)], dma=True)
        xr = xres[m % 2]
        p.op("sync", lambda e, m=m, n=n, xr=xr: e.dma_start(
            out=xr, in_=x2_dram[m * P:(m + 1) * P, n * 512:(n + 1) * 512]),
            writes=[("opxC", m % 2)], dma=True)
        bank = m % 4
        pp = b.ps(bank)
        for k in range(NC2):
            p.op("tensor", lambda e, k=k, sl=sl, pp=pp: e.matmul(
                pp, sl[:, k * P:(k + 1) * P], actb[k],
                start=(k == 0), stop=(k == NC2 - 1)),
                reads=[("opwC", m % 3), ("opactC", k)], writes=b.pk(bank),
                tag="opmm")
        p.op("vector", lambda e, pp=pp, ysb=ysb, m=m, xr=xr: e.tensor_add(ysb, pp, xr),
             reads=b.pk(bank) + [("opxC", m % 2)], writes=[("lny", "C", m)],
             tag="opadd")

    ln_epilogue(b, "C", gen_tile, x3_dram)
    p.barrier()
    b.release()


def stage_d(b, W, x3_dram, x4_dram):
    p = b.p

    b.mark()
    identf = b.sb("identfD", P, F32)
    p.op("sync", lambda e: e.dma_start(out=identf, in_=W["identf"]),
         writes=[("identfD",)], dma=True)
    moe_dram = b.dram("moeD", [D, S], F32)
    comb_dram = b.dram("combD", [8, S], F32)

    b.mark()
    rw = b.sb("rwD", NCH * 8, F32)
    for k in range(NCH):
        p.op("sync", lambda e, k=k: e.dma_start(
            out=rw[:, k * 8:(k + 1) * 8], in_=W["WrD"][k]),
            writes=[("rwD", k)], dma=True)
    xft = [b.sb(f"xftD{i}", P, F32) for i in range(2)]
    lgt = b.sb("lgD", 8, F32)
    m1n = b.sb("m1D", 1, F32)
    m1p = b.sb("m1pD", 1, F32)
    m2 = b.sb("m2D", 1, F32)
    den = b.sb("denD", 1, F32)
    msk = b.sb("mskD", 8, F32)
    comb = b.sb("combD", 8, F32)
    combT = b.sb("combTD", S, F32)
    for tt in range(NTT):
        pp = b.ps(5)
        for k in range(NCH):
            xf = xft[k % 2]
            p.op("sync", lambda e, tt=tt, k=k, xf=xf: e.dma_start(
                out=xf, in_=x3_dram[k * P:(k + 1) * P, tt * P:(tt + 1) * P]),
                writes=[("xftD", k % 2)], dma=True)
            p.op("tensor", lambda e, k=k, pp=pp, xf=xf: e.matmul(
                pp[:, 0:8], xf, rw[:, k * 8:(k + 1) * 8],
                start=(k == 0), stop=(k == NCH - 1)),
                reads=[("xftD", k % 2), ("rwD", k)], writes=b.pk(5), tag="rlogD")
        p.op("vector", lambda e, pp=pp: e.tensor_copy(lgt, pp[:, 0:8]),
             reads=b.pk(5), writes=[("lgD",)])
        p.op("vector", lambda e: e.reduce_max(m1n, lgt, axis=AX.X, negate=True),
             reads=[("lgD",)], writes=[("m1D",)])
        p.op("vector", lambda e: e.tensor_scalar_mul(m1p, m1n, -1.0),
             reads=[("m1D",)], writes=[("m1pD",)])
        p.op("vector", lambda e: e.tensor_single_scalar(msk, lgt, m1p, ALU.is_ge),
             reads=[("lgD",), ("m1pD",)], writes=[("mskD",)])
        p.op("vector", lambda e: e.scalar_tensor_tensor(
            msk, msk, -1e9, lgt, ALU.mult, ALU.add),
            reads=[("mskD",), ("lgD",)], writes=[("mskD",)])
        p.op("vector", lambda e: e.reduce_max(m2, msk, axis=AX.X),
             reads=[("mskD",)], writes=[("m2D",)])
        p.op("vector", lambda e: e.tensor_single_scalar(msk, lgt, m2, ALU.is_ge),
             reads=[("lgD",), ("m2D",)], writes=[("mskD",)])
        p.op("scalar", lambda e: e.activation(lgt, lgt, AF.Exp, bias=m1n),
             reads=[("lgD",), ("m1D",)], writes=[("lgD",)])
        p.op("scalar", lambda e: e.activation(den, m2, AF.Exp, bias=m1n),
             reads=[("m2D",), ("m1D",)], writes=[("denD",)])
        p.op("vector", lambda e: e.tensor_scalar_add(den, den, 1.0),
             reads=[("denD",)], writes=[("denD",)])
        p.op("vector", lambda e: e.reciprocal(den, den),
             reads=[("denD",)], writes=[("denD",)])
        p.op("vector", lambda e: e.tensor_mul(comb, lgt, msk),
             reads=[("lgD",), ("mskD",)], writes=[("combD",)])
        p.op("vector", lambda e: e.tensor_scalar_mul(comb, comb, den),
             reads=[("combD",), ("denD",)], writes=[("combD",)])
        pt = b.ps(4)
        p.op("tensor", lambda e, pt=pt: e.matmul(
            pt[0:8, 0:P], comb, identf, start=True, stop=True),
            reads=[("combD",), ("identfD",)], writes=b.pk(4), tag="rtrD")
        p.op("vector", lambda e, tt=tt, pt=pt: e.tensor_copy(
            combT[0:8, tt * P:(tt + 1) * P], pt[0:8, 0:P]),
            reads=b.pk(4), writes=[("combTD",)])
    p.op("sync", lambda e: e.dma_start(out=comb_dram, in_=combT[0:8, :]),
         reads=[("combTD",)], writes=[("combDd",)], dma=True)
    p.barrier()
    b.release()

    b.mark()
    x3b = [b.sb(f"x3bD{c}", S, BF16) for c in range(NCH)]
    for c in range(NCH):
        p.op("gpsimd", lambda e, c=c: e.dma_start(
            out=x3b[c], in_=x3_dram[c * P:(c + 1) * P, :]),
            writes=[("x3bD", c)], dma=True)
    NI = 11
    NWB = 4
    wg2 = [b.sb(f"wgD{i}", NCH * P, BF16) for i in range(NWB)]
    wu2 = [b.sb(f"wuD{i}", NCH * P, BF16) for i in range(NWB)]
    wd2 = [b.sb(f"wdD{i}", NI * P, BF16) for i in range(NWB)]
    hbf = [b.sb(f"hbfD{m}", S, BF16) for m in range(NI)]
    gs2 = [b.sb(f"gsD{j}", 512, F32) for j in range(2)]
    ceb = b.sb("cebD", S, F32)
    ysb2 = [b.sb(f"ymoeD{j}", 512, F32) for j in range(2)]

    for e_i in range(8):
        bce = bass.AP(comb_dram.tensor, e_i * S, [[0, P], [1, S]])
        p.op("gpsimd", lambda e, bce=bce: e.dma_start(out=ceb, in_=bce),
             reads=[("combDd",)], writes=[("cebD",)], dma=True)
        for m in range(NI):
            slg = wg2[m % NWB]
            slu = wu2[m % NWB]
            p.op("sync", lambda e, slg=slg, gi=e_i * NI + m: e.dma_start(
                out=slg, in_=W["WgD"][gi]),
                writes=[("wgD", m % NWB)], dma=True, tag="wgldD")
            p.op("scalar", lambda e, slu=slu, gi=e_i * NI + m: e.dma_start(
                out=slu, in_=W["WuD"][gi]),
                writes=[("wuD", m % NWB)], dma=True, tag="wuldD")
            for n in range(NTB):
                gs = gs2[n % 2]
                bank = n % 2
                pp = b.ps(bank)
                for k in range(NCH):
                    p.op("tensor", lambda e, k=k, n=n, slg=slg, pp=pp: e.matmul(
                        pp, slg[:, k * P:(k + 1) * P],
                        x3b[k][:, n * 512:(n + 1) * 512],
                        start=(k == 0), stop=(k == NCH - 1)),
                        reads=[("wgD", m % NWB), ("x3bD", k)], writes=b.pk(bank),
                        tag="gmmD")
                p.op("scalar", lambda e, pp=pp, gs=gs: e.activation(gs, pp, AF.Silu),
                     reads=b.pk(bank), writes=[("gsD", n % 2)], tag="siluD")
                bank2 = 2 + n % 2
                pp2 = b.ps(bank2)
                for k in range(NCH):
                    p.op("tensor", lambda e, k=k, n=n, slu=slu, pp2=pp2: e.matmul(
                        pp2, slu[:, k * P:(k + 1) * P],
                        x3b[k][:, n * 512:(n + 1) * 512],
                        start=(k == 0), stop=(k == NCH - 1)),
                        reads=[("wuD", m % NWB), ("x3bD", k)],
                        writes=b.pk(bank2), tag="ummD")
                p.op("vector", lambda e, pp2=pp2, gs=gs: e.tensor_mul(gs, gs, pp2),
                     reads=[("gsD", n % 2)] + b.pk(bank2), writes=[("gsD", n % 2)],
                     tag="hmulD")
                p.op("vector", lambda e, m=m, n=n, gs=gs: e.tensor_mul(
                    hbf[m][:, n * 512:(n + 1) * 512], gs,
                    ceb[:, n * 512:(n + 1) * 512]),
                    reads=[("gsD", n % 2), ("cebD",)], writes=[("hbfD", m)],
                    tag="hceD")
        for dm in range(NCH):
            sld = wd2[dm % NWB]
            q = ("sync", "scalar", "gpsimd")[dm % 3]
            p.op(q, lambda e, sld=sld, di=e_i * NCH + dm: e.dma_start(
                out=sld, in_=W["WdD"][di]),
                writes=[("wdD", dm % NWB)], dma=True, tag="wdldD")
            for n in range(NTB):
                ysb = ysb2[n % 2]
                bank = 4 + (dm * NTB + n) % 4
                pp = b.ps(bank)
                for k in range(NI):
                    p.op("tensor", lambda e, k=k, n=n, sld=sld, pp=pp: e.matmul(
                        pp, sld[:, k * P:(k + 1) * P],
                        hbf[k][:, n * 512:(n + 1) * 512],
                        start=(k == 0), stop=(k == NI - 1)),
                        reads=[("wdD", dm % NWB), ("hbfD", k)], writes=b.pk(bank),
                        tag="dmmD")
                p.op("vector", lambda e, pp=pp, ysb=ysb: e.tensor_copy(ysb, pp),
                     reads=b.pk(bank), writes=[("ymoeD", n % 2)], tag="devD")
                if e_i == 0:
                    p.op("gpsimd", lambda e, dm=dm, n=n, ysb=ysb: e.dma_start(
                        out=moe_dram[dm * P:(dm + 1) * P, n * 512:(n + 1) * 512],
                        in_=ysb),
                        reads=[("ymoeD", n % 2)], writes=[("moeDd", dm, n)], dma=True)
                else:
                    p.op("gpsimd", lambda e, dm=dm, n=n, ysb=ysb: e.dma_start(
                        out=moe_dram[dm * P:(dm + 1) * P, n * 512:(n + 1) * 512],
                        in_=ysb, accum_op=ALU.add),
                        reads=[("ymoeD", n % 2)], writes=[("moeDd", dm, n)], dma=True)
    p.barrier()
    b.release()

    b.mark()
    t1 = [b.sb(f"lnt1D{j}", 512, F32) for j in range(2)]
    t2 = [b.sb(f"lnt2D{j}", 512, F32) for j in range(2)]

    def gen_tile(n, m, ysb2):
        j = m % 2
        p.op("sync", lambda e, m=m, n=n, j=j: e.dma_start(
            out=t1[j], in_=x3_dram[m * P:(m + 1) * P, n * 512:(n + 1) * 512]),
            writes=[("lt1D", j)], dma=True)
        p.op("sync", lambda e, m=m, n=n, j=j: e.dma_start(
            out=t2[j], in_=moe_dram[m * P:(m + 1) * P, n * 512:(n + 1) * 512]),
            writes=[("lt2D", j)], dma=True)
        p.op("vector", lambda e, ysb2=ysb2, m=m, j=j: e.tensor_add(ysb2, t1[j], t2[j]),
             reads=[("lt1D", j), ("lt2D", j)], writes=[("lny", "D", m)])

    ln_epilogue(b, "D", gen_tile, x4_dram)
    p.barrier()
    b.release()
    b.release()


def host_pack(inputs):
    W = {}
    W["ident"] = np.eye(P, dtype=ml_dtypes.bfloat16)
    W["mask"] = causal_mask_np()
    W["perm16"] = perm_swap16()
    W["perm32"] = perm_swap32()
    cq, sq, ck, sk = l0attn_tables()
    W["ta_cosq"], W["ta_sinq"], W["ta_cosk"], W["ta_sink"] = cq, sq, ck, sk
    perm = np.concatenate([np.arange(0, 16), np.arange(32, 48),
                           np.arange(16, 32), np.arange(48, 64)])
    qwT = inputs["L0_attn_q_w"].T.reshape(D, 32, 64)[:, :, perm].reshape(D, D)
    W["q_wT"] = pack_lhsT(qwT)
    W["kr_wT"] = pack_lhsT(inputs["L0_attn_kr_w"].T)
    W["down_wT"] = pack_lhsT(inputs["L0_attn_down_w"].T)
    W["upk_wT"] = pack_lhsT(inputs["L0_attn_upk_w"].T)
    W["upv_wT"] = pack_rhs_bf(inputs["L0_attn_upv_w"].T)
    W["o_wT"] = pack_lhsT(inputs["L0_attn_o_w"].T)
    W["identf"] = np.eye(P, dtype=np.float32)
    W["ln1w16"] = _f32(inputs["L0_ffn_ln1_w"].reshape(NCH, P).T)
    W["ln2w16"] = _f32(inputs["L0_ffn_ln2_w"].reshape(NCH, P).T)
    W["kvdw"] = pack_lhsT(inputs["L0_ffn_kv_d_w"])
    W["qdw"] = pack_lhsT(inputs["L0_ffn_q_d_w"])
    W["kuw"] = pack_lhsT(inputs["L0_ffn_k_u_w"])
    W["quw"] = pack_lhsT(inputs["L0_ffn_q_u_w"])
    W["ropekw"] = pack_lhsT(inputs["L0_ffn_rope_k_w"])
    W["ropeqw"] = pack_lhsT(inputs["L0_ffn_rope_q_w"])
    W["vuw"] = pack_rhs_bf(inputs["L0_ffn_v_u_w"])
    W["owB"] = pack_lhsT(inputs["L0_ffn_o_w"])
    cr, sr = l0ffn_tables()
    W["tb_cos"], W["tb_sin"] = cr, sr
    scq = 1.0 / np.sqrt(128.0)
    W["tb_cosq"], W["tb_sinq"] = _f32(cr * scq), _f32(sr * scq)
    W["routerw"] = pack_rhs(inputs["L0_ffn_router_w"])
    W["rb7"] = _f32(inputs["L0_ffn_router_b"].reshape(1, 7))
    W["shg"] = pack_lhsT(inputs["L0_ffn_sh_gate"])
    W["shu"] = pack_lhsT(inputs["L0_ffn_sh_up"])
    W["shd"] = pack_lhsT(inputs["L0_ffn_sh_down"])
    W["rgate"] = np.concatenate([pack_lhsT(inputs["L0_ffn_r_gate"][e])
                                 for e in range(7)], axis=0)
    W["rup"] = np.concatenate([pack_lhsT(inputs["L0_ffn_r_up"][e])
                               for e in range(7)], axis=0)
    W["rdown"] = np.concatenate([pack_lhsT(inputs["L0_ffn_r_down"][e])
                                 for e in range(7)], axis=0)
    NC2 = 32
    W["inprojT"] = pack_lhsT(inputs["L1_attn_in_proj_w"].T)
    W["dtinT"] = pack_lhsT(inputs["L1_attn_dt_in_proj_w"].T)
    W["dtpT"] = pack_lhsT(inputs["L1_attn_dt_proj_w"].T)
    W["dtpb"] = _f32(inputs["L1_attn_dt_proj_b"].reshape(NC2, P).T)
    W["convw"] = _f32(inputs["L1_attn_conv_w"].reshape(NC2, P, 4)
                      .transpose(1, 0, 2).reshape(P, NC2 * 4))
    W["convb"] = _f32(inputs["L1_attn_conv_b"].reshape(NC2, P).T)
    W["alog"] = _f32(inputs["L1_attn_A_log"].reshape(NC2, P, 16)
                     .transpose(1, 0, 2).reshape(P, NC2 * 16))
    W["dvec"] = _f32(inputs["L1_attn_D"].reshape(NC2, P).T)
    W["outpT"] = pack_lhsT(inputs["L1_attn_out_proj_w"].T)
    selu8 = np.zeros((8, P, P), np.float32)
    for k in range(8):
        for pp_ in range(P):
            selu8[k, 16 * k + (pp_ % 16), pp_] = 1.0
    W["selu8"] = np.ascontiguousarray(selu8.transpose(1, 0, 2)).astype(ml_dtypes.bfloat16)
    selc16 = np.zeros((16, P, P), np.float32)
    for nn_ in range(16):
        for pp_ in range(P):
            selc16[nn_, nn_ + 16 * (pp_ // 16), pp_] = 1.0
    W["selc16"] = np.ascontiguousarray(selc16.transpose(1, 0, 2)).astype(ml_dtypes.bfloat16)
    W["WrD"] = pack_rhs(inputs["L1_ffn_Wr"])
    W["WgD"] = np.concatenate([pack_lhsT(inputs["L1_ffn_Wg"][e])
                               for e in range(8)], axis=0)
    W["WuD"] = np.concatenate([pack_lhsT(inputs["L1_ffn_Wu"][e])
                               for e in range(8)], axis=0)
    W["WdD"] = np.concatenate([pack_lhsT(inputs["L1_ffn_Wd"][e])
                               for e in range(8)], axis=0)
    return W


def build_program(inputs, n_stages=4, taps=(), only=None):
    prog = Prog()
    nc = bass.Bass("TRN2", target_bir_lowering=False, debug=False)
    b = Builder(nc, prog)
    Wh = host_pack(inputs)
    W = {k: b.hin(k, v) for k, v in Wh.items()}
    x0 = b.dram("x0", [D, S], F32, kind="ExternalInput")
    out = b.dram("xout", [D, S], F32, kind="ExternalOutput")
    b.init_arena()
    if only == "b1":
        x1i = b.dram("x1i", [D, S], F32, kind="ExternalInput")
        stage_b(b, W, x1i, out, phase1_only=True)
    elif only == "c":
        x2i = b.dram("x2i", [D, S], F32, kind="ExternalInput")
        stage_c(b, W, x2i, out)
    elif only == "d":
        x3i = b.dram("x3i", [D, S], F32, kind="ExternalInput")
        stage_d(b, W, x3i, out)
    else:
        d1 = out if n_stages == 1 else b.dram("x1", [D, S], F32)
        stage_a(b, W, x0, d1)
        if n_stages >= 2:
            d2 = out if n_stages == 2 else b.dram("x2", [D, S], F32)
            stage_b(b, W, d1, d2)
        if n_stages >= 3:
            d3 = out if n_stages == 3 else b.dram("x3", [D, S], F32)
            stage_c(b, W, d2, d3)
        if n_stages >= 4:
            stage_d(b, W, d3, out)

    for t in taps:
        ap = b.dram_cache[t]
        sh = [ap.tensor.shape[i] for i in range(len(ap.tensor.shape))] \
            if hasattr(ap.tensor, "shape") else list(ap.shape)
        dt = ap.dtype
        tout = b.dram("tap_" + t, list(ap.shape), dt, kind="ExternalOutput")
        prog.op("sync", lambda e, tout=tout, ap=ap: e.dma_start(out=tout, in_=ap),
                reads=[], writes=[], dma=True)
    prog.emit(nc)
    return nc, b


def kernel(**inputs):
    nc, b = build_program(inputs, n_stages=4)
    x = inputs["L0_attn_hidden_states"]
    in_maps = []
    for core in range(8):
        m = dict(b.in_arrays)
        m["x0"] = np.ascontiguousarray(x[core].T.astype(np.float32))
        in_maps.append(m)
    res = run_bass_kernel_spmd(nc, in_maps, core_ids=list(range(8)))
    outs = [np.ascontiguousarray(r["xout"].T) for r in res.results]
    return np.stack(outs)
